# version 05
# baseline (speedup 1.0000x reference)
import numpy as np
import ml_dtypes
import concourse.bass as bass
import concourse.mybir as mybir
from concourse.bass_utils import run_bass_kernel_spmd

F32 = mybir.dt.float32
BF16 = mybir.dt.bfloat16
AF = mybir.ActivationFunctionType
ALU = mybir.AluOpType
AX = mybir.AxisListType

T = 2048
D = 1024
NT = 16
I_FFN = 2816
EPS_LN = 1e-5
LN6 = float(np.log(6.0))
LN36 = float(np.log(36.0))

ENGS = ("sync", "vector", "scalar", "tensor", "gpsimd")

FP4_TH2 = [t * t for t in (0.25, 0.75, 1.25, 1.75, 2.5, 3.5, 5.0)]
FP4_D = [0.5, 0.5, 0.5, 0.5, 1.0, 1.0, 2.0]


class Bld:

    def __init__(self, nc):
        import os as _os
        self.nc = nc
        self.serial = bool(_os.environ.get("BLD_SERIAL"))
        self.strict = _os.environ.get("BLD_STRICT", "1") != "0"
        self._last = None
        self._elast = {}
        self.plan = {e: [] for e in ENGS}
        self.cnt = {}
        self.order = []
        for e in ENGS:
            self._sem("c_" + e)
        self.NQS = 8
        self.qslot = {}
        for q in ("sync", "gpsimd", "scalar"):
            for k in range(self.NQS):
                self._sem("d_%s%d" % (q, k))
            self.qslot[q] = 0

    def _sem(self, name):
        if name not in self.cnt:
            self.cnt[name] = 0
            self.order.append(name)
        return name

    def op(self, eng, fn, deps=(), inc=True):
        sem = "c_" + eng
        if self.serial and self._last is not None:
            deps = tuple(deps) + (self._last,)
        if self.strict and eng != "tensor" and self._elast.get(eng) is not None:
            deps = tuple(deps) + (self._elast[eng],)
        if inc:
            self.cnt[sem] += 1
            tok = (sem, self.cnt[sem])
        else:
            tok = None
        self.plan[eng].append((fn, tuple(d for d in deps if d is not None),
                               sem if inc else None, 1))
        if inc and self.serial:
            self._last = tok
        if inc and self.strict:
            self._elast[eng] = tok
        return tok

    def dma(self, queue, out, in_, deps=()):
        k = self.qslot[queue]
        self.qslot[queue] = (k + 1) % self.NQS
        sem = "d_%s%d" % (queue, k)
        if self.serial and self._last is not None:
            deps = tuple(deps) + (self._last,)
        if self.cnt[sem] > 0:
            deps = tuple(deps) + ((sem, self.cnt[sem]),)
        self.cnt[sem] += 16

        def fn(eng, out=out, in_=in_):
            return eng.dma_start(out=out, in_=in_)
        self.plan[queue].append((fn, tuple(d for d in deps if d is not None),
                                 sem, 16))
        tok = (sem, self.cnt[sem])
        if self.serial:
            self._last = tok
        return tok

    def wait(self, eng, deps):
        self.plan[eng].append((None, tuple(d for d in deps if d is not None),
                               None, 0))

    def build(self):
        nc = self.nc
        import contextlib
        stack = contextlib.ExitStack()
        sems = {name: stack.enter_context(nc.semaphore(name))
                for name in self.order}
        blk = stack.enter_context(nc.Block())

        def runner_for(ename):
            plan = self.plan[ename]

            def run(eng):
                mw = {}
                for fn, deps, sem, amt in plan:
                    for (s, v) in deps:
                        if v > mw.get(s, 0):
                            eng.wait_ge(sems[s], v)
                            mw[s] = v
                    if fn is None:
                        continue
                    inst = fn(eng)
                    if sem is not None:
                        inst.then_inc(sems[sem], amt)
            return run
        for e in ENGS:
            getattr(blk, e)(runner_for(e))
        stack.close()
        return nc



def emit_ln_stats(b, X, s1_col, s2_col, junk, deps=()):
    t1 = b.op("vector", lambda v: v.reduce_sum(out=s1_col, in_=X, axis=AX.X),
              deps=deps)
    t2 = b.op("vector", lambda v: v.scalar_tensor_tensor(
        out=junk, in0=X, scalar=1.0, in1=X, op0=ALU.mult, op1=ALU.mult,
        accum_out=s2_col), deps=deps)
    return t1, t2


def emit_ln_batch(b, g, st, n, deps, off=0):
    s1, s2 = st["s1"][:, off:off + n], st["s2"][:, off:off + n]
    m, vv = st["m"][:, off:off + n], st["v"][:, off:off + n]
    rstd, mr = st["rstd"][:, off:off + n], st["mr"][:, off:off + n]
    tm = b.op("vector", lambda v: v.tensor_scalar(
        out=m, in0=s1, scalar1=1.0 / D, scalar2=None, op0=ALU.mult), deps=deps)
    tms = b.op("vector", lambda v: v.tensor_mul(out=vv, in0=m, in1=m), deps=[tm])
    tv = b.op("vector", lambda v: v.tensor_scalar(
        out=rstd, in0=s2, scalar1=1.0 / D, scalar2=None, op0=ALU.mult), deps=deps)
    tsub = b.op("vector", lambda v: v.tensor_sub(out=vv, in0=rstd, in1=vv),
                deps=[tms, tv])
    eps_ap, eps_tok = get_const(b, g, EPS_LN)
    tln = b.op("scalar", lambda s: s.activation(out=vv, in_=vv, func=AF.Ln,
                                                bias=eps_ap), deps=[tsub, eps_tok])
    trs = b.op("scalar", lambda s: s.activation(out=rstd, in_=vv, func=AF.Exp,
                                                scale=-0.5), deps=[tln])
    tmr = b.op("vector", lambda v: v.tensor_mul(out=mr, in0=m, in1=rstd),
               deps=[trs])
    return tmr, trs


def emit_ln_apply(b, X, xn_bf, rstd_col, mr_col, deps):
    return b.op("vector", lambda v: v.tensor_scalar(
        out=xn_bf, in0=X, scalar1=rstd_col, scalar2=mr_col,
        op0=ALU.mult, op1=ALU.subtract), deps=deps)


def emit_quant(b, g, u, st, src, xq_bf, K, deps):
    nb = K // 16
    x2, a2 = u["x2"][:, 0:K], u["a2"][:, 0:K]
    lvl, sg = u["lvl"][:, 0:K], u["sg"][:, 0:K]
    is2e, se = u["is2e"][:, 0:K], u["se"][:, 0:K]
    bm2, L = st["bm2"][:, 0:nb], st["L"][:, 0:nb]
    is2, s_ = st["is2"][:, 0:nb], st["s"][:, 0:nb]
    tx2 = b.op("vector", lambda v: v.tensor_mul(out=x2, in0=src, in1=src),
               deps=deps)
    tbm = b.op("vector", lambda v: v.tensor_reduce(
        out=bm2, in_=x2.rearrange("p (bb w) -> p bb w", w=16), axis=AX.X,
        op=ALU.max), deps=[tx2])
    tcl = b.op("vector", lambda v: v.tensor_scalar(
        out=bm2, in0=bm2, scalar1=3.6e-23, scalar2=None, op0=ALU.max),
        deps=[tbm])
    tln = b.op("scalar", lambda s: s.activation(out=L, in_=bm2, func=AF.Ln),
               deps=[tcl])
    c36, c36t = get_const(b, g, LN36)
    cm6, cm6t = get_const(b, g, -LN6)
    tis = b.op("scalar", lambda s: s.activation(out=is2, in_=L, func=AF.Exp,
                                                scale=-1.0, bias=c36),
               deps=[tln, c36t])
    ts_ = b.op("scalar", lambda s: s.activation(out=s_, in_=L, func=AF.Exp,
                                                scale=0.5, bias=cm6),
               deps=[tln, cm6t])
    x2v = x2.rearrange("p (bb w) -> p bb w", w=16)
    a2v = a2.rearrange("p (bb w) -> p bb w", w=16)
    ta2 = b.op("vector", lambda v: v.tensor_mul(
        out=a2v, in0=x2v,
        in1=is2.unsqueeze(2).broadcast_to([128, nb, 16])), deps=[tis, tx2])
    cb = u["m0"][:, 0:K]
    cc = u["m1"][:, 0:K]
    t0 = b.op("vector", lambda v: v.tensor_scalar(
        out=lvl, in0=a2, scalar1=FP4_TH2[0], scalar2=None, op0=ALU.is_gt),
        deps=[ta2])
    tacc = t0
    for j in (1, 2, 3):
        tacc = b.op("vector", lambda v, th2=FP4_TH2[j]:
                    v.scalar_tensor_tensor(
            out=lvl, in0=a2, scalar=th2, in1=lvl, op0=ALU.is_gt,
            op1=ALU.add), deps=[tacc])
    t4 = b.op("vector", lambda v: v.tensor_scalar(
        out=cb, in0=a2, scalar1=FP4_TH2[4], scalar2=None, op0=ALU.is_gt),
        deps=[ta2])
    t5 = b.op("vector", lambda v: v.scalar_tensor_tensor(
        out=cb, in0=a2, scalar=FP4_TH2[5], in1=cb, op0=ALU.is_gt,
        op1=ALU.add), deps=[t4])
    t6 = b.op("vector", lambda v: v.tensor_scalar(
        out=cc, in0=a2, scalar1=FP4_TH2[6], scalar2=None, op0=ALU.is_gt),
        deps=[ta2])
    tm1 = b.op("vector", lambda v: v.scalar_tensor_tensor(
        out=lvl, in0=lvl, scalar=0.5, in1=cb, op0=ALU.mult, op1=ALU.add),
        deps=[tacc, t5])
    tm2 = b.op("vector", lambda v: v.scalar_tensor_tensor(
        out=lvl, in0=cc, scalar=2.0, in1=lvl, op0=ALU.mult, op1=ALU.add),
        deps=[tm1, t6])
    tsg = b.op("scalar", lambda s: s.sign(out=sg, in_=src), deps=list(deps) + [ts_])
    tls = b.op("vector", lambda v: v.tensor_mul(out=lvl, in0=lvl, in1=sg),
               deps=[tm2, tsg])
    lvlv = lvl.rearrange("p (bb w) -> p bb w", w=16)
    xqv = xq_bf.rearrange("p (bb w) -> p bb w", w=16)
    tq = b.op("vector", lambda v: v.tensor_mul(
        out=xqv, in0=lvlv,
        in1=s_.unsqueeze(2).broadcast_to([128, nb, 16])), deps=[tls, ts_])
    return tq


def emit_transpose(b, pT, src_bf, dst_bf, eye, nchunk, deps, psum_free=None):
    d = list(deps)
    free = psum_free
    tc = None
    for g0 in range(0, nchunk, 8):
        gn = min(8, nchunk - g0)
        dd = list(d)
        if free is not None:
            dd.append(free)
        pe_last = None
        for c in range(gn):
            cc = g0 + c
            pe_last = b.op("tensor", lambda t, cc=cc, c=c: t.transpose(
                out=pT[:, c * 128:(c + 1) * 128],
                in_=src_bf[:, cc * 128:(cc + 1) * 128], identity=eye),
                deps=dd if c == 0 else ())
        tc = b.op("vector", lambda v, g0=g0, gn=gn: v.tensor_copy(
            out=dst_bf[:, g0 * 128:(g0 + gn) * 128], in_=pT[:, 0:gn * 128]),
            deps=[pe_last])
        free = tc
    return tc


def strips_of(width, mx=512):
    out, o = [], 0
    while o < width:
        w = min(mx, width - o)
        out.append((o, w))
        o += w
    return out


def emit_preload(b, func, deps=()):
    return b.op("scalar", lambda s: s.preload_act_table(func), deps=deps)


def get_const(b, g, val):
    d = g.setdefault("_consts", {})
    val = float(val)
    if val not in d:
        idx = len(d)
        ap = g["cbuf"][:, idx:idx + 1]
        tok = b.op("vector", lambda v, ap=ap: v.memset(ap, val))
        d[val] = (ap, tok)
    return d[val]



FFN_SECS = [(0, 6), (6, 6), (12, 5), (17, 5)]
FFN_SEC_OFF = [0, 6144, 12288, 17408]
FFN_SEC_OF_R = {}
for _si, (_c0, _nch) in enumerate(FFN_SECS):
    for _rr in range(_c0, _c0 + _nch):
        FFN_SEC_OF_R[_rr] = (_si, _rr - _c0)


def alloc_ffn_buffers(nc, g):
    g["X"] = nc.alloc_sbuf_tensor("Xsb", [128, NT * D], F32)
    g["xqT"] = nc.alloc_sbuf_tensor("xqT", [128, NT * D], BF16)
    g["acc"] = nc.alloc_sbuf_tensor("ffn_acc", [128, 8 * D], F32)
    g["wg_sb"] = nc.alloc_sbuf_tensor("wg_sb", [128, 8 * 768], BF16)
    g["wu_sb"] = nc.alloc_sbuf_tensor("wu_sb", [128, 8 * 768], BF16)
    g["wd_sb"] = nc.alloc_sbuf_tensor("wd_sb", [128, 6 * 1024], BF16)
    g["sg_sb"] = nc.alloc_sbuf_tensor("sg_sb", [128, 768], F32)
    g["h_sb"] = nc.alloc_sbuf_tensor("h_sb", [128, 768], F32)
    g["hq"] = nc.alloc_sbuf_tensor("hq", [128, 768], BF16)
    g["hqT"] = nc.alloc_sbuf_tensor("hqT", [128, 768], BF16)
    g["xn_bf"] = nc.alloc_sbuf_tensor("xn_bf", [128, D], BF16)
    g["xq_bf"] = nc.alloc_sbuf_tensor("xq_bf", [128, D], BF16)
    KQ = 1408
    g["qu"] = {k: nc.alloc_sbuf_tensor("qu_" + k, [128, KQ], BF16)
               for k in ("x2", "a2", "m0", "m1", "lvl", "sg", "is2e", "se")}
    g["qst"] = {k: nc.alloc_sbuf_tensor("qst_" + k, [128, KQ // 16], F32)
                for k in ("bm2", "L", "is2", "s")}
    g["lnst"] = {k: nc.alloc_sbuf_tensor("ln_" + k, [128, NT], F32)
                 for k in ("s1", "s2", "m", "v", "rstd", "mr")}
    g["cbuf"] = nc.alloc_sbuf_tensor("cbuf", [128, 16], F32)
    g["junk"] = g["qu"]["x2"]
    g["wq_stage"] = g["acc"][:, 0:I_FFN]
    g["wq_q"] = g["acc"][:, 2816:3520].bitcast(BF16)
    g["wq_tr"] = g["acc"][:, 3520:4224].bitcast(BF16)


def emit_ffn_weight_quant(b, g):
    eye = g["eye_sb"][:]
    done = []
    prev_tq = None
    prev_tr = None
    prev_store = None
    for src_ap, nrows, K, scratch in (
            (g["wg_d"], I_FFN, D, g["wgT_q"]),
            (g["wu_d"], I_FFN, D, g["wuT_q"]),
            (g["wd_d"], D, I_FFN, g["wdT_q"])):
        ntile = nrows // 128
        outdim = nrows
        for r in range(ntile):
            dtile = b.dma("sync", out=g["wq_stage"][:, 0:K],
                          in_=src_ap[r * 128:(r + 1) * 128, :],
                          deps=[prev_tq])
            for (o, w) in strips_of(K, mx=1408):
                srcs = g["wq_stage"][:, o:o + w]
                tq = emit_quant(b, g, g["qu"], g["qst"], srcs, g["wq_q"][:, 0:w],
                                w, [dtile, prev_tq, prev_tr])
                prev_tq = tq
                ttr = emit_transpose(b, g["pT"], g["wq_q"][:, 0:w],
                                     g["wq_tr"][:, 0:w], eye, w // 128,
                                     [tq, prev_store], psum_free=prev_tr)
                prev_tr = ttr
                if K == D:
                    si, fr = FFN_SEC_OF_R[r]
                    i_s = 128 * FFN_SECS[si][1]
                    dst = scratch.ap()[:, FFN_SEC_OFF[si]:
                                       FFN_SEC_OFF[si] + 8 * i_s].rearrange(
                        "p (cc f) -> p cc f", f=i_s)[
                        :, :, fr * 128:(fr + 1) * 128]
                else:
                    dst = scratch.ap().rearrange("p (cc f) -> p cc f",
                                                 f=1024)[
                        :, o // 128:(o + w) // 128, r * 128:(r + 1) * 128]
                prev_store = b.dma("sync", out=dst,
                                   in_=g["wq_tr"][:, 0:w].rearrange(
                                       "p (cc f) -> p cc f", f=128),
                                   deps=[ttr])
        done.append(prev_store)
    return done, prev_tq, prev_tr


def emit_ffn_phaseA(b, g, prev_tq, prev_tr):
    st = g["lnst"]
    stat_toks = []
    s2_toks = []
    for i in range(NT):
        Xi = g["X"][:, i * D:(i + 1) * D]
        t1, t2 = emit_ln_stats(b, Xi, st["s1"][:, i:i + 1],
                               st["s2"][:, i:i + 1], g["junk"][:, 0:D],
                               deps=[prev_tq] if i == 0 else ())
        stat_toks += [t1, t2]
        s2_toks.append(t2)
    tmr, trs = emit_ln_batch(b, g, st, NT, stat_toks)
    out_toks = []
    for i in range(NT):
        Xi = g["X"][:, i * D:(i + 1) * D]
        tap = emit_ln_apply(b, Xi, g["xn_bf"][:], st["rstd"][:, i:i + 1],
                            st["mr"][:, i:i + 1], [tmr, trs, prev_tq])
        tq = emit_quant(b, g, g["qu"], g["qst"], g["xn_bf"][:], g["xq_bf"][:],
                        D, [tap, prev_tr])
        prev_tq = tq
        ttr = emit_transpose(b, g["pT"], g["xq_bf"][:],
                             g["xqT"][:, i * D:(i + 1) * D], g["eye_sb"][:],
                             8, [tq], psum_free=prev_tr)
        prev_tr = ttr
        out_toks.append(ttr)
    return out_toks, s2_toks, prev_tq, prev_tr


def emit_ffn_phaseB(b, g, xqT_toks, s2_toks, wq_done, prev_tq, prev_tr, taps=None,
                    nhalves=2, nsecs=4, ntiles=8):
    res_toks = []
    acc_reader = {}
    acc_writer = {}
    prev_pE_evac = None
    prev_silu = None
    prev_h = None
    prev_w_last_use = None
    for half in range(nhalves):
        for si, (c0, nch) in enumerate(FFN_SECS[:nsecs]):
            i_s = nch * 128
            wdeps = list(wq_done) + [prev_w_last_use]
            tg = b.dma("sync", out=g["wg_sb"][:, 0:8 * i_s],
                       in_=g["wgT_q"].ap()[:, FFN_SEC_OFF[si]:
                                           FFN_SEC_OFF[si] + 8 * i_s],
                       deps=wdeps)
            tu = b.dma("sync", out=g["wu_sb"][:, 0:8 * i_s],
                       in_=g["wuT_q"].ap()[:, FFN_SEC_OFF[si]:
                                           FFN_SEC_OFF[si] + 8 * i_s],
                       deps=wdeps)
            td = b.dma("scalar", out=g["wd_sb"][:, 0:nch * 1024],
                       in_=g["wdT_q"].ap()[:, c0 * 1024:(c0 + nch) * 1024],
                       deps=wdeps)
            for tj in range(ntiles):
                tile = half * 8 + tj
                lhs = [g["xqT"][:, tile * D + c * 128: tile * D + (c + 1) * 128]
                       for c in range(8)]
                xdep = xqT_toks[tile]
                h_toks = []
                for (o, w) in strips_of(i_s):
                    mm = None
                    for c in range(8):
                        mm = b.op("tensor", lambda t, oo=g["pA"][:, 0:w],
                                  ll=lhs[c],
                                  rr=g["wg_sb"][:, c * i_s + o:c * i_s + o + w],
                                  st=(c == 0), sp=(c == 7): t.matmul(
                            out=oo, lhsT=ll, rhs=rr, start=st, stop=sp),
                            deps=[tg, xdep, prev_silu] if c == 0 else ())
                    prev_silu = b.op("scalar", lambda s,
                                     oo=g["sg_sb"][:, o:o + w],
                                     ii=g["pA"][:, 0:w]: s.activation(
                        out=oo, in_=ii, func=AF.Silu), deps=[mm, prev_tq])
                    mm = None
                    for c in range(8):
                        mm = b.op("tensor", lambda t, oo=g["pB"][:, 0:w],
                                  ll=lhs[c],
                                  rr=g["wu_sb"][:, c * i_s + o:c * i_s + o + w],
                                  st=(c == 0), sp=(c == 7): t.matmul(
                            out=oo, lhsT=ll, rhs=rr, start=st, stop=sp),
                            deps=[tu, xdep, prev_h] if c == 0 else ())
                    prev_h = b.op("vector", lambda v,
                                  oo=g["h_sb"][:, o:o + w],
                                  i0=g["pB"][:, 0:w],
                                  i1=g["sg_sb"][:, o:o + w]: v.tensor_mul(
                        out=oo, in0=i0, in1=i1),
                        deps=[mm, prev_silu, prev_tq])
                    h_toks.append(prev_h)
                if taps is not None and tile == 0 and si == 0:
                    taps.append(b.dma("gpsimd", out=g["h_tap"], in_=g["h_sb"][:, 0:i_s],
                                      deps=h_toks))
                    taps.append(b.dma("gpsimd", out=g["sg_tap"], in_=g["sg_sb"][:, 0:i_s],
                                      deps=h_toks))
                    h_toks = h_toks + [taps[-1], taps[-2]]
                tqh = emit_quant(b, g, g["qu"], g["qst"], g["h_sb"][:, 0:i_s],
                                 g["hq"][:, 0:i_s], i_s,
                                 h_toks + [prev_tq, prev_tr])
                prev_tq = tqh
                if taps is not None and tile == 0 and si == 0:
                    taps.append(b.dma("gpsimd", out=g["hq_tap"], in_=g["hq"][:, 0:i_s],
                                      deps=[tqh]))
                    prev_tq = (prev_tq, taps[-1])[0]
                thT = emit_transpose(b, g["pT"], g["hq"][:, 0:i_s],
                                     g["hqT"][:, 0:i_s], g["eye_sb"][:],
                                     nch, [tqh, prev_pE_evac], psum_free=prev_tr)
                prev_tr = thT
                mm_last = None
                for (o, w) in ((0, 512), (512, 512)):
                    for c in range(nch):
                        mm_last = b.op("tensor", lambda t,
                                       oo=g["pE"][:, o:o + w],
                                       ll=g["hqT"][:, c * 128:(c + 1) * 128],
                                       rr=g["wd_sb"][:, c * 1024 + o:c * 1024 + o + w],
                                       st=(c == 0), sp=(c == nch - 1): t.matmul(
                            out=oo, lhsT=ll, rhs=rr, start=st, stop=sp),
                            deps=[thT, td, prev_pE_evac]
                            if (c == 0 and o == 0) else ())
                prev_w_last_use = mm_last
                accv = g["acc"][:, tj * D:(tj + 1) * D]
                edeps = [mm_last, acc_writer.get(tile), acc_reader.get(tj)]
                if si == 0:
                    tev = b.op("scalar", lambda s, accv=accv: s.activation(
                        out=accv, in_=g["pE"][:], func=AF.Identity, scale=0.5),
                        deps=edeps)
                else:
                    tev = b.op("vector",
                               lambda v, accv=accv: v.scalar_tensor_tensor(
                                   out=accv, in0=g["pE"][:], scalar=0.5,
                                   in1=accv, op0=ALU.mult, op1=ALU.add),
                               deps=edeps)
                acc_writer[tile] = tev
                prev_pE_evac = tev
                if taps is not None and tile == 0:
                    taps.append(b.dma("gpsimd", out=g["acc_tap"][si], in_=accv,
                                      deps=[tev]))
                    prev_pE_evac = taps[-1]
                    acc_writer[tile] = taps[-1]
                if si == nsecs - 1:
                    Xi = g["X"][:, tile * D:(tile + 1) * D]
                    tres = b.op("vector",
                                lambda v, Xi=Xi, accv=accv: v.tensor_add(
                                    out=Xi, in0=Xi, in1=accv),
                                deps=[tev, s2_toks[tile]])
                    acc_reader[tj] = tres
                    res_toks.append(tres)
    return res_toks, prev_tq, prev_tr


def emit_ffn0(b, g, quantize=True, wq_state=None):
    if quantize:
        wq_done, ptq, ptr = emit_ffn_weight_quant(b, g)
    else:
        wq_done, ptq, ptr = wq_state
    xqT_toks, s2_toks, ptq, ptr = emit_ffn_phaseA(b, g, ptq, ptr)
    import os
    res, ptq, ptr = emit_ffn_phaseB(b, g, xqT_toks, s2_toks, wq_done, ptq, ptr,
                                    taps=g.get("_taps"),
                                    nhalves=int(os.environ.get("FFN_NH", "2")),
                                    nsecs=int(os.environ.get("FFN_NS", "4")),
                                    ntiles=int(os.environ.get("FFN_NT", "8")))
    return res, (wq_done, ptq, ptr)



def build_ffn0_program(debug=False, wq_only=False, export_wq=False,
                       prequant=False):
    nc = bass.Bass("TRN2", target_bir_lowering=False)
    b = Bld(nc)
    g = {"nc": nc}
    alloc_ffn_buffers(nc, g)
    g["eye_sb"] = nc.alloc_sbuf_tensor("eye_sb", [128, 128], BF16)
    x_in = nc.dram_tensor("x", [T, D], F32, kind="ExternalInput").ap()
    y_out = nc.dram_tensor("y", [T, D], F32, kind="ExternalOutput").ap()
    if not prequant:
        g["wg_d"] = nc.dram_tensor("w_gate", [I_FFN, D], F32,
                                   kind="ExternalInput").ap()
        g["wu_d"] = nc.dram_tensor("w_up", [I_FFN, D], F32,
                                   kind="ExternalInput").ap()
        g["wd_d"] = nc.dram_tensor("w_down", [D, I_FFN], F32,
                                   kind="ExternalInput").ap()
    eye_d = nc.dram_tensor("eye", [128, 128], BF16, kind="ExternalInput").ap()
    if prequant:
        kind = "ExternalInput"
    else:
        kind = "ExternalOutput" if (debug or export_wq) else "Internal"
    g["wgT_q"] = nc.dram_tensor("wgT_q", [128, 22528], BF16, kind=kind)
    g["wuT_q"] = nc.dram_tensor("wuT_q", [128, 22528], BF16, kind=kind)
    g["wdT_q"] = nc.dram_tensor("wdT_q", [128, 22528], BF16, kind=kind)
    if debug:
        g["xqT_dump"] = nc.dram_tensor("xqT_dump", [128, NT * D], BF16,
                                       kind="ExternalOutput").ap()
        g["xln_tap_d"] = nc.dram_tensor("xln_tap", [128, D], BF16,
                                        kind="ExternalOutput").ap() if False else None
        g["acc_tap"] = [nc.dram_tensor("acc_tap%d" % i, [128, D], F32,
                                       kind="ExternalOutput").ap() for i in range(4)]
        g["h_tap"] = nc.dram_tensor("h_tap", [128, 768], F32, kind="ExternalOutput").ap()
        g["sg_tap"] = nc.dram_tensor("sg_tap", [128, 768], F32, kind="ExternalOutput").ap()
        g["hq_tap"] = nc.dram_tensor("hq_tap", [128, 768], BF16, kind="ExternalOutput").ap()
        g["_taps"] = []

    with (
        nc.psum_tensor("pA", [128, 512], F32) as pA,
        nc.psum_tensor("pB", [128, 512], F32) as pB,
        nc.psum_tensor("pE", [128, 1024], F32) as pE,
        nc.psum_tensor("pT", [128, 1024], BF16) as pT,
    ):
        g["pA"], g["pB"], g["pE"], g["pT"] = pA, pB, pE, pT
        teye = b.dma("gpsimd", out=g["eye_sb"][:], in_=eye_d)
        tx_toks = []
        for hh in range(2):
            tx_toks.append(b.dma("gpsimd",
                           out=g["X"][:, hh * 8 * D:(hh + 1) * 8 * D].rearrange(
                               "p (i f) -> p i f", f=D),
                           in_=x_in.rearrange("(i p) f -> p i f", p=128)
                           [:, hh * 8:(hh + 1) * 8, :]))
        tx = tx_toks
        for e in ("vector", "scalar", "tensor"):
            b.wait(e, [teye] + tx)
        if wq_only:
            wq_done, ptq, ptr = emit_ffn_weight_quant(b, g)
            res = wq_done
        elif prequant:
            res, _ = emit_ffn0(b, g, quantize=False,
                               wq_state=([], None, None))
        else:
            res, _ = emit_ffn0(b, g, quantize=True)
        if debug:
            res = list(res)
            res.append(b.dma("gpsimd", out=g["xqT_dump"], in_=g["xqT"][:],
                             deps=list(res)))
        tst = []
        for hh in range(2):
            tst.append(b.dma("gpsimd",
                       out=y_out.rearrange("(i p) f -> p i f", p=128)
                       [:, hh * 8:(hh + 1) * 8, :],
                       in_=g["X"][:, hh * 8 * D:(hh + 1) * 8 * D].rearrange(
                           "p (i f) -> p i f", f=D),
                       deps=res))
        b.wait("gpsimd", tst)
        nc_built = b.build()
    return nc_built





def alloc_attn0_buffers(nc, g):
    g["q_tm"] = g["junk"]
    g["xln"] = nc.alloc_sbuf_tensor("xln", [128, NT * D], BF16)
    g["xlnT"] = nc.alloc_sbuf_tensor("xlnT", [128, 8 * T], BF16)
    g["kT"] = nc.alloc_sbuf_tensor("kT", [128, 8 * T], BF16)
    g["qTi"] = nc.alloc_sbuf_tensor("qTi", [128, 8 * 128], BF16)
    g["WqT_bf"] = nc.alloc_sbuf_tensor("WqT_bf", [128, 8 * 1024], BF16)
    g["Wstage"] = nc.alloc_sbuf_tensor("Wstage", [128, 1024], F32)
    g["Wk_bf"] = nc.alloc_sbuf_tensor("Wk_bf", [128, 2 * 1024], BF16)
    g["S_row"] = nc.alloc_sbuf_tensor("S_row", [128, T], BF16)
    g["P_bf"] = g["Wstage"][:].bitcast(BF16)
    g["PT_sb"] = nc.alloc_sbuf_tensor("PT_sb", [128, 256], BF16)
    g["s_bc"] = nc.alloc_sbuf_tensor("s_bc", [128, D], F32)
    g["o_sb"] = g["Wk_bf"][:].bitcast(F32)
    g["att_st"] = {k: nc.alloc_sbuf_tensor("att_" + k, [128, NT], F32)
                   for k in ("m4a", "m1", "bm", "l", "rl")}
    g["srow_s"] = nc.alloc_sbuf_tensor("srow_s", [1, 1024], F32)
    g["srow_bf"] = nc.alloc_sbuf_tensor("srow_bf", [1, 1024], BF16)
    g["bq_row_bf"] = nc.alloc_sbuf_tensor("bq_row_bf", [1, 1024], BF16)
    g["bk_cols_sb"] = nc.alloc_sbuf_tensor("bk_cols_sb", [128, 8], F32)
    g["ones_row"] = nc.alloc_sbuf_tensor("ones_row", [1, 128], BF16)
    g["ones_col"] = nc.alloc_sbuf_tensor("ones_col", [128, 1], BF16)


def emit_attn0(b, g, tin):
    st = g["lnst"]
    stat_toks = []
    s2_toks = []
    for i in range(NT):
        Xi = g["X"][:, i * D:(i + 1) * D]
        t1, t2 = emit_ln_stats(b, Xi, st["s1"][:, i:i + 1], st["s2"][:, i:i + 1],
                               g["junk"][:, 0:D], deps=tin if i == 0 else ())
        stat_toks += [t1, t2]
        s2_toks.append(t2)
    tmr, trs = emit_ln_batch(b, g, st, NT, stat_toks)
    prev_tr = g.get("attn_ptr")
    xln_toks = []
    xlnT_toks = []
    cs_last = None
    for i in range(NT):
        Xi = g["X"][:, i * D:(i + 1) * D]
        xln_i = g["xln"][:, i * D:(i + 1) * D]
        tap = emit_ln_apply(b, Xi, xln_i, st["rstd"][:, i:i + 1],
                            st["mr"][:, i:i + 1], [tmr, trs])
        xln_toks.append(tap)
        d = [tap]
        if prev_tr is not None:
            d.append(prev_tr)
        pe_last = None
        for c in range(8):
            pe_last = b.op("tensor", lambda t,
                           oo=g["pT"][:, c * 128:(c + 1) * 128],
                           ii=xln_i[:, c * 128:(c + 1) * 128],
                           ee=g["eye_sb"][:]: t.transpose(out=oo, in_=ii,
                                                          identity=ee),
                           deps=d if c == 0 else ())
        tc = b.op("vector", lambda v, i=i: v.tensor_copy(
            out=g["xlnT"][:].rearrange("p (c t) -> p c t", t=T)
            [:, :, i * 128:(i + 1) * 128],
            in_=g["pT"][:].rearrange("p (c f) -> p c f", f=128)), deps=[pe_last])
        prev_tr = tc
        xlnT_toks.append(tc)
        for (o, w) in ((0, 512), (512, 512)):
            cs_last = b.op("tensor", lambda t, oo=g["pF"][:, o:o + w],
                           ll=g["ones_col"][:], rr=xln_i[:, o:o + w],
                           st_=(i == 0), sp=(i == NT - 1): t.matmul(
                out=oo, lhsT=ll, rhs=rr, start=st_, stop=sp),
                deps=[tap] if (o == 0 and i == 0) else ())
    g["attn_ptr"] = prev_tr
    tse = b.op("vector", lambda v: v.tensor_scalar(
        out=g["srow_s"][:], in0=g["pF"][:, 0:1024], scalar1=-2.0 / T,
        scalar2=None, op0=ALU.mult), deps=[cs_last])
    tsb = b.op("vector", lambda v: v.tensor_copy(out=g["srow_bf"][:],
                                                 in_=g["srow_s"][:]),
               deps=[tse])
    sbc_toks = []
    for o in range(0, 1024, 256):
        tmmb = b.op("tensor", lambda t, oo=g["pS"][:, 0:256],
                    ll=g["ones_row"][:], rr=g["srow_bf"][:, o:o + 256]: t.matmul(
            out=oo, lhsT=ll, rhs=rr, start=True, stop=True),
            deps=[tsb, tse] + sbc_toks[-1:])
        sbc_toks.append(b.op("vector", lambda v, o=o: v.tensor_copy(
            out=g["s_bc"][:, o:o + 256], in_=g["pS"][:, 0:256]), deps=[tmmb]))
    kT_toks = []
    prev_kevac = sbc_toks[-1]
    wk_prev_use = g["tWq"]
    for ncb in range(8):
        dW = b.dma("sync", out=g["Wstage"][:, 0:1024].rearrange(
            "p (c f) -> p c f", f=128),
            in_=g["WkT_d"].rearrange("(c p) n -> p c n", p=128)
            [:, :, ncb * 128:(ncb + 1) * 128], deps=[wk_prev_use])
        slot = (ncb % 2) * 1024
        tcast = b.op("vector", lambda v, slot=slot: v.tensor_copy(
            out=g["Wk_bf"][:, slot:slot + 1024], in_=g["Wstage"][:, 0:1024]),
            deps=[dW, wk_prev_use])
        for js in range(4):
            mm = None
            for dc in range(8):
                mm = b.op("tensor", lambda t, oo=g["pS"][:, 0:512],
                          ll=g["Wk_bf"][:, slot + dc * 128: slot + (dc + 1) * 128],
                          rr=g["xlnT"][:].rearrange("p (c t) -> p c t", t=T)
                          [:, dc, js * 512:(js + 1) * 512],
                          st_=(dc == 0), sp=(dc == 7): t.matmul(
                    out=oo, lhsT=ll, rhs=rr, start=st_, stop=sp),
                    deps=[tcast, xlnT_toks[-1], prev_kevac]
                    if dc == 0 else ())
            prev_kevac = b.op("scalar", lambda s, ncb=ncb, js=js: s.activation(
                out=g["kT"][:].rearrange("p (c t) -> p c t", t=T)
                [:, ncb, js * 512:(js + 1) * 512],
                in_=g["pS"][:, 0:512], func=AF.Identity,
                bias=g["bk_cols_sb"][:, ncb:ncb + 1]), deps=[mm])
            kT_toks.append(prev_kevac)
        wk_prev_use = mm
    res_toks = []
    ast = g["att_st"]
    prev_oevac = None
    prev_sevac = None
    prev_qevac = None
    prev_smm_last = None
    prev_pv_tcopy = None
    prev_tqT = None
    import os as _os
    nt_c = int(_os.environ.get("ATTN_NT", str(NT)))
    for i in range(nt_c):
        xln_i = g["xln"][:, i * D:(i + 1) * D]
        qev = []
        for (o, w) in ((0, 512), (512, 512)):
            mm = None
            for dc in range(8):
                mm = b.op("tensor", lambda t, oo=g["pA"][:, 0:w],
                          ll=g["xlnT"][:].rearrange("p (c t) -> p c t", t=T)
                          [:, dc, i * 128:(i + 1) * 128],
                          rr=g["WqT_bf"][:].rearrange("p (c n) -> p c n", n=1024)
                          [:, dc, o:o + w],
                          st_=(dc == 0), sp=False: t.matmul(
                    out=oo, lhsT=ll, rhs=rr, start=st_, stop=sp),
                    deps=[g["tWq"], xlnT_toks[i], prev_qevac, prev_tqT]
                    if dc == 0 else ())
            mm = b.op("tensor", lambda t, oo=g["pA"][:, 0:256],
                      ll=g["ones_row"][:], rr=g["bq_row_bf"][:, o:o + 256]:
                      t.matmul(out=oo, lhsT=ll, rhs=rr, start=False, stop=False),
                      deps=[g["tbq"]])
            mm = b.op("tensor", lambda t, oo=g["pA"][:, 256:512],
                      ll=g["ones_row"][:], rr=g["bq_row_bf"][:, o + 256:o + 512]:
                      t.matmul(out=oo, lhsT=ll, rhs=rr, start=False, stop=True),
                      deps=())
            prev_qevac = b.op("scalar", lambda s, o=o, w=w: s.activation(
                out=g["q_tm"][:, o:o + w], in_=g["pA"][:, 0:w],
                func=AF.Identity), deps=[mm])
            qev.append(prev_qevac)
        tqT = emit_transpose(b, g["pT"], g["q_tm"][:], g["qTi"][:],
                             g["eye_sb"][:], 8, qev + [prev_smm_last],
                             psum_free=g["attn_ptr"])
        g["attn_ptr"] = tqT
        prev_tqT = tqT
        sm_toks = []
        for js in range(4):
            mm = None
            for dc in range(8):
                mm = b.op("tensor", lambda t, oo=g["pS"][:, 0:512],
                          ll=g["qTi"][:, dc * 128:(dc + 1) * 128],
                          rr=g["kT"][:].rearrange("p (c t) -> p c t", t=T)
                          [:, dc, js * 512:(js + 1) * 512],
                          st_=(dc == 0), sp=(dc == 7): t.matmul(
                    out=oo, lhsT=ll, rhs=rr, start=st_, stop=sp),
                    deps=[tqT, kT_toks[-1], prev_sevac] if dc == 0 else ())
            prev_smm_last = mm
            prev_sevac = b.op("scalar", lambda s, js=js: s.activation(
                out=g["S_row"][:, js * 512:(js + 1) * 512],
                in_=g["pS"][:, 0:512], func=AF.Identity), deps=[mm])
            sm_toks.append(b.op("vector", lambda v, js=js, i=i: v.reduce_max(
                out=ast["m4a"][:, js:js + 1]
                if False else g["m4buf"][:, js:js + 1],
                in_=g["S_row"][:, js * 512:(js + 1) * 512], axis=AX.X),
                deps=[prev_sevac]))
        tm1 = b.op("vector", lambda v, i=i: v.reduce_max(
            out=ast["m1"][:, i:i + 1], in_=g["m4buf"][:, 0:4], axis=AX.X),
            deps=sm_toks)
        tbm = b.op("vector", lambda v, i=i: v.tensor_scalar(
            out=ast["bm"][:, i:i + 1], in0=ast["m1"][:, i:i + 1],
            scalar1=-1.0 / 32.0, scalar2=None, op0=ALU.mult), deps=[tm1])
        tP = b.op("scalar", lambda s, i=i: s.activation(
            out=g["P_bf"][:], in_=g["S_row"][:], func=AF.Exp, scale=1.0 / 32.0,
            bias=ast["bm"][:, i:i + 1]),
            deps=[tbm, prev_sevac, prev_pv_tcopy])
        tl = b.op("vector", lambda v, i=i: v.reduce_sum(
            out=ast["l"][:, i:i + 1], in_=g["P_bf"][:], axis=AX.X), deps=[tP])
        trl = b.op("vector", lambda v, i=i: v.reciprocal(
            out=ast["rl"][:, i:i + 1], in_=ast["l"][:, i:i + 1]), deps=[tl])
        mmv = None
        tcopy = [None, None]
        mm_slot = [None, None]
        for jb in range(NT):
            slot = jb % 2
            dtr = [tP]
            if tcopy[slot] is not None:
                dtr.append(tcopy[slot])
            if prev_oevac is not None:
                dtr.append(prev_oevac)
            pe_t = b.op("tensor", lambda t,
                        oo=g["pT"][:, slot * 128:(slot + 1) * 128],
                        ii=g["P_bf"][:, jb * 128:(jb + 1) * 128],
                        ee=g["eye_sb"][:]: t.transpose(out=oo, in_=ii,
                                                       identity=ee),
                        deps=dtr)
            tcopy[slot] = b.op("vector", lambda v, slot=slot: v.tensor_copy(
                out=g["PT_sb"][:, slot * 128:(slot + 1) * 128],
                in_=g["pT"][:, slot * 128:(slot + 1) * 128]),
                deps=[pe_t, mm_slot[slot]])
            for (o, w) in ((0, 512), (512, 512)):
                mmv = b.op("tensor", lambda t, oo=g["pE"][:, o:o + w],
                           ll=g["PT_sb"][:, slot * 128:(slot + 1) * 128],
                           rr=g["xln"][:, jb * D + o: jb * D + o + w],
                           st_=(jb == 0), sp=(jb == NT - 1): t.matmul(
                    out=oo, lhsT=ll, rhs=rr, start=st_, stop=sp),
                    deps=[tcopy[slot], xln_toks[jb]] if o == 0 else ())
            mm_slot[slot] = mmv
        prev_pv_tcopy = tcopy[(NT - 1) % 2]
        g["attn_ptr"] = prev_pv_tcopy
        tev = b.op("vector", lambda v, i=i: v.tensor_scalar(
            out=g["o_sb"][:], in0=g["pE"][:], scalar1=ast["rl"][:, i:i + 1],
            scalar2=None, op0=ALU.mult), deps=[mmv, trl])
        prev_oevac = tev
        Xi = g["X"][:, i * D:(i + 1) * D]
        nres = int(_os.environ.get("ATTN_RES", "3"))
        tr = b.op("vector", lambda v, Xi=Xi: v.tensor_add(
            out=Xi, in0=Xi, in1=g["o_sb"][:]), deps=[tev, s2_toks[i]])
        if nres >= 2:
            alt = _os.environ.get("ATTN_ALT")
            if alt == "o2":
                tr = b.op("vector", lambda v, Xi=Xi: v.tensor_add(
                    out=Xi, in0=Xi, in1=g["o_sb"][:]), deps=[tr])
            elif alt == "tapxln":
                tp = b.dma("gpsimd", out=g["xln_tap_d"], in_=xln_i, deps=[tr])
                tr = b.op("vector", lambda v, Xi=Xi, xln_i=xln_i: v.tensor_add(
                    out=Xi, in0=Xi, in1=xln_i), deps=[tr, tp])
            else:
                tr = b.op("vector", lambda v, Xi=Xi, xln_i=xln_i: v.tensor_add(
                    out=Xi, in0=Xi, in1=xln_i), deps=[tr])
        if nres >= 3:
            tr = b.op("vector", lambda v, Xi=Xi: v.tensor_add(
                out=Xi, in0=Xi, in1=g["s_bc"][:]), deps=[tr] + sbc_toks)
        res_toks.append(tr)
    return res_toks


def build_attn0_program(debug=False):
    nc = bass.Bass("TRN2", target_bir_lowering=False)
    b = Bld(nc)
    g = {"nc": nc}
    g["X"] = nc.alloc_sbuf_tensor("Xsb", [128, NT * D], F32)
    g["lnst"] = {k: nc.alloc_sbuf_tensor("ln_" + k, [128, NT], F32)
                 for k in ("s1", "s2", "m", "v", "rstd", "mr")}
    g["junk"] = nc.alloc_sbuf_tensor("junk", [128, D], BF16)
    g["cbuf"] = nc.alloc_sbuf_tensor("cbuf", [128, 16], F32)
    g["m4buf"] = nc.alloc_sbuf_tensor("m4buf", [128, 4], F32)
    alloc_attn0_buffers(nc, g)
    g["eye_sb"] = nc.alloc_sbuf_tensor("eye_sb", [128, 128], BF16)
    x_in = nc.dram_tensor("x", [T, D], F32, kind="ExternalInput").ap()
    y_out = nc.dram_tensor("y", [T, D], F32, kind="ExternalOutput").ap()
    g["WqT_d"] = nc.dram_tensor("WqT", [D, D], F32, kind="ExternalInput").ap()
    g["WkT_d"] = nc.dram_tensor("WkT", [D, D], F32, kind="ExternalInput").ap()
    bqr = nc.dram_tensor("bq_row", [1, D], F32, kind="ExternalInput").ap()
    bkc = nc.dram_tensor("bk_cols", [128, 8], F32, kind="ExternalInput").ap()
    eye_d = nc.dram_tensor("eye", [128, 128], BF16, kind="ExternalInput").ap()
    onr = nc.dram_tensor("ones_row_d", [1, 128], BF16, kind="ExternalInput").ap()
    onc_d = nc.dram_tensor("ones_col_d", [128, 1], BF16, kind="ExternalInput").ap()

    dbg_outs = {}
    if debug:
        g_xln_tap = nc.dram_tensor("xln_tap", [128, D], BF16, kind="ExternalOutput").ap()
        for nm, sh, dt in (("d_xln", [128, NT * D], BF16), ("d_kT", [128, 8 * T], BF16),
                           ("d_S", [128, T], F32), ("d_P", [128, T], BF16),
                           ("d_l", [128, NT], F32), ("d_rl", [128, NT], F32),
                           ("d_sbc", [128, D], F32), ("d_o", [128, D], F32),
                           ("d_qTi", [128, 1024], BF16), ("d_xlnT", [128, 8 * T], BF16)):
            dbg_outs[nm] = nc.dram_tensor(nm, sh, dt, kind="ExternalOutput").ap()
    with (
        nc.psum_tensor("pA", [128, 512], F32) as pA,
        nc.psum_tensor("pS", [128, 512], F32) as pS,
        nc.psum_tensor("pE", [128, 1024], F32) as pE,
        nc.psum_tensor("pT", [128, 1024], BF16) as pT,
        nc.psum_tensor("pF", [1, 1024], F32) as pF,
    ):
        g["pA"], g["pS"], g["pE"], g["pT"], g["pF"] = pA, pS, pE, pT, pF
        if debug:
            g["xln_tap_d"] = g_xln_tap
        teye = b.dma("gpsimd", out=g["eye_sb"][:], in_=eye_d)
        tonr = b.dma("gpsimd", out=g["ones_row"][:], in_=onr)
        tonc = b.dma("gpsimd", out=g["ones_col"][:], in_=onc_d)
        tbkc = b.dma("gpsimd", out=g["bk_cols_sb"][:], in_=bkc)
        tx_toks = []
        for hh in range(2):
            tx_toks.append(b.dma("gpsimd",
                           out=g["X"][:, hh * 8 * D:(hh + 1) * 8 * D].rearrange(
                               "p (i f) -> p i f", f=D),
                           in_=x_in.rearrange("(i p) f -> p i f", p=128)
                           [:, hh * 8:(hh + 1) * 8, :]))
        tx = tx_toks
        tWq_parts = []
        prev_cast = None
        for c in range(8):
            dW = b.dma("sync", out=g["Wstage"][:, 0:1024],
                       in_=g["WqT_d"][c * 128:(c + 1) * 128, :],
                       deps=[prev_cast])
            prev_cast = b.op("vector", lambda v, c=c: v.tensor_copy(
                out=g["WqT_bf"][:].rearrange("p (c n) -> p c n", n=1024)[:, c, :],
                in_=g["Wstage"][:, 0:1024]), deps=[dW])
            tWq_parts.append(prev_cast)
        g["tWq"] = tWq_parts[-1]
        dbq = b.dma("sync", out=g["srow_s"][:], in_=bqr)
        g["tbq"] = b.op("vector", lambda v: v.tensor_copy(
            out=g["bq_row_bf"][:], in_=g["srow_s"][:]), deps=[dbq])
        for e in ("vector", "scalar", "tensor"):
            b.wait(e, [teye, tonr, tonc, tbkc] + tx)
        res = emit_attn0(b, g, tx + [g["tbq"]])
        if debug:
            for nm, src in (("d_xln", g["xln"][:]), ("d_kT", g["kT"][:]),
                            ("d_S", g["S_row"][:]), ("d_P", g["P_bf"][:]),
                            ("d_l", g["att_st"]["l"][:]), ("d_rl", g["att_st"]["rl"][:]),
                            ("d_sbc", g["s_bc"][:]), ("d_o", g["o_sb"][:]),
                            ("d_qTi", g["qTi"][:]), ("d_xlnT", g["xlnT"][:])):
                res = res + [b.dma("scalar", out=dbg_outs[nm], in_=src, deps=res[-1:])]
        tst = []
        for hh in range(2):
            tst.append(b.dma("gpsimd",
                       out=y_out.rearrange("(i p) f -> p i f", p=128)
                       [:, hh * 8:(hh + 1) * 8, :],
                       in_=g["X"][:, hh * 8 * D:(hh + 1) * 8 * D].rearrange(
                           "p (i f) -> p i f", f=D),
                       deps=res))
        b.wait("gpsimd", tst)
        nc_built = b.build()
    return nc_built





def alloc_moe_buffers(nc, g):
    g["xnT_h"] = nc.alloc_sbuf_tensor("xnT_h", [128, 8 * D], BF16)
    g["macc"] = nc.alloc_sbuf_tensor("macc", [128, 8 * D], F32)
    g["ew_bf"] = nc.alloc_sbuf_tensor("ew_bf", [128, 8 * D], BF16)
    g["ew_stage"] = nc.alloc_sbuf_tensor("ew_stage", [128, D], F32)
    g["rw_bf"] = nc.alloc_sbuf_tensor("rw_bf", [128, 64], BF16)
    g["eb_bf"] = nc.alloc_sbuf_tensor("eb_bf", [8, D], BF16)
    g["rb_row"] = nc.alloc_sbuf_tensor("rb_row", [1, 8], BF16)
    g["g8"] = nc.alloc_sbuf_tensor("g8", [128, 64], F32)
    g["g8bf"] = nc.alloc_sbuf_tensor("g8bf", [128, 8], BF16)
    g["gT_all"] = nc.alloc_sbuf_tensor("gT_all", [8, 8 * 128], BF16)
    g["e8"] = nc.alloc_sbuf_tensor("e8", [128, 8], F32)
    g["mask8"] = nc.alloc_sbuf_tensor("mask8", [128, 8], F32)
    g["mhave"] = nc.alloc_sbuf_tensor("mhave", [128, 2], F32)
    g["mst"] = {k: nc.alloc_sbuf_tensor("mst_" + k, [128, NT], F32)
                for k in ("m8", "bm8", "l8", "gate")}
    g["xn_bf2"] = nc.alloc_sbuf_tensor("xn_bf2", [128, D], BF16)


def emit_moe(b, g, tin):
    import os as _os
    ser_a = bool(_os.environ.get("MOE_SER_A"))
    ser_b = bool(_os.environ.get("MOE_SER_B"))
    st = g["lnst"]
    res_toks = []
    prev_tr = g.get("moe_ptr")
    prev_ep_evac = None
    prev_racc = {}
    for half in range(2):
        b.serial = ser_a
        b._last = None
        stat_toks = []
        s2_toks = {}
        for tj in range(8):
            i = half * 8 + tj
            Xi = g["X"][:, i * D:(i + 1) * D]
            t1, t2 = emit_ln_stats(b, Xi, st["s1"][:, tj:tj + 1],
                                   st["s2"][:, tj:tj + 1], g["junk"][:, 0:D],
                                   deps=(list(tin) + [g.get("moe_lnfree")])
                                   if tj == 0 else ())
            stat_toks += [t1, t2]
            s2_toks[i] = t2
        tmr, trs = emit_ln_batch(b, g, st, 8, stat_toks)
        g["moe_lnfree"] = tmr
        xnT_toks = {}
        gdone = {}
        prev_small = None
        for tj in range(8):
            i = half * 8 + tj
            Xi = g["X"][:, i * D:(i + 1) * D]
            tap = emit_ln_apply(b, Xi, g["xn_bf2"][:], st["rstd"][:, tj:tj + 1],
                                st["mr"][:, tj:tj + 1],
                                [tmr, trs, prev_tr, prev_ep_evac])
            ttr = emit_transpose(b, g["pT"], g["xn_bf2"][:],
                                 g["xnT_h"][:, tj * D:(tj + 1) * D],
                                 g["eye_sb"][:], 8, [tap], psum_free=prev_tr)
            prev_tr = ttr
            xnT_toks[i] = ttr
            mm = None
            for dc in range(8):
                mm = b.op("tensor", lambda t, oo=g["pA"][:, 0:8],
                          ll=g["xnT_h"][:, tj * D + dc * 128:tj * D + (dc + 1) * 128],
                          rr=g["rw_bf"][:].rearrange("p (c e) -> p c e", e=8)[:, dc, :],
                          st_=(dc == 0), sp=False: t.matmul(
                    out=oo, lhsT=ll, rhs=rr, start=st_, stop=sp),
                    deps=[ttr, g["t_rw"]] + list(prev_racc.get(half, ()))
                    if dc == 0 else ())
            mm = b.op("tensor", lambda t, oo=g["pA"][:, 0:8],
                      ll=g["ones_row"][:], rr=g["rb_row"][:]: t.matmul(
                out=oo, lhsT=ll, rhs=rr, start=False, stop=True),
                deps=[g["t_rb"]])
            tm8 = b.op("vector", lambda v, tj=tj: v.reduce_max(
                out=g["mst"]["m8"][:, tj:tj + 1], in_=g["pA"][:, 0:8],
                axis=AX.X), deps=[mm, prev_small])
            tmask = b.op("vector", lambda v, tj=tj: v.tensor_scalar(
                out=g["mask8"][:], in0=g["pA"][:, 0:8],
                scalar1=g["mst"]["m8"][:, tj:tj + 1], scalar2=None,
                op0=ALU.is_ge), deps=[tm8])
            tbm8 = b.op("vector", lambda v, tj=tj: v.tensor_scalar(
                out=g["mst"]["bm8"][:, tj:tj + 1],
                in0=g["mst"]["m8"][:, tj:tj + 1], scalar1=-1.0, scalar2=None,
                op0=ALU.mult), deps=[tm8])
            te8 = b.op("scalar", lambda s, tj=tj: s.activation(
                out=g["e8"][:], in_=g["pA"][:, 0:8], func=AF.Exp,
                bias=g["mst"]["bm8"][:, tj:tj + 1]), deps=[tbm8, tmask])
            te8s = b.op("scalar", lambda s: s.activation(
                out=g["e8"][:], in_=g["e8"][:], func=AF.Identity),
                deps=[te8])
            prev_racc[half] = (tmask, te8s)
            tl8 = b.op("vector", lambda v, tj=tj: v.reduce_sum(
                out=g["mst"]["l8"][:, tj:tj + 1], in_=g["e8"][:], axis=AX.X),
                deps=[te8s])
            tgate = b.op("vector", lambda v, tj=tj: v.reciprocal(
                out=g["mst"]["gate"][:, tj:tj + 1],
                in_=g["mst"]["l8"][:, tj:tj + 1]), deps=[tl8])
            th = b.op("vector", lambda v: v.memset(g["mhave"][:, 0:1], 0.0),
                      deps=[tgate])
            for e in range(8):
                tinv = b.op("vector", lambda v: v.tensor_scalar(
                    out=g["mhave"][:, 1:2], in0=g["mhave"][:, 0:1],
                    scalar1=-1.0, scalar2=1.0, op0=ALU.mult, op1=ALU.add),
                    deps=[th])
                tkeep = b.op("vector", lambda v, e=e, tj=tj: v.tensor_mul(
                    out=g["g8"][:, tj * 8 + e:tj * 8 + e + 1],
                    in0=g["mask8"][:, e:e + 1], in1=g["mhave"][:, 1:2]),
                    deps=[tinv, tmask])
                th = b.op("vector", lambda v, e=e, tj=tj: v.tensor_add(
                    out=g["mhave"][:, 0:1], in0=g["mhave"][:, 0:1],
                    in1=g["g8"][:, tj * 8 + e:tj * 8 + e + 1]), deps=[tkeep])
            tgs = b.op("vector", lambda v, tj=tj: v.tensor_scalar(
                out=g["g8"][:, tj * 8:(tj + 1) * 8],
                in0=g["g8"][:, tj * 8:(tj + 1) * 8],
                scalar1=g["mst"]["gate"][:, tj:tj + 1], scalar2=0.5,
                op0=ALU.mult, op1=ALU.mult), deps=[th, tgate])
            tgb = b.op("vector", lambda v, tj=tj: v.tensor_copy(
                out=g["g8bf"][:], in_=g["g8"][:, tj * 8:(tj + 1) * 8]),
                deps=[tgs, prev_tr])
            pe_g = b.op("tensor", lambda t, oo=g["pT"][0:8, 0:128],
                        ii=g["g8bf"][:], ee=g["eye_sb"][:]: t.transpose(
                out=oo, in_=ii, identity=ee), deps=[tgb, prev_tr])
            tgT = b.op("vector", lambda v, tj=tj: v.tensor_copy(
                out=g["gT_all"][0:8, tj * 128:(tj + 1) * 128],
                in_=g["pT"][0:8, 0:128]), deps=[pe_g])
            prev_tr = tgT
            gdone[i] = (tgT, tgs)
            prev_small = tgs
        b.serial = ser_b
        b._last = None
        acc_init = {}
        for tj in range(8):
            i = half * 8 + tj
            mm = None
            for (o, w) in ((0, 512), (512, 512)):
                mm = b.op("tensor", lambda t, oo=g["pE"][:, o:o + w],
                          ll=g["gT_all"][0:8, tj * 128:(tj + 1) * 128],
                          rr=g["eb_bf"][0:8, o:o + w]: t.matmul(
                    out=oo, lhsT=ll, rhs=rr, start=True, stop=True),
                    deps=[gdone[i][0], g["t_eb"], prev_ep_evac]
                    if o == 0 else ())
            tev = b.op("vector", lambda v, tj=tj: v.tensor_copy(
                out=g["macc"][:, tj * D:(tj + 1) * D], in_=g["pE"][:]),
                deps=[mm, acc_init.get(tj)])
            acc_init[tj] = tev
            prev_ep_evac = tev
        prev_use_ew = g.get("t_ew_last")
        for e in range(8):
            tcasts = []
            prev_cast = g.get("moe_cast_prev")
            for dc in range(8):
                dW = b.dma("sync", out=g["ew_stage"][:],
                           in_=g["ew_d"][e * 1024 + dc * 128:
                                         e * 1024 + (dc + 1) * 128, :],
                           deps=[prev_cast, prev_use_ew])
                prev_cast = b.op("vector", lambda v, dc=dc: v.tensor_copy(
                    out=g["ew_bf"][:, dc * D:(dc + 1) * D],
                    in_=g["ew_stage"][:]), deps=[dW])
                tcasts.append(prev_cast)
            g["moe_cast_prev"] = prev_cast
            for tj in range(8):
                i = half * 8 + tj
                mm = None
                for (o, w) in ((0, 512), (512, 512)):
                    for dc in range(8):
                        mm = b.op("tensor", lambda t, oo=g["pE"][:, o:o + w],
                                  ll=g["xnT_h"][:, tj * D + dc * 128:
                                                tj * D + (dc + 1) * 128],
                                  rr=g["ew_bf"][:, dc * D + o:dc * D + o + w],
                                  st_=(dc == 0), sp=(dc == 7): t.matmul(
                            out=oo, lhsT=ll, rhs=rr, start=st_, stop=sp),
                            deps=[tcasts[-1], xnT_toks[i], prev_ep_evac]
                            if (dc == 0 and o == 0) else ())
                prev_use_ew = mm
                tev = b.op("vector", lambda v, tj=tj, e=e, i=i:
                           v.scalar_tensor_tensor(
                    out=g["macc"][:, tj * D:(tj + 1) * D], in0=g["pE"][:],
                    scalar=g["g8"][:, tj * 8 + e:tj * 8 + e + 1],
                    in1=g["macc"][:, tj * D:(tj + 1) * D],
                    op0=ALU.mult, op1=ALU.add),
                    deps=[mm, gdone[i][1], acc_init[tj]])
                acc_init[tj] = tev
                prev_ep_evac = tev
                if e == 7:
                    Xi = g["X"][:, i * D:(i + 1) * D]
                    tres = b.op("vector", lambda v, Xi=Xi, tj=tj: v.tensor_add(
                        out=Xi, in0=Xi,
                        in1=g["macc"][:, tj * D:(tj + 1) * D]),
                        deps=[tev, s2_toks[i]])
                    res_toks.append(tres)
        g["t_ew_last"] = prev_use_ew
    g["moe_ptr"] = prev_tr
    b.serial = False
    return res_toks


def build_moe_program(debug=False):
    nc = bass.Bass("TRN2", target_bir_lowering=False)
    b = Bld(nc)
    g = {"nc": nc}
    g["X"] = nc.alloc_sbuf_tensor("Xsb", [128, NT * D], F32)
    g["lnst"] = {k: nc.alloc_sbuf_tensor("ln_" + k, [128, NT], F32)
                 for k in ("s1", "s2", "m", "v", "rstd", "mr")}
    g["junk"] = nc.alloc_sbuf_tensor("junk", [128, D], BF16)
    g["cbuf"] = nc.alloc_sbuf_tensor("cbuf", [128, 16], F32)
    alloc_moe_buffers(nc, g)
    g["eye_sb"] = nc.alloc_sbuf_tensor("eye_sb", [128, 128], BF16)
    g["ones_row"] = nc.alloc_sbuf_tensor("ones_row", [1, 128], BF16)
    x_in = nc.dram_tensor("x", [T, D], F32, kind="ExternalInput").ap()
    y_out = nc.dram_tensor("y", [T, D], F32, kind="ExternalOutput").ap()
    g["ew_d"] = nc.dram_tensor("expert_w", [8 * D, D], F32,
                               kind="ExternalInput").ap()
    g["rw_d"] = nc.dram_tensor("router_w", [D, 8], F32,
                               kind="ExternalInput").ap()
    eb_d = nc.dram_tensor("expert_b", [8, D], F32, kind="ExternalInput").ap()
    rb_d = nc.dram_tensor("router_b", [1, 8], F32, kind="ExternalInput").ap()
    eye_d = nc.dram_tensor("eye", [128, 128], BF16, kind="ExternalInput").ap()
    onr = nc.dram_tensor("ones_row_d", [1, 128], BF16,
                         kind="ExternalInput").ap()
    with (
        nc.psum_tensor("pA", [128, 512], F32) as pA,
        nc.psum_tensor("pE", [128, 1024], F32) as pE,
        nc.psum_tensor("pT", [128, 1024], BF16) as pT,
    ):
        g["pA"], g["pE"], g["pT"] = pA, pE, pT
        if debug:
            g["dbg_moe"] = {nm: nc.dram_tensor(nm, sh, dt, kind="ExternalOutput").ap()
                            for nm, sh, dt in (("d_g8", [128, 64], F32),
                                               ("d_rw", [128, 64], BF16),
                                               ("d_logit", [128, 8], F32),
                                               ("d_macc", [128, 8 * D], F32),
                                               ("d_xnT", [128, 8 * D], BF16),
                                               ("d_gT", [8, 8 * 128], BF16))}
        teye = b.dma("gpsimd", out=g["eye_sb"][:], in_=eye_d)
        tonr = b.dma("gpsimd", out=g["ones_row"][:], in_=onr)
        tx_toks = []
        for hh in range(2):
            tx_toks.append(b.dma("gpsimd",
                           out=g["X"][:, hh * 8 * D:(hh + 1) * 8 * D].rearrange(
                               "p (i f) -> p i f", f=D),
                           in_=x_in.rearrange("(i p) f -> p i f", p=128)
                           [:, hh * 8:(hh + 1) * 8, :]))
        tx = tx_toks
        drw = b.dma("sync", out=g["macc"][:, 0:64].rearrange(
                        "p (c e) -> p c e", e=8),
                    in_=g["rw_d"].rearrange("(c p) e -> p c e", p=128))
        g["t_rw"] = b.op("vector", lambda v: v.tensor_copy(
            out=g["rw_bf"][:], in_=g["macc"][:, 0:64]), deps=[drw])
        deb = b.dma("sync", out=g["macc"][0:8, 64:64 + D], in_=eb_d)
        g["t_eb"] = b.op("vector", lambda v: v.tensor_copy(
            out=g["eb_bf"][:], in_=g["macc"][0:8, 64:64 + D]), deps=[deb])
        drb = b.dma("sync", out=g["macc"][0:1, 1088:1096], in_=rb_d)
        g["t_rb"] = b.op("vector", lambda v: v.tensor_copy(
            out=g["rb_row"][:], in_=g["macc"][0:1, 1088:1096]), deps=[drb])
        for e in ("vector", "scalar", "tensor"):
            b.wait(e, [teye, tonr, g["t_rw"], g["t_eb"], g["t_rb"]] + tx)
        res = emit_moe(b, g, tx)
        if debug:
            dd = g["dbg_moe"]
            res = list(res)
            for nm, src in (("d_g8", g["g8"][:]), ("d_rw", g["rw_bf"][:]),
                            ("d_macc", g["macc"][:]), ("d_xnT", g["xnT_h"][:]),
                            ("d_gT", g["gT_all"][:])):
                res.append(b.dma("scalar", out=dd[nm], in_=src, deps=res[-1:]))
        tst = []
        for hh in range(2):
            tst.append(b.dma("gpsimd",
                       out=y_out.rearrange("(i p) f -> p i f", p=128)
                       [:, hh * 8:(hh + 1) * 8, :],
                       in_=g["X"][:, hh * 8 * D:(hh + 1) * 8 * D].rearrange(
                           "p (i f) -> p i f", f=D),
                       deps=res))
        b.wait("gpsimd", tst)
        nc_built = b.build()
    return nc_built





def alloc_l1a_buffers(nc, g):
    g["xlnT"] = nc.alloc_sbuf_tensor("xlnT", [128, 8 * T], BF16)
    g["Wstage"] = nc.alloc_sbuf_tensor("Wstage", [128, 1024], F32)
    g["Wblk"] = nc.alloc_sbuf_tensor("Wblk", [128, 2 * 1024], BF16)
    g["xn_bf2"] = nc.alloc_sbuf_tensor("xn_bf2", [128, D], BF16)
    g["res_sb"] = nc.alloc_sbuf_tensor("res_sb", [128, 512], F32)
    g["rn2_sb"] = nc.alloc_sbuf_tensor("rn2_sb", [1, T], F32)
    g["xn2_row"] = nc.alloc_sbuf_tensor("xn2_row", [1, T], F32)
    g["gp_small"] = nc.alloc_sbuf_tensor("gp_small", [128, 4], F32)
    g["bproj_cols"] = nc.alloc_sbuf_tensor("bproj_cols", [128, 8], F32)
    g["res_bf"] = nc.alloc_sbuf_tensor("res_bf", [128, 512], BF16)
    g["rat_bf"] = nc.alloc_sbuf_tensor("rat_bf", [128, NT], BF16)


def emit_l1_xlnT(b, g, tin, keep_v=False):
    st = g["lnst"]
    stat_toks = []
    s2_toks = []
    for i in range(NT):
        Xi = g["X"][:, i * D:(i + 1) * D]
        t1, t2 = emit_ln_stats(b, Xi, st["s1"][:, i:i + 1], st["s2"][:, i:i + 1],
                               g["junk"][:, 0:D], deps=tin if i == 0 else ())
        stat_toks += [t1, t2]
        s2_toks.append(t2)
    tmr, trs = emit_ln_batch(b, g, st, NT, stat_toks)
    prev_tr = None
    xlnT_toks = []
    for i in range(NT):
        Xi = g["X"][:, i * D:(i + 1) * D]
        tap = emit_ln_apply(b, Xi, g["xn_bf2"][:], st["rstd"][:, i:i + 1],
                            st["mr"][:, i:i + 1], [tmr, trs])
        pe_last = None
        for c in range(8):
            pe_last = b.op("tensor", lambda t,
                           oo=g["pT"][:, c * 128:(c + 1) * 128],
                           ii=g["xn_bf2"][:, c * 128:(c + 1) * 128],
                           ee=g["eye_sb"][:]: t.transpose(out=oo, in_=ii,
                                                          identity=ee),
                           deps=[tap, prev_tr] if c == 0 else ())
        tc = b.op("vector", lambda v, i=i: v.tensor_copy(
            out=g["xlnT"][:].rearrange("p (c t) -> p c t", t=T)
            [:, :, i * 128:(i + 1) * 128],
            in_=g["pT"][:].rearrange("p (c f) -> p c f", f=128)), deps=[pe_last])
        prev_tr = tc
        xlnT_toks.append(tc)
    return xlnT_toks, s2_toks, prev_tr, (tmr, trs)


def emit_wcolblock(b, g, w_dram, ncb, deps=(), slot=0):
    dW = b.dma("sync", out=g["Wstage"][:, 0:1024].rearrange(
        "p (c f) -> p c f", f=128),
        in_=w_dram.rearrange("(c p) n -> p c n", p=128)
        [:, :, ncb * 128:(ncb + 1) * 128], deps=deps)
    return b.op("vector", lambda v: v.tensor_copy(
        out=g["Wblk"][:, slot * 1024:(slot + 1) * 1024],
        in_=g["Wstage"][:, 0:1024]), deps=[dW])


def emit_l1a(b, g, tin):
    xlnT_toks, s2_toks, ptr, _ = emit_l1_xlnT(b, g, tin)
    prev_cast = None
    prev_use = None
    rn_last = None
    prev_sq = None
    for ncb in range(8):
        tcast = emit_wcolblock(b, g, g["WprojT_d"], ncb,
                               deps=[prev_cast, prev_use], slot=ncb % 2)
        prev_cast = tcast
        slot = ncb % 2
        for gs in range(4):
            mm = None
            for dc in range(8):
                mm = b.op("tensor", lambda t, oo=g["pS"][:, 0:512],
                          ll=g["Wblk"][:, slot * 1024 + dc * 128:
                                       slot * 1024 + (dc + 1) * 128],
                          rr=g["xlnT"][:].rearrange("p (c t) -> p c t", t=T)
                          [:, dc, gs * 512:(gs + 1) * 512],
                          st_=(dc == 0), sp=(dc == 7): t.matmul(
                    out=oo, lhsT=ll, rhs=rr, start=st_, stop=sp),
                    deps=[tcast, xlnT_toks[-1], prev_sq] if dc == 0 else ())
            prev_use = mm
            tres = b.op("vector", lambda v, ncb=ncb, gs=gs:
                        v.scalar_tensor_tensor(
                out=g["res_sb"][:], in0=g["pS"][:, 0:512],
                scalar=g["bproj_cols"][:, ncb:ncb + 1],
                in1=g["xlnT"][:].rearrange("p (c t) -> p c t", t=T)
                [:, ncb, gs * 512:(gs + 1) * 512],
                op0=ALU.add, op1=ALU.subtract), deps=[mm, g["t_bproj"]])
            tsq = b.op("vector", lambda v: v.tensor_mul(
                out=g["res_bf"][:], in0=g["res_sb"][:], in1=g["res_sb"][:]),
                deps=[tres])
            prev_sq = tsq
            rn_last = b.op("tensor", lambda t, gs=gs,
                           oo=g["pF"][0:1, gs * 512:(gs + 1) * 512],
                           ll=g["ones_col"][:], rr=g["res_bf"][:],
                           st_=(ncb == 0), sp=(ncb == 7): t.matmul(
                out=oo, lhsT=ll, rhs=rr, start=st_, stop=sp),
                deps=[tsq])
    tev = b.op("vector", lambda v: v.tensor_copy(
        out=g["rn2_sb"][:], in_=g["pF"][0:1, 0:T]), deps=[rn_last])
    xn_last = None
    prev_sq2 = tev
    for dc in range(8):
        for gs in range(4):
            tsq = b.op("vector", lambda v, dc=dc, gs=gs: v.tensor_mul(
                out=g["res_bf"][:],
                in0=g["xlnT"][:].rearrange("p (c t) -> p c t", t=T)
                [:, dc, gs * 512:(gs + 1) * 512],
                in1=g["xlnT"][:].rearrange("p (c t) -> p c t", t=T)
                [:, dc, gs * 512:(gs + 1) * 512]), deps=[prev_sq2])
            prev_sq2 = tsq
            xn_last = b.op("tensor", lambda t, gs=gs,
                           oo=g["pF"][0:1, gs * 512:(gs + 1) * 512],
                           ll=g["ones_col"][:], rr=g["res_bf"][:],
                           st_=(dc == 0), sp=(dc == 7): t.matmul(
                out=oo, lhsT=ll, rhs=rr, start=st_, stop=sp),
                deps=[tsq, tev] if dc == 0 and gs == 0 else [tsq])
    tev2 = b.op("vector", lambda v: v.tensor_copy(
        out=g["xn2_row"][:], in_=g["pF"][0:1, 0:T]), deps=[xn_last])
    tsr = b.op("scalar", lambda s: s.activation(out=g["rn2_sb"][:],
                                                in_=g["rn2_sb"][:],
                                                func=AF.Ln), deps=[tev])
    tsr2 = b.op("scalar", lambda s: s.activation(out=g["rn2_sb"][:],
                                                 in_=g["rn2_sb"][:],
                                                 func=AF.Exp, scale=0.5),
                deps=[tsr])
    tsx = b.op("scalar", lambda s: s.activation(out=g["xn2_row"][:],
                                                in_=g["xn2_row"][:],
                                                func=AF.Ln), deps=[tev2])
    tsx2 = b.op("scalar", lambda s: s.activation(out=g["xn2_row"][:],
                                                 in_=g["xn2_row"][:],
                                                 func=AF.Exp, scale=0.5),
                deps=[tsx])
    tsx3 = b.op("vector", lambda v: v.tensor_scalar(
        out=g["xn2_row"][:], in0=g["xn2_row"][:], scalar1=1e-8, scalar2=None,
        op0=ALU.add), deps=[tsx2])
    trc = b.op("vector", lambda v: v.reciprocal(out=g["xn2_row"][:],
                                                in_=g["xn2_row"][:]),
               deps=[tsx3])
    trt = b.op("vector", lambda v: v.tensor_mul(
        out=g["rn2_sb"][:], in0=g["rn2_sb"][:], in1=g["xn2_row"][:]),
        deps=[trc, tsr2])
    tps = b.op("vector", lambda v: v.reduce_sum(
        out=g["gp_small"][0:1, 0:1], in_=g["rn2_sb"][:], axis=AX.X),
        deps=[trt])
    return tps


def build_l1a_program():
    nc = bass.Bass("TRN2", target_bir_lowering=False)
    b = Bld(nc)
    g = {"nc": nc}
    g["X"] = nc.alloc_sbuf_tensor("Xsb", [128, NT * D], F32)
    g["lnst"] = {k: nc.alloc_sbuf_tensor("ln_" + k, [128, NT], F32)
                 for k in ("s1", "s2", "m", "v", "rstd", "mr")}
    g["junk"] = nc.alloc_sbuf_tensor("junk", [128, D], BF16)
    g["cbuf"] = nc.alloc_sbuf_tensor("cbuf", [128, 16], F32)
    alloc_l1a_buffers(nc, g)
    g["eye_sb"] = nc.alloc_sbuf_tensor("eye_sb", [128, 128], BF16)
    g["ones_col"] = nc.alloc_sbuf_tensor("ones_col", [128, 1], BF16)
    x_in = nc.dram_tensor("x", [T, D], F32, kind="ExternalInput").ap()
    gp_out = nc.dram_tensor("gp", [1, 1], F32, kind="ExternalOutput").ap()
    g["WprojT_d"] = nc.dram_tensor("WprojT", [D, D], F32,
                                   kind="ExternalInput").ap()
    bpc = nc.dram_tensor("bproj_cols_d", [128, 8], F32,
                         kind="ExternalInput").ap()
    eye_d = nc.dram_tensor("eye", [128, 128], BF16, kind="ExternalInput").ap()
    onc_d = nc.dram_tensor("ones_col_d", [128, 1], BF16,
                           kind="ExternalInput").ap()
    with (
        nc.psum_tensor("pS", [128, 512], F32) as pS,
        nc.psum_tensor("pT", [128, 1024], BF16) as pT,
        nc.psum_tensor("pF", [1, 2048], F32) as pF,
    ):
        g["pS"], g["pT"], g["pF"] = pS, pT, pF
        teye = b.dma("gpsimd", out=g["eye_sb"][:], in_=eye_d)
        tonc = b.dma("gpsimd", out=g["ones_col"][:], in_=onc_d)
        tbp = b.dma("gpsimd", out=g["bproj_cols"][:], in_=bpc)
        g["t_bproj"] = tbp
        tx_toks = []
        for hh in range(2):
            tx_toks.append(b.dma("gpsimd",
                           out=g["X"][:, hh * 8 * D:(hh + 1) * 8 * D].rearrange(
                               "p (i f) -> p i f", f=D),
                           in_=x_in.rearrange("(i p) f -> p i f", p=128)
                           [:, hh * 8:(hh + 1) * 8, :]))
        for e in ("vector", "scalar", "tensor"):
            b.wait(e, [teye, tonc, tbp] + tx_toks)
        tps = emit_l1a(b, g, tx_toks)
        tst = b.dma("gpsimd", out=gp_out, in_=g["gp_small"][0:1, 0:1],
                    deps=[tps])
        b.wait("gpsimd", [tst])
        nc_built = b.build()
    return nc_built





L1_NEG = -1.0e9
GAUGE_TH = 0.3
ENT_TH = 0.15


def alloc_l1b_buffers(nc, g):
    g["xstage"] = nc.alloc_sbuf_tensor("xstage", [128, 2 * 1024], F32)
    g["xlnT"] = nc.alloc_sbuf_tensor("xlnT", [128, 8 * T], BF16)
    g["dT"] = nc.alloc_sbuf_tensor("dT", [128, 8 * T], BF16)
    g["knT"] = nc.alloc_sbuf_tensor("knT", [128, 8 * T], BF16)
    g["V"] = nc.alloc_sbuf_tensor("Vsb", [128, NT * D], BF16)
    g["Wfull"] = nc.alloc_sbuf_tensor("Wfull", [128, 8 * 1024], BF16)
    g["Wstage"] = nc.alloc_sbuf_tensor("Wstage", [128, 1024], F32)
    g["scr4k"] = nc.alloc_sbuf_tensor("scr4k", [128, 1024], F32)
    g["scr2k"] = nc.alloc_sbuf_tensor("scr2k", [128, 512], F32)
    g["qn_bf"] = nc.alloc_sbuf_tensor("qn_bf", [128, 1024], BF16)
    g["qbf2"] = nc.alloc_sbuf_tensor("qbf2", [128, 1024], BF16)
    g["u384"] = nc.alloc_sbuf_tensor("u384", [128, 384], F32)
    g["P384"] = nc.alloc_sbuf_tensor("P384", [128, 384], BF16)
    g["PT3"] = nc.alloc_sbuf_tensor("PT3", [128, 384], BF16)
    g["mask_mid"] = nc.alloc_sbuf_tensor("mask_mid", [128, 384], BF16)
    g["mask_first"] = nc.alloc_sbuf_tensor("mask_first", [128, 256], BF16)
    g["lbuf"] = nc.alloc_sbuf_tensor("lbuf", [128, 8], F32)
    g["f8"] = nc.alloc_sbuf_tensor("f8", [128, 8], F32)
    g["n2_8"] = nc.alloc_sbuf_tensor("n2_8", [128, 8], F32)
    g["mx8"] = nc.alloc_sbuf_tensor("mx8", [128, 8], F32)
    g["n_all"] = nc.alloc_sbuf_tensor("n_all", [128, 128], F32)
    g["in_all"] = nc.alloc_sbuf_tensor("in_all", [128, 128], F32)
    g["kn_all"] = nc.alloc_sbuf_tensor("kn_all", [128, 128], F32)
    g["kin_all"] = nc.alloc_sbuf_tensor("kin_all", [128, 128], F32)
    g["hacc"] = nc.alloc_sbuf_tensor("hacc", [128, 64], F32)
    g["hcol"] = nc.alloc_sbuf_tensor("hcol", [128, 4], F32)
    g["hcol_bf"] = nc.alloc_sbuf_tensor("hcol_bf", [128, 1], BF16)
    g["Hrows"] = nc.alloc_sbuf_tensor("Hrows", [1, 64], F32)
    g["prow"] = nc.alloc_sbuf_tensor("prow", [1, 1024], F32)
    g["brow"] = nc.alloc_sbuf_tensor("brow", [1, 1024], F32)
    g["pmax8"] = nc.alloc_sbuf_tensor("pmax8", [1, 8], F32)
    g["w2bc"] = nc.alloc_sbuf_tensor("w2bc", [128, 512], BF16)
    g["cvacc"] = nc.alloc_sbuf_tensor("cvacc", [128, 4], F32)
    g["g128"] = nc.alloc_sbuf_tensor("g128", [128, 2], F32)
    g["nb2c"] = nc.alloc_sbuf_tensor("nb2c", [128, 1], F32)
    g["mrow"] = nc.alloc_sbuf_tensor("mrow", [1, 16], F32)
    g["mrow_bf"] = nc.alloc_sbuf_tensor("mrow_bf", [1, 16], BF16)
    g["gp_row"] = nc.alloc_sbuf_tensor("gp_row", [1, 8], F32)
    g["bias_row"] = nc.alloc_sbuf_tensor("bias_row", [1, 1024], BF16)
    g["bproj_cols"] = nc.alloc_sbuf_tensor("bproj_cols", [128, 8], F32)
    g["ones_row"] = nc.alloc_sbuf_tensor("ones_row", [1, 128], BF16)
    g["ones_col"] = nc.alloc_sbuf_tensor("ones_col", [128, 1], BF16)


def emit_l1_xlnT_stream(b, g, x_dram, tin):
    st = g["lnst"]
    taps = {}
    prev_tr = None
    prev_pe = None
    xlnT_toks = []
    for i in range(NT):
        slot = i % 2
        stage = g["xstage"][:, slot * 1024:(slot + 1) * 1024]
        tX = b.dma("gpsimd", out=stage, in_=x_dram[i * 128:(i + 1) * 128, :],
                   deps=list(tin if i == 0 else []) + [taps.get(i - 2)])
        t1, t2 = emit_ln_stats(b, stage, st["s1"][:, i:i + 1],
                               st["s2"][:, i:i + 1], g["junk"][:, 0:D],
                               deps=[tX])
        tmr, trs = emit_ln_batch(b, g, st, 1, [t1, t2], off=i)
        tap = emit_ln_apply(b, stage, g["qn_bf"][:], st["rstd"][:, i:i + 1],
                            st["mr"][:, i:i + 1], [tmr, trs, prev_pe])
        taps[i] = tap
        pe_last = None
        for c in range(8):
            pe_last = b.op("tensor", lambda t,
                           oo=g["pT"][:, c * 128:(c + 1) * 128],
                           ii=g["qn_bf"][:, c * 128:(c + 1) * 128],
                           ee=g["eye_sb"][:]: t.transpose(out=oo, in_=ii,
                                                          identity=ee),
                           deps=[tap, prev_tr] if c == 0 else ())
        prev_pe = pe_last
        tc = b.op("vector", lambda v, i=i: v.tensor_copy(
            out=g["xlnT"][:].rearrange("p (c t) -> p c t", t=T)
            [:, :, i * 128:(i + 1) * 128],
            in_=g["pT"][:].rearrange("p (c f) -> p c f", f=128)),
            deps=[pe_last])
        prev_tr = tc
        xlnT_toks.append(tc)
    return xlnT_toks


def emit_wfull(b, g, w_dram, deps=(), ncols=1024, nchunks=8):
    toks = []
    prev = None
    for dc in range(nchunks):
        dW = b.dma("sync", out=g["Wstage"][:, 0:ncols],
                   in_=w_dram[dc * 128:(dc + 1) * 128, :],
                   deps=list(deps) + [prev, g.get("_wfull_prev_use")])
        prev = b.op("vector", lambda v, dc=dc, ncols=ncols: v.tensor_copy(
            out=g["Wfull"][:, dc * ncols:(dc + 1) * ncols],
            in_=g["Wstage"][:, 0:ncols]), deps=[dW])
        toks.append(prev)
    return toks[-1]


def tile_blocks(i):
    if i == 0:
        return [0, 1], "first"
    if i == NT - 1:
        return [NT - 2, NT - 1], "last"
    return [i - 1, i, i + 1], "mid"


def emit_row_entropy(b, g, outcol, deps):
    c10, c10t = get_const(b, g, 1e-10)
    prow = g["prow"][:]
    t1 = b.op("vector", lambda v: v.tensor_reduce(
        out=g["pmax8"][:], in_=prow.rearrange("o (h k) -> o h k", k=128),
        axis=AX.X, op=ALU.max), deps=deps)
    t2 = b.op("vector", lambda v: v.scalar_tensor_tensor(
        out=prow.rearrange("o (h k) -> o h k", k=128),
        in0=g["pmax8"][:].unsqueeze(2).broadcast_to([1, 8, 128]),
        scalar=-1.0, in1=prow.rearrange("o (h k) -> o h k", k=128),
        op0=ALU.mult, op1=ALU.add), deps=[t1])
    t3 = b.op("scalar", lambda s: s.activation(
        out=prow, in_=prow, func=AF.Exp), deps=[t2])
    t4 = b.op("vector", lambda v: v.tensor_reduce(
        out=g["pmax8"][:], in_=prow.rearrange("o (h k) -> o h k", k=128),
        axis=AX.X, op=ALU.add), deps=[t3])
    t5 = b.op("vector", lambda v: v.reciprocal(
        out=g["pmax8"][:], in_=g["pmax8"][:]), deps=[t4])
    t6 = b.op("vector", lambda v: v.tensor_mul(
        out=prow.rearrange("o (h k) -> o h k", k=128),
        in0=prow.rearrange("o (h k) -> o h k", k=128),
        in1=g["pmax8"][:].unsqueeze(2).broadcast_to([1, 8, 128])),
        deps=[t5])
    t7 = b.op("scalar", lambda s: s.activation(
        out=g["brow"][:], in_=prow, func=AF.Ln, bias=c10[0:1, 0:1]),
        deps=[t6, c10t])
    t8 = b.op("vector", lambda v: v.tensor_mul(
        out=g["brow"][:], in0=g["brow"][:], in1=prow), deps=[t7])
    t9 = b.op("vector", lambda v, outcol=outcol: v.tensor_reduce(
        out=outcol, in_=g["brow"][:].rearrange("o (h k) -> o h k", k=128),
        axis=AX.X, op=ALU.add), deps=[t8])
    return t9


def emit_l1b(b, g, tin):
    xlnT_toks = emit_l1_xlnT_stream(b, g, g["x_d"], tin)
    g["_l1b_ptfree"] = xlnT_toks[-1]
    CT = g["xlnT"][:].rearrange("p (c t) -> p c t", t=T)
    tgp = b.op("vector", lambda v: v.reduce_sum(
        out=g["mrow"][0:1, 8:9], in_=g["gp_row"][0:1, 0:8], axis=AX.X),
        deps=[g["t_gp"]])
    tgm = b.op("vector", lambda v: v.tensor_scalar(
        out=g["mrow"][0:1, 9:10], in0=g["mrow"][0:1, 8:9],
        scalar1=1.0 / 16384.0, scalar2=GAUGE_TH, op0=ALU.mult, op1=ALU.is_gt),
        deps=[tgp])
    tgmb = b.op("vector", lambda v: v.tensor_copy(
        out=g["mrow_bf"][0:1, 0:1], in_=g["mrow"][0:1, 9:10]), deps=[tgm])
    tg128 = b.op("tensor", lambda t: t.matmul(
        out=g["pS"][:, 0:1], lhsT=g["ones_row"][:],
        rhs=g["mrow_bf"][0:1, 0:1], start=True, stop=True),
        deps=[tgmb, g["t_ones"]])
    tmb = b.op("vector", lambda v: v.tensor_copy(
        out=g["g128"][:, 0:1], in_=g["pS"][:, 0:1]), deps=[tg128])
    twp = emit_wfull(b, g, g["WprojT_d"])
    prev_ev = None
    dT_toks = []
    for ncb in range(8):
        for gs in range(4):
            mm = None
            for dc in range(8):
                mm = b.op("tensor", lambda t, oo=g["pS"][:, 0:512],
                          ll=g["Wfull"][:, dc * 1024 + ncb * 128:
                                        dc * 1024 + (ncb + 1) * 128],
                          rr=CT[:, dc, gs * 512:(gs + 1) * 512],
                          st_=(dc == 0), sp=(dc == 7): t.matmul(
                    out=oo, lhsT=ll, rhs=rr, start=st_, stop=sp),
                    deps=[twp, xlnT_toks[-1], prev_ev, tmb] if dc == 0 else ())
            prev_ev = b.op("vector", lambda v, ncb=ncb, gs=gs:
                           v.scalar_tensor_tensor(
                out=g["dT"][:].rearrange("p (c t) -> p c t", t=T)
                [:, ncb, gs * 512:(gs + 1) * 512],
                in0=g["pS"][:, 0:512],
                scalar=g["bproj_cols"][:, ncb:ncb + 1],
                in1=CT[:, ncb, gs * 512:(gs + 1) * 512],
                op0=ALU.add, op1=ALU.subtract), deps=[mm, g["t_bproj"]])
            dT_toks.append(prev_ev)
    g["_wfull_prev_use"] = prev_ev
    twv = emit_wfull(b, g, g["WvT_d"])
    tbv = b.dma("sync", out=g["bias_row"][:], in_=g["bv_d"], deps=[])
    prev_vev = None
    for i in range(NT):
        for (o, w) in ((0, 512), (512, 512)):
            mm = None
            for dc in range(8):
                mm = b.op("tensor", lambda t, oo=g["pS"][:, 0:512],
                          ll=CT[:, dc, i * 128:(i + 1) * 128],
                          rr=g["Wfull"][:, dc * 1024 + o:dc * 1024 + o + w],
                          st_=(dc == 0), sp=False: t.matmul(
                    out=oo, lhsT=ll, rhs=rr, start=st_, stop=sp),
                    deps=[twv, xlnT_toks[i], prev_vev] if dc == 0 else ())
            for bo in (0, 256):
                mm = b.op("tensor", lambda t, bo=bo, o=o,
                          oo=g["pS"][:, bo:bo + 256],
                          ll=g["ones_row"][:],
                          rr=g["bias_row"][0:1, o + bo:o + bo + 256]:
                          t.matmul(out=oo, lhsT=ll, rhs=rr, start=False,
                                   stop=True),
                          deps=[tbv, g["t_ones"]] if bo == 0 else ())
            prev_vev = b.op("vector", lambda v, i=i, o=o: v.tensor_copy(
                out=g["V"][:, i * D + o:i * D + o + 512],
                in_=g["pS"][:, 0:512]), deps=[mm])
    g["_wfull_prev_use"] = prev_vev
    xqk_toks = []
    prev = None
    for dc in range(8):
        prev = b.op("vector", lambda v, dc=dc: v.scalar_tensor_tensor(
            out=CT[:, dc, :], in0=g["dT"][:].rearrange(
                "p (c t) -> p c t", t=T)[:, dc, :],
            scalar=g["g128"][:, 0:1], in1=CT[:, dc, :],
            op0=ALU.mult, op1=ALU.add),
            deps=[tmb, dT_toks[-1], prev, prev_vev])
        xqk_toks.append(prev)

    def qk_pipeline(w_dram, tbias, dst, dst_n, dst_in, csguard):
        tw = emit_wfull(b, g, w_dram)
        out_toks = []
        prev_q = None
        prev_tin = None
        cs_last = None
        for i in range(NT):
            for (o, w) in ((0, 512), (512, 512)):
                mm = None
                for dc in range(8):
                    mm = b.op("tensor", lambda t, oo=g["pS"][:, 0:512],
                              ll=CT[:, dc, i * 128:(i + 1) * 128],
                              rr=g["Wfull"][:, dc * 1024 + o:dc * 1024 + o + w],
                              st_=(dc == 0), sp=False: t.matmul(
                        out=oo, lhsT=ll, rhs=rr, start=st_, stop=sp),
                        deps=[tw, xqk_toks[-1], prev_q] if dc == 0 else ())
                for bo in (0, 256):
                    mm = b.op("tensor", lambda t, bo=bo, o=o,
                              oo=g["pS"][:, bo:bo + 256],
                              ll=g["ones_row"][:],
                              rr=g["bias_row"][0:1, o + bo:o + bo + 256]:
                              t.matmul(out=oo, lhsT=ll, rhs=rr, start=False,
                                       stop=True),
                              deps=[tbias, g["t_ones"]] if bo == 0 else ())
                prev_q = b.op("vector", lambda v, o=o: v.tensor_copy(
                    out=g["scr4k"][:, o:o + 512], in_=g["pS"][:, 0:512]),
                    deps=[mm])
            tqb = b.op("vector", lambda v: v.tensor_copy(
                out=g["qbf2"][:], in_=g["scr4k"][:]), deps=[prev_q])
            for (o, w) in ((0, 512), (512, 512)):
                cs_last = b.op("tensor", lambda t, o=o, w=w, i=i,
                               oo=g["pFq"][0:1, o:o + w],
                               ll=g["ones_col"][:],
                               rr=g["qbf2"][:, o:o + w],
                               st_=(i == 0), sp=(i == NT - 1): t.matmul(
                    out=oo, lhsT=ll, rhs=rr, start=st_, stop=sp),
                    deps=[tqb, g["t_onec"]] +
                    ([csguard] if (i == 0 and o == 0) else []))
            tn2 = None
            for h in range(8):
                tn2 = b.op("vector", lambda v, h=h: v.scalar_tensor_tensor(
                    out=g["junk"][:, 0:128],
                    in0=g["scr4k"][:, h * 128:(h + 1) * 128], scalar=1.0,
                    in1=g["scr4k"][:, h * 128:(h + 1) * 128],
                    op0=ALU.mult, op1=ALU.mult,
                    accum_out=g["n2_8"][:, h:h + 1]),
                    deps=[prev_q, prev_tin] if h == 0 else [prev_q])
            tcl = b.op("vector", lambda v: v.tensor_scalar(
                out=g["n2_8"][:], in0=g["n2_8"][:], scalar1=1e-24,
                scalar2=None, op0=ALU.max), deps=[tn2])
            tln = b.op("scalar", lambda s: s.activation(
                out=g["n2_8"][:], in_=g["n2_8"][:], func=AF.Ln), deps=[tcl])
            tn = b.op("scalar", lambda s, i=i, dst_n=dst_n: s.activation(
                out=dst_n[:, i * 8:(i + 1) * 8], in_=g["n2_8"][:],
                func=AF.Exp, scale=0.5), deps=[tln])
            tin_ = b.op("scalar", lambda s, i=i, dst_in=dst_in: s.activation(
                out=dst_in[:, i * 8:(i + 1) * 8], in_=g["n2_8"][:],
                func=AF.Exp, scale=-0.5), deps=[tn])
            prev_tin = tin_
            tqn = b.op("vector", lambda v, i=i, dst_in=dst_in: v.tensor_mul(
                out=g["qn_bf"][:].rearrange("p (h k) -> p h k", k=128),
                in0=g["scr4k"][:].rearrange("p (h k) -> p h k", k=128),
                in1=dst_in[:, i * 8:(i + 1) * 8].unsqueeze(2).broadcast_to(
                    [128, 8, 128])), deps=[tin_, tqb])
            pe_last = None
            for c in range(8):
                pe_last = b.op("tensor", lambda t, c=c,
                               oo=g["pT"][:, c * 128:(c + 1) * 128],
                               ii=g["qn_bf"][:, c * 128:(c + 1) * 128],
                               ee=g["eye_sb"][:]: t.transpose(
                    out=oo, in_=ii, identity=ee),
                    deps=[tqn, g["_l1b_ptfree"]] if c == 0 else ())
            tc = b.op("vector", lambda v, i=i, dst=dst: v.tensor_copy(
                out=dst.rearrange("p (c t) -> p c t", t=T)
                [:, :, i * 128:(i + 1) * 128],
                in_=g["pT"][:].rearrange("p (c f) -> p c f", f=128)),
                deps=[pe_last])
            g["_l1b_ptfree"] = tc
            out_toks.append(tc)
        g["_wfull_prev_use"] = out_toks[-1]
        return out_toks, cs_last

    tbq = b.dma("sync", out=g["bias_row"][:], in_=g["bq_d"], deps=[prev_vev])
    qnT_toks, q_cs = qk_pipeline(g["WqT_d"], tbq, g["dT"][:],
                                 g["n_all"], g["in_all"], None)
    tpq = b.op("vector", lambda v: v.tensor_copy(
        out=g["prow"][:], in_=g["pFq"][0:1, 0:1024]), deps=[q_cs])
    tq9 = emit_row_entropy(b, g, g["Hrows"][0:1, 16:24], [tpq])
    tbk = b.dma("sync", out=g["bias_row"][:], in_=g["bk_d"],
                deps=[qnT_toks[-1]])
    knT_toks, k_cs = qk_pipeline(g["WkT_d"], tbk, g["knT"][:],
                                 g["kn_all"], g["kin_all"], tpq)
    tpk = b.op("vector", lambda v: v.tensor_copy(
        out=g["prow"][:], in_=g["pFq"][0:1, 0:1024]), deps=[k_cs, tq9])
    tk9 = emit_row_entropy(b, g, g["Hrows"][0:1, 24:32], [tpk])
    g["_l1b_hqk"] = (tq9, tk9)
    return xlnT_toks, qnT_toks, knT_toks


def emit_l1b_attn(b, g, qnT_toks, knT_toks):
    qT = g["dT"][:].rearrange("p (c t) -> p c t", t=T)
    kT = g["knT"][:].rearrange("p (c t) -> p c t", t=T)
    o_buf = g["xlnT"][:]
    o_toks = []
    prev_u = None
    prev_P = None
    prev_PT = None
    prev_oev = None
    for i in range(NT):
        blocks, kind = tile_blocks(i)
        nb = len(blocks)
        W = nb * 128
        j0 = blocks[0] * 128
        mask = g["mask_first"][:, 0:W] if kind == "first" \
            else g["mask_mid"][:, 0:W]
        tl_last = None
        for h in range(8):
            mm = b.op("tensor", lambda t, h=h, i=i, j0=j0, W=W,
                      oo=g["pS"][:, 0:W],
                      ll=qT[:, h, i * 128:(i + 1) * 128],
                      rr=kT[:, h, j0:j0 + W]: t.matmul(
                out=oo, lhsT=ll, rhs=rr, start=True, stop=True),
                deps=[prev_u] + ([qnT_toks[-1], knT_toks[-1]]
                                 if (i == 0 and h == 0) else []))
            tu = b.op("vector", lambda v, W=W: v.tensor_scalar(
                out=g["u384"][:, 0:W], in0=g["pS"][:, 0:W], scalar1=-2.0,
                scalar2=2.0, op0=ALU.mult, op1=ALU.add), deps=[mm, prev_P])
            prev_u = tu
            tu2 = b.op("vector", lambda v, W=W: v.tensor_scalar(
                out=g["u384"][:, 0:W], in0=g["u384"][:, 0:W], scalar1=1e-30,
                scalar2=None, op0=ALU.max), deps=[tu])
            td1 = b.op("scalar", lambda s, W=W: s.activation(
                out=g["u384"][:, 0:W], in_=g["u384"][:, 0:W], func=AF.Ln),
                deps=[tu2])
            td2 = b.op("scalar", lambda s, W=W: s.activation(
                out=g["u384"][:, 0:W], in_=g["u384"][:, 0:W], func=AF.Exp,
                scale=0.5), deps=[td1])
            tz = b.op("vector", lambda v, W=W, mask=mask: v.tensor_sub(
                out=g["u384"][:, 0:W], in0=mask, in1=g["u384"][:, 0:W]),
                deps=[td2, g["t_masks"]])
            tP = b.op("scalar", lambda s, W=W: s.activation(
                out=g["P384"][:, 0:W], in_=g["u384"][:, 0:W], func=AF.Exp),
                deps=[tz])
            prev_P = tP
            tl_last = b.op("vector", lambda v, h=h, W=W: v.reduce_sum(
                out=g["lbuf"][:, h:h + 1], in_=g["P384"][:, 0:W], axis=AX.X),
                deps=[tP])
            pe_t = None
            for bi in range(nb):
                pe_t = b.op("tensor", lambda t, bi=bi,
                            oo=g["pT"][:, bi * 128:(bi + 1) * 128],
                            ii=g["P384"][:, bi * 128:(bi + 1) * 128],
                            ee=g["eye_sb"][:]: t.transpose(
                    out=oo, in_=ii, identity=ee),
                    deps=[tP, prev_PT, g["_l1b_ptfree"]] if bi == 0 else ())
            tptc = b.op("vector", lambda v, nb=nb: v.tensor_copy(
                out=g["PT3"][:, 0:nb * 128], in_=g["pT"][:, 0:nb * 128]),
                deps=[pe_t, g.get("_l1b_pvuse")])
            prev_PT = tptc
            mmv = None
            for bi, jb in enumerate(blocks):
                mmv = b.op("tensor", lambda t, bi=bi, jb=jb, h=h, nb=nb,
                           oo=g["pE"][:, h * 128:(h + 1) * 128],
                           ll=g["PT3"][:, bi * 128:(bi + 1) * 128],
                           rr=g["V"][:, jb * D + h * 128:
                                     jb * D + (h + 1) * 128],
                           st_=(bi == 0), sp=(bi == nb - 1): t.matmul(
                    out=oo, lhsT=ll, rhs=rr, start=st_, stop=sp),
                    deps=[tptc, prev_oev] if bi == 0 else ())
            g["_l1b_pvuse"] = mmv
        tr1 = b.op("vector", lambda v: v.reciprocal(out=g["f8"][:],
                                                    in_=g["lbuf"][:]),
                   deps=[tl_last])
        ts_ = b.op("vector", lambda v: v.tensor_mul(
            out=g["n2_8"][:], in0=g["lbuf"][:], in1=g["f8"][:]), deps=[tr1])
        ts2 = b.op("vector", lambda v: v.tensor_scalar(
            out=g["n2_8"][:], in0=g["n2_8"][:], scalar1=1e-10, scalar2=None,
            op0=ALU.add), deps=[ts_])
        ts3 = b.op("vector", lambda v: v.reciprocal(out=g["n2_8"][:],
                                                    in_=g["n2_8"][:]),
                   deps=[ts2])
        tf8 = b.op("vector", lambda v: v.tensor_mul(
            out=g["f8"][:], in0=g["f8"][:], in1=g["n2_8"][:]), deps=[ts3])
        tev = None
        for h in range(8):
            tev = b.op("vector", lambda v, h=h, i=i: v.tensor_scalar(
                out=o_buf[:, i * D + h * 128:i * D + (h + 1) * 128],
                in0=g["pE"][:, h * 128:(h + 1) * 128],
                scalar1=g["f8"][:, h:h + 1], scalar2=None, op0=ALU.mult),
                deps=[tf8, g["_l1b_pvuse"]] if h == 0 else [tf8])
        prev_oev = tev
        o_toks.append(tev)
    g["_l1b_ptfree"] = prev_PT
    return o_toks


def emit_l1b_entropy(b, g):
    c10, c10t = get_const(b, g, 1e-10)

    def softmax_transform(buf, n_cols):
        BT = buf.rearrange("p (c t) -> p c t", t=T)
        out_toks = []
        for i in range(NT):
            pe_last = None
            for c in range(8):
                pe_last = b.op("tensor", lambda t, c=c, i=i, BT=BT,
                               oo=g["pT"][:, c * 128:(c + 1) * 128],
                               ii=BT[:, c, i * 128:(i + 1) * 128],
                               ee=g["eye_sb"][:]: t.transpose(
                    out=oo, in_=ii, identity=ee),
                    deps=[g["_l1b_ptfree"]] if c == 0 else ())
            tc = b.op("vector", lambda v: v.tensor_copy(
                out=g["qn_bf"][:], in_=g["pT"][:, 0:1024]), deps=[pe_last])
            tQ = b.op("vector", lambda v, i=i, n_cols=n_cols: v.tensor_mul(
                out=g["scr4k"][:].rearrange("p (h k) -> p h k", k=128),
                in0=g["qn_bf"][:].rearrange("p (h k) -> p h k", k=128),
                in1=n_cols[:, i * 8:(i + 1) * 8].unsqueeze(2).broadcast_to(
                    [128, 8, 128])), deps=[tc])
            tmx = b.op("vector", lambda v: v.tensor_reduce(
                out=g["mx8"][:], in_=g["scr4k"][:].rearrange(
                    "p (h k) -> p h k", k=128), axis=AX.X, op=ALU.max),
                deps=[tQ])
            tsb = b.op("vector", lambda v: v.scalar_tensor_tensor(
                out=g["scr4k"][:].rearrange("p (h k) -> p h k", k=128),
                in0=g["mx8"][:].unsqueeze(2).broadcast_to([128, 8, 128]),
                scalar=-1.0,
                in1=g["scr4k"][:].rearrange("p (h k) -> p h k", k=128),
                op0=ALU.mult, op1=ALU.add), deps=[tmx])
            te = b.op("scalar", lambda s: s.activation(
                out=g["scr4k"][:], in_=g["scr4k"][:], func=AF.Exp),
                deps=[tsb])
            tsm = b.op("vector", lambda v: v.tensor_reduce(
                out=g["mx8"][:], in_=g["scr4k"][:].rearrange(
                    "p (h k) -> p h k", k=128), axis=AX.X, op=ALU.add),
                deps=[te])
            trs = b.op("vector", lambda v: v.reciprocal(
                out=g["mx8"][:], in_=g["mx8"][:]), deps=[tsm])
            tqs = b.op("vector", lambda v: v.tensor_mul(
                out=g["qn_bf"][:].rearrange("p (h k) -> p h k", k=128),
                in0=g["scr4k"][:].rearrange("p (h k) -> p h k", k=128),
                in1=g["mx8"][:].unsqueeze(2).broadcast_to([128, 8, 128])),
                deps=[trs])
            pe_last = None
            for c in range(8):
                pe_last = b.op("tensor", lambda t, c=c,
                               oo=g["pT"][:, c * 128:(c + 1) * 128],
                               ii=g["qn_bf"][:, c * 128:(c + 1) * 128],
                               ee=g["eye_sb"][:]: t.transpose(
                    out=oo, in_=ii, identity=ee),
                    deps=[tqs, tc] if c == 0 else ())
            tc2 = b.op("vector", lambda v, i=i, BT=BT: v.tensor_copy(
                out=BT[:, :, i * 128:(i + 1) * 128],
                in_=g["pT"][:].rearrange("p (c f) -> p c f", f=128)),
                deps=[pe_last])
            g["_l1b_ptfree"] = tc2
            out_toks.append(tc2)
        return out_toks

    qs_toks = softmax_transform(g["dT"][:], g["n_all"])
    ks_toks = softmax_transform(g["knT"][:], g["kn_all"])
    QT = g["dT"][:].rearrange("p (c t) -> p c t", t=T)
    KT = g["knT"][:].rearrange("p (c t) -> p c t", t=T)
    hj_done = []
    prev_r = [None, None]
    prev_hred = None
    blk = 0
    for h in range(8):
        slot = 0
        prev_ln = None
        for i in range(NT):
            for gs in range(4):
                pbuf = g["pS"][:, 0:512] if blk % 2 == 0 \
                    else g["pE"][:, 0:512]
                mm = b.op("tensor", lambda t, h=h, i=i, gs=gs, pbuf=pbuf,
                          ll=QT[:, h, i * 128:(i + 1) * 128],
                          rr=KT[:, h, gs * 512:(gs + 1) * 512]: t.matmul(
                    out=pbuf, lhsT=ll, rhs=rr, start=True, stop=True),
                    deps=[prev_r[blk % 2]] +
                    ([qs_toks[-1], ks_toks[-1]] if blk < 2 else []))
                tln = b.op("scalar", lambda s, pbuf=pbuf: s.activation(
                    out=g["scr2k"][:], in_=pbuf, func=AF.Ln,
                    bias=c10), deps=[mm, c10t])
                tst_ = b.op("vector", lambda v, slot=slot, pbuf=pbuf:
                            v.scalar_tensor_tensor(
                    out=g["scr2k"][:], in0=g["scr2k"][:], scalar=1.0,
                    in1=pbuf, op0=ALU.mult, op1=ALU.mult,
                    accum_out=g["hacc"][:, slot:slot + 1]), deps=[tln])
                prev_r[blk % 2] = tst_
                prev_ln = tst_
                blk += 1
                slot += 1
        tred = b.op("vector", lambda v: v.reduce_sum(
            out=g["hcol"][:, 0:1], in_=g["hacc"][:], axis=AX.X),
            deps=[prev_ln, prev_hred])
        tbf = b.op("vector", lambda v: v.tensor_copy(
            out=g["hcol_bf"][:], in_=g["hcol"][:, 0:1]), deps=[tred])
        tmm = b.op("tensor", lambda t, h=h: t.matmul(
            out=g["pFq"][0:1, 0:1], lhsT=g["ones_col"][:],
            rhs=g["hcol_bf"][:], start=True, stop=True), deps=[tbf])
        prev_hred = b.op("vector", lambda v, h=h: v.tensor_copy(
            out=g["Hrows"][0:1, h:h + 1], in_=g["pFq"][0:1, 0:1]),
            deps=[tmm])
        hj_done.append(prev_hred)
    tq9, tk9 = g["_l1b_hqk"]
    tS1 = b.op("vector", lambda v: v.tensor_add(
        out=g["Hrows"][0:1, 32:40], in0=g["Hrows"][0:1, 16:24],
        in1=g["Hrows"][0:1, 24:32]), deps=[tq9, tk9] + hj_done)
    tS2 = b.op("vector", lambda v: v.tensor_sub(
        out=g["Hrows"][0:1, 32:40], in0=g["Hrows"][0:1, 32:40],
        in1=g["Hrows"][0:1, 0:8]), deps=[tS1])
    tS3 = b.op("vector", lambda v: v.reduce_sum(
        out=g["Hrows"][0:1, 40:41], in_=g["Hrows"][0:1, 32:40], axis=AX.X),
        deps=[tS2])
    tS4 = b.op("vector", lambda v: v.tensor_scalar(
        out=g["Hrows"][0:1, 41:42], in0=g["Hrows"][0:1, 40:41],
        scalar1=0.125, scalar2=ENT_TH, op0=ALU.mult, op1=ALU.is_gt),
        deps=[tS3])
    tgb = b.op("vector", lambda v: v.tensor_copy(
        out=g["mrow_bf"][0:1, 1:2], in_=g["Hrows"][0:1, 41:42]), deps=[tS4])
    tg128 = b.op("tensor", lambda t: t.matmul(
        out=g["pS"][:, 0:1], lhsT=g["ones_row"][:],
        rhs=g["mrow_bf"][0:1, 1:2], start=True, stop=True), deps=[tgb])
    tgc = b.op("vector", lambda v: v.tensor_copy(
        out=g["g128"][:, 1:2], in_=g["pS"][:, 0:1]), deps=[tg128])
    return tgc, tS4


def emit_l1b_social(b, g, tgc, o_toks, knT_last):
    o_buf = g["xlnT"][:]
    o_bfA = g["scr4k"][:, 0:512].bitcast(BF16)
    h_bfA = g["scr4k"][:, 512:768].bitcast(BF16)
    tw1 = emit_wfull(b, g, g["W1T_d"], ncols=512)
    tw2l = b.dma("sync", out=g["bias_row"][0:1, 0:512], in_=g["w2_d"],
                 deps=[knT_last])
    mmw = None
    for o2 in (0, 256):
        mmw = b.op("tensor", lambda t, o2=o2,
                   oo=g["pS"][:, o2:o2 + 256],
                   rr=g["bias_row"][0:1, o2:o2 + 256]: t.matmul(
            out=oo, lhsT=g["ones_row"][:], rhs=rr, start=True, stop=True),
            deps=[tw2l, tgc] if o2 == 0 else ())
    tw2bc = b.op("vector", lambda v: v.tensor_copy(
        out=g["w2bc"][:], in_=g["pS"][:, 0:512]), deps=[mmw])
    tb2l = b.dma("sync", out=g["mrow"][0:1, 10:11], in_=g["b2_d"])
    tneg = b.op("vector", lambda v: v.tensor_scalar(
        out=g["mrow"][0:1, 11:12], in0=g["mrow"][0:1, 10:11], scalar1=-1.0,
        scalar2=None, op0=ALU.mult), deps=[tb2l])
    tnb = b.op("vector", lambda v: v.tensor_copy(
        out=g["mrow_bf"][0:1, 2:3], in_=g["mrow"][0:1, 11:12]), deps=[tneg])
    tmmb = b.op("tensor", lambda t: t.matmul(
        out=g["pS"][:, 0:1], lhsT=g["ones_row"][:],
        rhs=g["mrow_bf"][0:1, 2:3], start=True, stop=True),
        deps=[tnb, tw2bc])
    tnb2 = b.op("vector", lambda v: v.tensor_copy(
        out=g["nb2c"][:], in_=g["pS"][:, 0:1]), deps=[tmmb])
    tb1 = b.dma("sync", out=g["bias_row"][0:1, 0:512], in_=g["b1_d"],
                deps=[mmw])
    tcv0 = b.op("vector", lambda v: v.memset(g["cvacc"][:, 0:1], 0.0))
    spill_toks = []
    prev_pe = None
    prev_sp = None
    prev_relu = None
    for i in range(NT):
        tgate = b.op("vector", lambda v, i=i: v.tensor_scalar(
            out=o_bfA, in0=o_buf[:, i * D:(i + 1) * D],
            scalar1=g["g128"][:, 1:2], scalar2=None, op0=ALU.mult),
            deps=[tgc, prev_pe, prev_sp] + ([o_toks[-1]] if i == 0 else []))
        tsp_i = b.dma("gpsimd", out=g["o_dram"][i * 128:(i + 1) * 128, :],
                      in_=o_bfA, deps=[tgate])
        prev_sp = tsp_i
        spill_toks.append(tsp_i)
        pe_last = None
        for c in range(8):
            pe_last = b.op("tensor", lambda t, c=c,
                           oo=g["pT"][:, c * 128:(c + 1) * 128],
                           ii=o_bfA[:, c * 128:(c + 1) * 128],
                           ee=g["eye_sb"][:]: t.transpose(
                out=oo, in_=ii, identity=ee),
                deps=[tgate, g["_l1b_ptfree"]] if c == 0 else ())
        prev_pe = pe_last
        toT = b.op("vector", lambda v: v.tensor_copy(
            out=g["qn_bf"][:], in_=g["pT"][:, 0:1024]), deps=[pe_last])
        g["_l1b_ptfree"] = toT
        mm = None
        for dc in range(8):
            mm = b.op("tensor", lambda t, dc=dc,
                      ll=g["qn_bf"][:, dc * 128:(dc + 1) * 128],
                      rr=g["Wfull"][:, dc * 512:(dc + 1) * 512],
                      st_=(dc == 0): t.matmul(
                out=g["pS"][:, 0:512], lhsT=ll, rhs=rr, start=st_,
                stop=False),
                deps=[toT, tw1, tnb2, prev_relu] if dc == 0 else ())
        for bo in (0, 256):
            mm = b.op("tensor", lambda t, bo=bo,
                      oo=g["pS"][:, bo:bo + 256],
                      rr=g["bias_row"][0:1, bo:bo + 256]: t.matmul(
                out=oo, lhsT=g["ones_row"][:], rhs=rr, start=False,
                stop=True), deps=[tb1] if bo == 0 else ())
        trelu = b.op("scalar", lambda s: s.activation(
            out=h_bfA, in_=g["pS"][:, 0:512], func=AF.Relu), deps=[mm])
        prev_relu = trelu
        tz = b.op("vector", lambda v: v.scalar_tensor_tensor(
            out=g["scr2k"][:], in0=h_bfA, scalar=1.0, in1=g["w2bc"][:],
            op0=ALU.mult, op1=ALU.mult, accum_out=g["cvacc"][:, 1:2]),
            deps=[trelu, tw2bc])
        te = b.op("scalar", lambda s: s.activation(
            out=g["hcol"][:, 1:2], in_=g["cvacc"][:, 1:2], func=AF.Exp,
            scale=-1.0, bias=g["nb2c"][:]), deps=[tz, tnb2])
        tp1 = b.op("vector", lambda v: v.tensor_scalar(
            out=g["hcol"][:, 1:2], in0=g["hcol"][:, 1:2], scalar1=1.0,
            scalar2=None, op0=ALU.add), deps=[te])
        trc = b.op("vector", lambda v: v.reciprocal(
            out=g["hcol"][:, 1:2], in_=g["hcol"][:, 1:2]), deps=[tp1])
        b.op("vector", lambda v: v.tensor_add(
            out=g["cvacc"][:, 0:1], in0=g["cvacc"][:, 0:1],
            in1=g["hcol"][:, 1:2]), deps=[trc, tcv0])
    tcb = b.op("vector", lambda v: v.tensor_copy(
        out=g["hcol_bf"][:], in_=g["cvacc"][:, 0:1]), deps=[])
    tms = b.op("tensor", lambda t: t.matmul(
        out=g["pFq"][0:1, 4:5], lhsT=g["ones_col"][:], rhs=g["hcol_bf"][:],
        start=True, stop=True), deps=[tcb])
    tsp = b.op("vector", lambda v: v.tensor_copy(
        out=g["mrow"][0:1, 12:13], in_=g["pFq"][0:1, 4:5]), deps=[tms])
    return spill_toks, tsp


def build_l1b_program(debug=False):
    nc = bass.Bass("TRN2", target_bir_lowering=False)
    b = Bld(nc)
    g = {"nc": nc}
    g["lnst"] = {k: nc.alloc_sbuf_tensor("ln_" + k, [128, NT], F32)
                 for k in ("s1", "s2", "m", "v", "rstd", "mr")}
    g["junk"] = nc.alloc_sbuf_tensor("junk", [128, D], BF16)
    g["cbuf"] = nc.alloc_sbuf_tensor("cbuf", [128, 16], F32)
    alloc_l1b_buffers(nc, g)
    g["eye_sb"] = nc.alloc_sbuf_tensor("eye_sb", [128, 128], BF16)
    g["x_d"] = nc.dram_tensor("x", [T, D], F32, kind="ExternalInput").ap()
    for nm in ("WprojT", "WvT", "WqT", "WkT"):
        g[nm + "_d"] = nc.dram_tensor(nm, [D, D], F32,
                                      kind="ExternalInput").ap()
    g["W1T_d"] = nc.dram_tensor("W1T", [1024, 512], F32,
                                kind="ExternalInput").ap()
    g["bv_d"] = nc.dram_tensor("bv_row", [1, D], BF16,
                               kind="ExternalInput").ap()
    g["bq_d"] = nc.dram_tensor("bq_row", [1, D], BF16,
                               kind="ExternalInput").ap()
    g["bk_d"] = nc.dram_tensor("bk_row", [1, D], BF16,
                               kind="ExternalInput").ap()
    g["b1_d"] = nc.dram_tensor("b1_row", [1, 512], BF16,
                               kind="ExternalInput").ap()
    g["w2_d"] = nc.dram_tensor("w2_row", [1, 512], BF16,
                               kind="ExternalInput").ap()
    g["b2_d"] = nc.dram_tensor("b2_in", [1, 1], F32,
                               kind="ExternalInput").ap()
    gp_d = nc.dram_tensor("gp_in", [1, 8], F32, kind="ExternalInput").ap()
    bpc = nc.dram_tensor("bproj_cols_d", [128, 8], F32,
                         kind="ExternalInput").ap()
    eye_d = nc.dram_tensor("eye", [128, 128], BF16, kind="ExternalInput").ap()
    onr_d = nc.dram_tensor("ones_row_d", [1, 128], BF16,
                           kind="ExternalInput").ap()
    onc_d = nc.dram_tensor("ones_col_d", [128, 1], BF16,
                           kind="ExternalInput").ap()
    mm_d = nc.dram_tensor("mask_mid_d", [128, 384], BF16,
                          kind="ExternalInput").ap()
    mf_d = nc.dram_tensor("mask_first_d", [128, 256], BF16,
                          kind="ExternalInput").ap()
    g["o_dram"] = nc.dram_tensor("o_out", [T, D], BF16,
                                 kind="ExternalOutput").ap()
    g_out = nc.dram_tensor("g_out", [1, 1], F32, kind="ExternalOutput").ap()
    sp_out = nc.dram_tensor("sp_out", [1, 1], F32, kind="ExternalOutput").ap()
    dbg_outs = {}
    if debug:
        for nm, sh, dt in (("d_xqkT", [128, 8 * T], BF16),
                           ("d_qnT", [128, 8 * T], BF16),
                           ("d_knT", [128, 8 * T], BF16),
                           ("d_V", [128, NT * D], BF16),
                           ("d_nall", [128, 128], F32),
                           ("d_Hrows", [1, 64], F32),
                           ("d_g128", [128, 2], F32)):
            dbg_outs[nm] = nc.dram_tensor(nm, sh, dt, kind="ExternalOutput").ap()
    with (
        nc.psum_tensor("pS", [128, 512], F32) as pS,
        nc.psum_tensor("pT", [128, 1024], BF16) as pT,
        nc.psum_tensor("pE", [128, 1024], F32) as pE,
        nc.psum_tensor("pFq", [1, 1024], F32) as pFq,
    ):
        g["pS"], g["pT"], g["pE"], g["pFq"] = pS, pT, pE, pFq
        teye = b.dma("gpsimd", out=g["eye_sb"][:], in_=eye_d)
        tonr = b.dma("gpsimd", out=g["ones_row"][:], in_=onr_d)
        tonc = b.dma("gpsimd", out=g["ones_col"][:], in_=onc_d)
        tbp = b.dma("gpsimd", out=g["bproj_cols"][:], in_=bpc)
        tgp = b.dma("gpsimd", out=g["gp_row"][:], in_=gp_d)
        tmm_ = b.dma("gpsimd", out=g["mask_mid"][:], in_=mm_d)
        tmf_ = b.dma("gpsimd", out=g["mask_first"][:], in_=mf_d)
        g["t_bproj"] = tbp
        g["t_gp"] = tgp
        g["t_ones"] = tonr
        g["t_onec"] = tonc
        g["t_masks"] = tmm_
        pre = [teye, tonr, tonc, tbp, tgp, tmm_, tmf_]
        for e in ("vector", "scalar", "tensor"):
            b.wait(e, pre)
        xlnT_toks, qnT_toks, knT_toks = emit_l1b(b, g, [])
        o_toks = emit_l1b_attn(b, g, qnT_toks, knT_toks)
        dbg_t = []
        if debug:
            dbg_t.append(b.dma("scalar", out=dbg_outs["d_qnT"],
                               in_=g["dT"][:], deps=[o_toks[-1]]))
            dbg_t.append(b.dma("scalar", out=dbg_outs["d_knT"],
                               in_=g["knT"][:], deps=[o_toks[-1]]))
            dbg_t.append(b.dma("scalar", out=dbg_outs["d_V"],
                               in_=g["V"][:], deps=[o_toks[-1]]))
            dbg_t.append(b.dma("scalar", out=dbg_outs["d_nall"],
                               in_=g["n_all"][:], deps=[o_toks[-1]]))
        tgc, tS4 = emit_l1b_entropy(b, g)
        spill_toks, tsp = emit_l1b_social(b, g, tgc, o_toks,
                                          knT_toks[-1])
        tso = b.dma("gpsimd", out=sp_out, in_=g["mrow"][0:1, 12:13],
                    deps=[tsp])
        tgo = b.dma("gpsimd", out=g_out, in_=g["Hrows"][0:1, 41:42],
                    deps=[tS4])
        if debug:
            dbg_t.append(b.dma("scalar", out=dbg_outs["d_Hrows"],
                               in_=g["Hrows"][:], deps=[tsp]))
            dbg_t.append(b.dma("scalar", out=dbg_outs["d_g128"],
                               in_=g["g128"][:], deps=[tgc, tsp]))
        b.wait("gpsimd", spill_toks + [tso, tgo])
        if dbg_t:
            b.wait("scalar", dbg_t)
        nc_built = b.build()
    return nc_built



def emit_l1c(b, g, tin):
    tf1 = b.op("vector", lambda v: v.reduce_sum(
        out=g["frow"][0:1, 0:1], in_=g["sp_row"][:], axis=AX.X),
        deps=[g["t_sp"]])
    tf2 = b.op("scalar", lambda s: s.activation(
        out=g["frow"][0:1, 1:2], in_=g["frow"][0:1, 0:1], func=AF.Exp,
        scale=-2.0 / 16384.0), deps=[tf1])
    tf3 = b.op("vector", lambda v: v.tensor_scalar(
        out=g["frow"][0:1, 1:2], in0=g["frow"][0:1, 1:2], scalar1=1.5,
        scalar2=0.5, op0=ALU.min, op1=ALU.max), deps=[tf2])
    tf4 = b.op("vector", lambda v: v.tensor_scalar(
        out=g["frow"][0:1, 2:3], in0=g["frow"][0:1, 1:2], scalar1=0.3,
        scalar2=0.7, op0=ALU.mult, op1=ALU.add), deps=[tf3])
    tf5 = b.op("vector", lambda v: v.tensor_copy(
        out=g["frow_bf"][0:1, 0:1], in_=g["frow"][0:1, 2:3]), deps=[tf4])
    tfm = b.op("tensor", lambda t: t.matmul(
        out=g["pB"][:, 0:1], lhsT=g["ones_row"][:],
        rhs=g["frow_bf"][0:1, 0:1], start=True, stop=True),
        deps=[tf5, g["t_ones"]])
    tfc = b.op("vector", lambda v: v.tensor_copy(
        out=g["fcol"][:], in_=g["pB"][:, 0:1]), deps=[tfm])
    tbc = None
    for o in (0, 512):
        mmb = None
        for bo2 in (0, 256):
            mmb = b.op("tensor", lambda t, o=o, bo2=bo2,
                       oo=g["pB"][:, bo2:bo2 + 256],
                       rr=g["bias_row"][0:1, o + bo2:o + bo2 + 256]:
                       t.matmul(out=oo, lhsT=g["ones_row"][:], rhs=rr,
                                start=True, stop=True),
                       deps=[g["t_bo"], tfc, tbc] if bo2 == 0 else ())
        tbc = b.op("vector", lambda v, o=o: v.tensor_copy(
            out=g["bo_bc"][:, o:o + 512], in_=g["pB"][:, 0:512]),
            deps=[mmb])
    two = emit_wfull(b, g, g["WoT_d"])
    prev_ev = None
    prev_oT = None
    tile_toks = []
    od_toks = {}
    for i in range(NT):
        slot = i % 2
        tod = b.dma("gpsimd",
                    out=g["ostage"][:, slot * 1024:(slot + 1) * 1024],
                    in_=g["o_in_d"][i * 128:(i + 1) * 128, :],
                    deps=[od_toks.get(i - 2)])
        pe_last = None
        for c in range(8):
            pe_last = b.op("tensor", lambda t, c=c, slot=slot,
                           oo=g["pT"][:, c * 128:(c + 1) * 128],
                           ii=g["ostage"][:, slot * 1024 + c * 128:
                                          slot * 1024 + (c + 1) * 128],
                           ee=g["eye_sb"][:]: t.transpose(
                out=oo, in_=ii, identity=ee),
                deps=[tod, prev_oT] if c == 0 else ())
        od_toks[i] = pe_last
        toT = b.op("vector", lambda v: v.tensor_copy(
            out=g["oT_bf"][:], in_=g["pT"][:, 0:1024]), deps=[pe_last])
        prev_oT = toT
        for (o, w) in ((0, 512), (512, 512)):
            mm = None
            for dc in range(8):
                mm = b.op("tensor", lambda t, dc=dc, o=o,
                          ll=g["oT_bf"][:, dc * 128:(dc + 1) * 128],
                          rr=g["Wfull"][:, dc * 1024 + o:
                                        dc * 1024 + o + 512],
                          st_=(dc == 0), sp=(dc == 7): t.matmul(
                    out=g["pS"][:, 0:512], lhsT=ll, rhs=rr, start=st_,
                    stop=sp),
                    deps=[toT, two, prev_ev] if dc == 0 else ())
            tev = b.op("vector", lambda v, o=o: v.scalar_tensor_tensor(
                out=g["scr"][:], in0=g["pS"][:, 0:512],
                scalar=g["fcol"][:, 0:1], in1=g["bo_bc"][:, o:o + 512],
                op0=ALU.mult, op1=ALU.add), deps=[mm, tfc, tbc])
            prev_ev = b.op("vector", lambda v, i=i, o=o: v.tensor_add(
                out=g["X"][:, i * D + o:i * D + o + 512],
                in0=g["X"][:, i * D + o:i * D + o + 512],
                in1=g["scr"][:]), deps=[tev] + (list(tin) if i == 0 else []))
        tile_toks.append(prev_ev)
    return tile_toks


def _alloc_l1c_buffers(nc, g):
    g["Wfull"] = nc.alloc_sbuf_tensor("Wfull", [128, 8 * 1024], BF16)
    g["Wstage"] = nc.alloc_sbuf_tensor("Wstage", [128, 1024], F32)
    g["ostage"] = nc.alloc_sbuf_tensor("ostage", [128, 2 * 1024], BF16)
    g["oT_bf"] = nc.alloc_sbuf_tensor("oT_bf", [128, 1024], BF16)
    g["bo_bc"] = nc.alloc_sbuf_tensor("bo_bc", [128, 1024], F32)
    g["scr"] = nc.alloc_sbuf_tensor("scr", [128, 512], F32)
    g["frow"] = nc.alloc_sbuf_tensor("frow", [1, 16], F32)
    g["frow_bf"] = nc.alloc_sbuf_tensor("frow_bf", [1, 16], BF16)
    g["fcol"] = nc.alloc_sbuf_tensor("fcol", [128, 1], F32)
    g["sp_row"] = nc.alloc_sbuf_tensor("sp_row", [1, 8], F32)
    g["bias_row"] = nc.alloc_sbuf_tensor("bias_row", [1, 1024], BF16)


def build_l1c_program():
    nc = bass.Bass("TRN2", target_bir_lowering=False)
    b = Bld(nc)
    g = {"nc": nc}
    g["X"] = nc.alloc_sbuf_tensor("Xsb", [128, NT * D], F32)
    _alloc_l1c_buffers(nc, g)
    g["ones_row"] = nc.alloc_sbuf_tensor("ones_row", [1, 128], BF16)
    g["eye_sb"] = nc.alloc_sbuf_tensor("eye_sb", [128, 128], BF16)
    g["cbuf"] = nc.alloc_sbuf_tensor("cbuf", [128, 16], F32)
    x_in = nc.dram_tensor("x", [T, D], F32, kind="ExternalInput").ap()
    g["o_in_d"] = nc.dram_tensor("o_in", [T, D], BF16,
                                 kind="ExternalInput").ap()
    sp_d = nc.dram_tensor("sp_in", [1, 8], F32, kind="ExternalInput").ap()
    g["WoT_d"] = nc.dram_tensor("WoT", [D, D], F32,
                                kind="ExternalInput").ap()
    bo_d = nc.dram_tensor("bo_row", [1, D], BF16, kind="ExternalInput").ap()
    eye_d = nc.dram_tensor("eye", [128, 128], BF16, kind="ExternalInput").ap()
    onr_d = nc.dram_tensor("ones_row_d", [1, 128], BF16,
                           kind="ExternalInput").ap()
    y_out = nc.dram_tensor("y", [T, D], F32, kind="ExternalOutput").ap()
    with (
        nc.psum_tensor("pS", [128, 512], F32) as pS,
        nc.psum_tensor("pB", [128, 512], F32) as pB,
        nc.psum_tensor("pT", [128, 1024], BF16) as pT,
    ):
        g["pS"], g["pB"], g["pT"] = pS, pB, pT
        teye = b.dma("gpsimd", out=g["eye_sb"][:], in_=eye_d)
        tonr = b.dma("gpsimd", out=g["ones_row"][:], in_=onr_d)
        tspr = b.dma("gpsimd", out=g["sp_row"][:], in_=sp_d)
        tbo = b.dma("sync", out=g["bias_row"][:], in_=bo_d)
        g["t_sp"], g["t_bo"], g["t_ones"] = tspr, tbo, tonr
        tx_toks = []
        for hh in range(2):
            tx_toks.append(b.dma("gpsimd",
                           out=g["X"][:, hh * 8 * D:(hh + 1) * 8 * D].rearrange(
                               "p (i f) -> p i f", f=D),
                           in_=x_in.rearrange("(i p) f -> p i f", p=128)
                           [:, hh * 8:(hh + 1) * 8, :]))
        for e in ("vector", "scalar", "tensor"):
            b.wait(e, [teye, tonr, tspr, tbo] + tx_toks)
        tile_toks = emit_l1c(b, g, tx_toks)
        st_toks = []
        for i in range(NT):
            st_toks.append(b.dma("gpsimd",
                           out=y_out[i * 128:(i + 1) * 128, :],
                           in_=g["X"][:, i * D:(i + 1) * D],
                           deps=[tile_toks[i]]))
        b.wait("gpsimd", st_toks)
        nc_built = b.build()
    return nc_built



def build_moela_program():
    nc = bass.Bass("TRN2", target_bir_lowering=False)
    b = Bld(nc)
    g = {"nc": nc}
    g["X"] = nc.alloc_sbuf_tensor("Xsb", [128, NT * D], F32)
    g["lnst"] = {k: nc.alloc_sbuf_tensor("ln_" + k, [128, NT], F32)
                 for k in ("s1", "s2", "m", "v", "rstd", "mr")}
    g["junk"] = nc.alloc_sbuf_tensor("junk", [128, D], BF16)
    g["cbuf"] = nc.alloc_sbuf_tensor("cbuf", [128, 16], F32)
    alloc_moe_buffers(nc, g)
    g["eye_sb"] = nc.alloc_sbuf_tensor("eye_sb", [128, 128], BF16)
    g["ones_row"] = nc.alloc_sbuf_tensor("ones_row", [1, 128], BF16)
    g["ones_col"] = nc.alloc_sbuf_tensor("ones_col", [128, 1], BF16)
    g["xlnT"] = g["macc"][:].bitcast(BF16)
    g["Wstage"] = nc.alloc_sbuf_tensor("Wstage", [128, 1024], F32)
    g["Wblk"] = nc.alloc_sbuf_tensor("Wblk", [128, 2 * 1024], BF16)
    g["res_sb"] = nc.alloc_sbuf_tensor("res_sb", [128, 512], F32)
    g["rn2_sb"] = nc.alloc_sbuf_tensor("rn2_sb", [1, T], F32)
    g["xn2_row"] = nc.alloc_sbuf_tensor("xn2_row", [1, T], F32)
    g["gp_small"] = nc.alloc_sbuf_tensor("gp_small", [128, 4], F32)
    g["bproj_cols"] = nc.alloc_sbuf_tensor("bproj_cols", [128, 8], F32)
    g["res_bf"] = nc.alloc_sbuf_tensor("res_bf", [128, 512], BF16)
    x_in = nc.dram_tensor("x", [T, D], F32, kind="ExternalInput").ap()
    y_out = nc.dram_tensor("y", [T, D], F32, kind="ExternalOutput").ap()
    gp_out = nc.dram_tensor("gp", [1, 1], F32, kind="ExternalOutput").ap()
    g["ew_d"] = nc.dram_tensor("expert_w", [8 * D, D], F32,
                               kind="ExternalInput").ap()
    g["rw_d"] = nc.dram_tensor("router_w", [D, 8], F32,
                               kind="ExternalInput").ap()
    eb_d = nc.dram_tensor("expert_b", [8, D], F32, kind="ExternalInput").ap()
    rb_d = nc.dram_tensor("router_b", [1, 8], F32, kind="ExternalInput").ap()
    g["WprojT_d"] = nc.dram_tensor("WprojT", [D, D], F32,
                                   kind="ExternalInput").ap()
    bpc = nc.dram_tensor("bproj_cols_d", [128, 8], F32,
                         kind="ExternalInput").ap()
    eye_d = nc.dram_tensor("eye", [128, 128], BF16, kind="ExternalInput").ap()
    onr = nc.dram_tensor("ones_row_d", [1, 128], BF16,
                         kind="ExternalInput").ap()
    onc_d = nc.dram_tensor("ones_col_d", [128, 1], BF16,
                           kind="ExternalInput").ap()
    with (
        nc.psum_tensor("pA", [128, 512], F32) as pA,
        nc.psum_tensor("pE", [128, 1024], F32) as pE,
        nc.psum_tensor("pT", [128, 1024], BF16) as pT,
        nc.psum_tensor("pF", [1, 2048], F32) as pF,
    ):
        g["pA"], g["pE"], g["pT"], g["pF"] = pA, pE, pT, pF
        g["pS"] = pA
        teye = b.dma("gpsimd", out=g["eye_sb"][:], in_=eye_d)
        tonr = b.dma("gpsimd", out=g["ones_row"][:], in_=onr)
        tonc = b.dma("gpsimd", out=g["ones_col"][:], in_=onc_d)
        tbp = b.dma("gpsimd", out=g["bproj_cols"][:], in_=bpc)
        g["t_bproj"] = tbp
        tx_toks = []
        for hh in range(2):
            tx_toks.append(b.dma("gpsimd",
                           out=g["X"][:, hh * 8 * D:(hh + 1) * 8 * D].rearrange(
                               "p (i f) -> p i f", f=D),
                           in_=x_in.rearrange("(i p) f -> p i f", p=128)
                           [:, hh * 8:(hh + 1) * 8, :]))
        tx = tx_toks
        drw = b.dma("sync", out=g["macc"][:, 0:64].rearrange(
                        "p (c e) -> p c e", e=8),
                    in_=g["rw_d"].rearrange("(c p) e -> p c e", p=128))
        g["t_rw"] = b.op("vector", lambda v: v.tensor_copy(
            out=g["rw_bf"][:], in_=g["macc"][:, 0:64]), deps=[drw])
        deb = b.dma("sync", out=g["macc"][0:8, 64:64 + D], in_=eb_d)
        g["t_eb"] = b.op("vector", lambda v: v.tensor_copy(
            out=g["eb_bf"][:], in_=g["macc"][0:8, 64:64 + D]), deps=[deb])
        drb = b.dma("sync", out=g["macc"][0:1, 1088:1096], in_=rb_d)
        g["t_rb"] = b.op("vector", lambda v: v.tensor_copy(
            out=g["rb_row"][:], in_=g["macc"][0:1, 1088:1096]), deps=[drb])
        for e in ("vector", "scalar", "tensor"):
            b.wait(e, [teye, tonr, tonc, tbp, g["t_rw"], g["t_eb"],
                       g["t_rb"]] + tx)
        res = emit_moe(b, g, tx)
        tps = emit_l1a(b, g, list(res))
        tst = [b.dma("gpsimd", out=gp_out, in_=g["gp_small"][0:1, 0:1],
                     deps=[tps])]
        for hh in range(2):
            tst.append(b.dma("gpsimd",
                       out=y_out.rearrange("(i p) f -> p i f", p=128)
                       [:, hh * 8:(hh + 1) * 8, :],
                       in_=g["X"][:, hh * 8 * D:(hh + 1) * 8 * D].rearrange(
                           "p (i f) -> p i f", f=D),
                       deps=res))
        b.wait("gpsimd", tst)
        nc_built = b.build()
    return nc_built


def build_l1cmoe_program():
    nc = bass.Bass("TRN2", target_bir_lowering=False)
    b = Bld(nc)
    g = {"nc": nc}
    g["X"] = nc.alloc_sbuf_tensor("Xsb", [128, NT * D], F32)
    g["lnst"] = {k: nc.alloc_sbuf_tensor("ln_" + k, [128, NT], F32)
                 for k in ("s1", "s2", "m", "v", "rstd", "mr")}
    g["junk"] = nc.alloc_sbuf_tensor("junk", [128, D], BF16)
    g["cbuf"] = nc.alloc_sbuf_tensor("cbuf", [128, 16], F32)
    _alloc_l1c_buffers(nc, g)
    alloc_moe_buffers(nc, g)
    g["eye_sb"] = nc.alloc_sbuf_tensor("eye_sb", [128, 128], BF16)
    g["ones_row"] = nc.alloc_sbuf_tensor("ones_row", [1, 128], BF16)
    x_in = nc.dram_tensor("x", [T, D], F32, kind="ExternalInput").ap()
    g["o_in_d"] = nc.dram_tensor("o_in", [T, D], BF16,
                                 kind="ExternalInput").ap()
    sp_d = nc.dram_tensor("sp_in", [1, 8], F32, kind="ExternalInput").ap()
    g["WoT_d"] = nc.dram_tensor("WoT", [D, D], F32,
                                kind="ExternalInput").ap()
    bo_d = nc.dram_tensor("bo_row", [1, D], BF16, kind="ExternalInput").ap()
    g["ew_d"] = nc.dram_tensor("expert_w", [8 * D, D], F32,
                               kind="ExternalInput").ap()
    g["rw_d"] = nc.dram_tensor("router_w", [D, 8], F32,
                               kind="ExternalInput").ap()
    eb_d = nc.dram_tensor("expert_b", [8, D], F32, kind="ExternalInput").ap()
    rb_d = nc.dram_tensor("router_b", [1, 8], F32, kind="ExternalInput").ap()
    eye_d = nc.dram_tensor("eye", [128, 128], BF16, kind="ExternalInput").ap()
    onr = nc.dram_tensor("ones_row_d", [1, 128], BF16,
                         kind="ExternalInput").ap()
    y_out = nc.dram_tensor("y", [T, D], F32, kind="ExternalOutput").ap()
    with (
        nc.psum_tensor("pS", [128, 512], F32) as pS,
        nc.psum_tensor("pB", [128, 512], F32) as pB,
        nc.psum_tensor("pT", [128, 1024], BF16) as pT,
        nc.psum_tensor("pE", [128, 1024], F32) as pE,
    ):
        g["pS"], g["pB"], g["pT"], g["pE"] = pS, pB, pT, pE
        g["pA"] = pB
        teye = b.dma("gpsimd", out=g["eye_sb"][:], in_=eye_d)
        tonr = b.dma("gpsimd", out=g["ones_row"][:], in_=onr)
        tspr = b.dma("gpsimd", out=g["sp_row"][:], in_=sp_d)
        tbo = b.dma("sync", out=g["bias_row"][:], in_=bo_d)
        g["t_sp"], g["t_bo"], g["t_ones"] = tspr, tbo, tonr
        tx_toks = []
        for hh in range(2):
            tx_toks.append(b.dma("gpsimd",
                           out=g["X"][:, hh * 8 * D:(hh + 1) * 8 * D].rearrange(
                               "p (i f) -> p i f", f=D),
                           in_=x_in.rearrange("(i p) f -> p i f", p=128)
                           [:, hh * 8:(hh + 1) * 8, :]))
        drw = b.dma("sync", out=g["macc"][:, 0:64].rearrange(
                        "p (c e) -> p c e", e=8),
                    in_=g["rw_d"].rearrange("(c p) e -> p c e", p=128))
        g["t_rw"] = b.op("vector", lambda v: v.tensor_copy(
            out=g["rw_bf"][:], in_=g["macc"][:, 0:64]), deps=[drw])
        deb = b.dma("sync", out=g["macc"][0:8, 64:64 + D], in_=eb_d)
        g["t_eb"] = b.op("vector", lambda v: v.tensor_copy(
            out=g["eb_bf"][:], in_=g["macc"][0:8, 64:64 + D]), deps=[deb])
        drb = b.dma("sync", out=g["macc"][0:1, 1088:1096], in_=rb_d)
        g["t_rb"] = b.op("vector", lambda v: v.tensor_copy(
            out=g["rb_row"][:], in_=g["macc"][0:1, 1088:1096]), deps=[drb])
        for e in ("vector", "scalar", "tensor"):
            b.wait(e, [teye, tonr, tspr, tbo, g["t_rw"], g["t_eb"],
                       g["t_rb"]] + tx_toks)
        tile_toks = emit_l1c(b, g, tx_toks)
        res = emit_moe(b, g, tile_toks)
        tst = []
        for hh in range(2):
            tst.append(b.dma("gpsimd",
                       out=y_out.rearrange("(i p) f -> p i f", p=128)
                       [:, hh * 8:(hh + 1) * 8, :],
                       in_=g["X"][:, hh * 8 * D:(hh + 1) * 8 * D].rearrange(
                           "p (i f) -> p i f", f=D),
                       deps=res))
        b.wait("gpsimd", tst)
        nc_built = b.build()
    return nc_built


def _eye_bf():
    return np.eye(128, dtype=ml_dtypes.bfloat16)


def _l1b_masks():
    p = np.arange(128)[:, None]
    c = np.arange(384)[None, :]
    mm = np.where(np.abs(p + 128 - c) <= 32, 0.0, -1e9)
    mm = mm.astype(ml_dtypes.bfloat16)
    c2 = np.arange(256)[None, :]
    mf = np.where(np.abs(p - c2) <= 32, 0.0, -1e9)
    mf = mf.astype(ml_dtypes.bfloat16)
    return mm, mf


_PROGS = {}


def _prog(name):
    if name not in _PROGS:
        _PROGS[name] = {
            "ffn0": build_ffn0_program,
            "ffn0x": lambda: build_ffn0_program(export_wq=True),
            "ffn0p": lambda: build_ffn0_program(prequant=True),
            "attn0": build_attn0_program,
            "moe": build_moe_program,
            "l1a": build_l1a_program,
            "l1b": build_l1b_program,
            "l1c": build_l1c_program,
            "moela": build_moela_program,
            "l1cmoe": build_l1cmoe_program,
        }[name]()
    return _PROGS[name]


def _launch(name, per_core):
    nc = _prog(name)
    res = run_bass_kernel_spmd(nc, per_core, core_ids=list(range(8)))
    return res.results


def kernel(**inputs):
    BF = ml_dtypes.bfloat16
    f32c = np.ascontiguousarray
    x = f32c(inputs["L0_attn_x"].astype(np.float32))
    eye = _eye_bf()
    onr = np.ones((1, 128), dtype=BF)
    onc = np.ones((128, 1), dtype=BF)
    mm, mf = _l1b_masks()

    wg = f32c(inputs["L0_ffn_w_gate"].astype(np.float32))
    wu = f32c(inputs["L0_ffn_w_up"].astype(np.float32))
    wd = f32c(inputs["L0_ffn_w_down"].astype(np.float32))
    ffn_ins = [{"x": f32c(x[c]), "w_gate": wg, "w_up": wu, "w_down": wd,
                "eye": eye} for c in range(8)]
    r = _launch("ffn0x", ffn_ins)
    xs = [np.asarray(r[c]["y"]) for c in range(8)]
    wq_cache = {k: np.asarray(r[0][k]) for k in ("wgT_q", "wuT_q", "wdT_q")}

    WqT0 = f32c(inputs["L0_attn_Wq"].T.astype(np.float32))
    WkT0 = f32c(inputs["L0_attn_Wk"].T.astype(np.float32))
    bq0 = f32c(inputs["L0_attn_bq"].reshape(1, 1024).astype(np.float32))
    bk0 = f32c(inputs["L0_attn_bk"].reshape(8, 128).T.astype(np.float32))
    a_ins = [{"x": xs[c], "WqT": WqT0, "WkT": WkT0, "bq_row": bq0,
              "bk_cols": bk0, "eye": eye, "ones_row_d": onr,
              "ones_col_d": onc} for c in range(8)]
    r = _launch("attn0", a_ins)
    xs = [np.asarray(r[c]["y"]) for c in range(8)]

    ffn_ins = [dict(x=xs[c], eye=eye, **wq_cache) for c in range(8)]
    r = _launch("ffn0p", ffn_ins)
    xs = [np.asarray(r[c]["y"]) for c in range(8)]

    ew = f32c(inputs["L1_ffn_expert_w"].reshape(8 * 1024, 1024)
              .astype(np.float32))
    eb = f32c(inputs["L1_ffn_expert_b"].astype(np.float32))
    rw = f32c(inputs["L1_ffn_router_w"].astype(np.float32))
    rb = f32c(inputs["L1_ffn_router_b"].reshape(1, 8).astype(np.float32))
    WprojT = f32c(inputs["L1_attn_Wproj"].T.astype(np.float32))
    bproj_cols = f32c(inputs["L1_attn_bproj"].reshape(8, 128).T
                      .astype(np.float32))
    m_ins = [{"x": xs[c], "expert_w": ew, "router_w": rw, "expert_b": eb,
              "router_b": rb, "eye": eye, "ones_row_d": onr,
              "ones_col_d": onc, "WprojT": WprojT,
              "bproj_cols_d": bproj_cols}
             for c in range(8)]
    r = _launch("moela", m_ins)
    xs = [np.asarray(r[c]["y"]) for c in range(8)]
    gp_row = np.concatenate([np.asarray(r[c]["gp"]) for c in range(8)],
                            axis=1).astype(np.float32)
    gp_row = f32c(gp_row)

    lb_ins = [{
        "x": xs[c],
        "WprojT": WprojT,
        "WvT": f32c(inputs["L1_attn_Wv"].T.astype(np.float32)),
        "WqT": f32c(inputs["L1_attn_Wq"].T.astype(np.float32)),
        "WkT": f32c(inputs["L1_attn_Wk"].T.astype(np.float32)),
        "W1T": f32c(inputs["L1_attn_W1"].T.astype(np.float32)),
        "bv_row": inputs["L1_attn_bv"].reshape(1, 1024).astype(BF),
        "bq_row": inputs["L1_attn_bq"].reshape(1, 1024).astype(BF),
        "bk_row": inputs["L1_attn_bk"].reshape(1, 1024).astype(BF),
        "b1_row": inputs["L1_attn_b1"].reshape(1, 512).astype(BF),
        "w2_row": inputs["L1_attn_W2"].reshape(1, 512).astype(BF),
        "b2_in": f32c(inputs["L1_attn_b2"].reshape(1, 1)
                      .astype(np.float32)),
        "gp_in": gp_row,
        "bproj_cols_d": bproj_cols,
        "eye": eye, "ones_row_d": onr, "ones_col_d": onc,
        "mask_mid_d": mm, "mask_first_d": mf,
    } for c in range(8)]
    r = _launch("l1b", lb_ins)
    os_ = [np.asarray(r[c]["o_out"]) for c in range(8)]
    sp_row = np.concatenate([np.asarray(r[c]["sp_out"]) for c in range(8)],
                            axis=1).astype(np.float32)
    sp_row = f32c(sp_row)

    WoT = f32c(inputs["L1_attn_Wo"].T.astype(np.float32))
    bo_row = inputs["L1_attn_bo"].reshape(1, 1024).astype(BF)
    lc_ins = [{"x": xs[c], "o_in": os_[c], "sp_in": sp_row, "WoT": WoT,
               "bo_row": bo_row, "eye": eye, "ones_row_d": onr,
               "expert_w": ew, "router_w": rw, "expert_b": eb,
               "router_b": rb}
              for c in range(8)]
    r = _launch("l1cmoe", lc_ins)
    out = np.stack([np.asarray(r[c]["y"]) for c in range(8)], axis=0)
    return out.astype(np.float32)


# version 06 (final)
# speedup vs baseline: 1.0085x; 1.0085x over previous
import numpy as np
import ml_dtypes
import concourse.bass as bass
import concourse.mybir as mybir
from concourse.bass_utils import run_bass_kernel_spmd

F32 = mybir.dt.float32
BF16 = mybir.dt.bfloat16
AF = mybir.ActivationFunctionType
ALU = mybir.AluOpType
AX = mybir.AxisListType

T = 2048
D = 1024
NT = 16
I_FFN = 2816
EPS_LN = 1e-5
LN6 = float(np.log(6.0))
LN36 = float(np.log(36.0))

ENGS = ("sync", "vector", "scalar", "tensor", "gpsimd")

FP4_TH2 = [t * t for t in (0.25, 0.75, 1.25, 1.75, 2.5, 3.5, 5.0)]
FP4_D = [0.5, 0.5, 0.5, 0.5, 1.0, 1.0, 2.0]


class Bld:

    def __init__(self, nc):
        import os as _os
        self.nc = nc
        self.serial = bool(_os.environ.get("BLD_SERIAL"))
        self.strict = _os.environ.get("BLD_STRICT", "1") != "0"
        self._last = None
        self._elast = {}
        self.plan = {e: [] for e in ENGS}
        self.cnt = {}
        self.order = []
        for e in ENGS:
            self._sem("c_" + e)
        self.NQS = 8
        self.qslot = {}
        for q in ("sync", "gpsimd", "scalar"):
            for k in range(self.NQS):
                self._sem("d_%s%d" % (q, k))
            self.qslot[q] = 0

    def _sem(self, name):
        if name not in self.cnt:
            self.cnt[name] = 0
            self.order.append(name)
        return name

    def op(self, eng, fn, deps=(), inc=True):
        sem = "c_" + eng
        if self.serial and self._last is not None:
            deps = tuple(deps) + (self._last,)
        if self.strict and eng != "tensor" and self._elast.get(eng) is not None:
            deps = tuple(deps) + (self._elast[eng],)
        if inc:
            self.cnt[sem] += 1
            tok = (sem, self.cnt[sem])
        else:
            tok = None
        self.plan[eng].append((fn, tuple(d for d in deps if d is not None),
                               sem if inc else None, 1))
        if inc and self.serial:
            self._last = tok
        if inc and self.strict:
            self._elast[eng] = tok
        return tok

    def dma(self, queue, out, in_, deps=()):
        k = self.qslot[queue]
        self.qslot[queue] = (k + 1) % self.NQS
        sem = "d_%s%d" % (queue, k)
        if self.serial and self._last is not None:
            deps = tuple(deps) + (self._last,)
        if self.cnt[sem] > 0:
            deps = tuple(deps) + ((sem, self.cnt[sem]),)
        self.cnt[sem] += 16

        def fn(eng, out=out, in_=in_):
            return eng.dma_start(out=out, in_=in_)
        self.plan[queue].append((fn, tuple(d for d in deps if d is not None),
                                 sem, 16))
        tok = (sem, self.cnt[sem])
        if self.serial:
            self._last = tok
        return tok

    def wait(self, eng, deps):
        self.plan[eng].append((None, tuple(d for d in deps if d is not None),
                               None, 0))

    def build(self):
        nc = self.nc
        import contextlib
        stack = contextlib.ExitStack()
        sems = {name: stack.enter_context(nc.semaphore(name))
                for name in self.order}
        blk = stack.enter_context(nc.Block())

        def runner_for(ename):
            plan = self.plan[ename]

            def run(eng):
                mw = {}
                for fn, deps, sem, amt in plan:
                    for (s, v) in deps:
                        if v > mw.get(s, 0):
                            eng.wait_ge(sems[s], v)
                            mw[s] = v
                    if fn is None:
                        continue
                    inst = fn(eng)
                    if sem is not None:
                        inst.then_inc(sems[sem], amt)
            return run
        for e in ENGS:
            getattr(blk, e)(runner_for(e))
        stack.close()
        return nc



def emit_ln_stats(b, X, s1_col, s2_col, junk, deps=()):
    t1 = b.op("vector", lambda v: v.reduce_sum(out=s1_col, in_=X, axis=AX.X),
              deps=deps)
    t2 = b.op("vector", lambda v: v.scalar_tensor_tensor(
        out=junk, in0=X, scalar=1.0, in1=X, op0=ALU.mult, op1=ALU.mult,
        accum_out=s2_col), deps=deps)
    return t1, t2


def emit_ln_batch(b, g, st, n, deps, off=0):
    s1, s2 = st["s1"][:, off:off + n], st["s2"][:, off:off + n]
    m, vv = st["m"][:, off:off + n], st["v"][:, off:off + n]
    rstd, mr = st["rstd"][:, off:off + n], st["mr"][:, off:off + n]
    tm = b.op("vector", lambda v: v.tensor_scalar(
        out=m, in0=s1, scalar1=1.0 / D, scalar2=None, op0=ALU.mult), deps=deps)
    tms = b.op("vector", lambda v: v.tensor_mul(out=vv, in0=m, in1=m), deps=[tm])
    tv = b.op("vector", lambda v: v.tensor_scalar(
        out=rstd, in0=s2, scalar1=1.0 / D, scalar2=None, op0=ALU.mult), deps=deps)
    tsub = b.op("vector", lambda v: v.tensor_sub(out=vv, in0=rstd, in1=vv),
                deps=[tms, tv])
    eps_ap, eps_tok = get_const(b, g, EPS_LN)
    tln = b.op("scalar", lambda s: s.activation(out=vv, in_=vv, func=AF.Ln,
                                                bias=eps_ap), deps=[tsub, eps_tok])
    trs = b.op("scalar", lambda s: s.activation(out=rstd, in_=vv, func=AF.Exp,
                                                scale=-0.5), deps=[tln])
    tmr = b.op("vector", lambda v: v.tensor_mul(out=mr, in0=m, in1=rstd),
               deps=[trs])
    return tmr, trs


def emit_ln_apply(b, X, xn_bf, rstd_col, mr_col, deps):
    return b.op("vector", lambda v: v.tensor_scalar(
        out=xn_bf, in0=X, scalar1=rstd_col, scalar2=mr_col,
        op0=ALU.mult, op1=ALU.subtract), deps=deps)


def emit_quant(b, g, u, st, src, xq_bf, K, deps):
    nb = K // 16
    x2, a2 = u["x2"][:, 0:K], u["a2"][:, 0:K]
    lvl, sg = u["lvl"][:, 0:K], u["sg"][:, 0:K]
    is2e, se = u["is2e"][:, 0:K], u["se"][:, 0:K]
    bm2, L = st["bm2"][:, 0:nb], st["L"][:, 0:nb]
    is2, s_ = st["is2"][:, 0:nb], st["s"][:, 0:nb]
    tx2 = b.op("vector", lambda v: v.tensor_mul(out=x2, in0=src, in1=src),
               deps=deps)
    tbm = b.op("vector", lambda v: v.tensor_reduce(
        out=bm2, in_=x2.rearrange("p (bb w) -> p bb w", w=16), axis=AX.X,
        op=ALU.max), deps=[tx2])
    tcl = b.op("vector", lambda v: v.tensor_scalar(
        out=bm2, in0=bm2, scalar1=3.6e-23, scalar2=None, op0=ALU.max),
        deps=[tbm])
    tln = b.op("scalar", lambda s: s.activation(out=L, in_=bm2, func=AF.Ln),
               deps=[tcl])
    c36, c36t = get_const(b, g, LN36)
    cm6, cm6t = get_const(b, g, -LN6)
    tis = b.op("scalar", lambda s: s.activation(out=is2, in_=L, func=AF.Exp,
                                                scale=-1.0, bias=c36),
               deps=[tln, c36t])
    ts_ = b.op("scalar", lambda s: s.activation(out=s_, in_=L, func=AF.Exp,
                                                scale=0.5, bias=cm6),
               deps=[tln, cm6t])
    x2v = x2.rearrange("p (bb w) -> p bb w", w=16)
    a2v = a2.rearrange("p (bb w) -> p bb w", w=16)
    ta2 = b.op("vector", lambda v: v.tensor_mul(
        out=a2v, in0=x2v,
        in1=is2.unsqueeze(2).broadcast_to([128, nb, 16])), deps=[tis, tx2])
    cb = u["m0"][:, 0:K]
    cc = u["m1"][:, 0:K]
    t0 = b.op("vector", lambda v: v.tensor_scalar(
        out=lvl, in0=a2, scalar1=FP4_TH2[0], scalar2=None, op0=ALU.is_gt),
        deps=[ta2])
    tacc = t0
    for j in (1, 2, 3):
        tacc = b.op("vector", lambda v, th2=FP4_TH2[j]:
                    v.scalar_tensor_tensor(
            out=lvl, in0=a2, scalar=th2, in1=lvl, op0=ALU.is_gt,
            op1=ALU.add), deps=[tacc])
    t4 = b.op("vector", lambda v: v.tensor_scalar(
        out=cb, in0=a2, scalar1=FP4_TH2[4], scalar2=None, op0=ALU.is_gt),
        deps=[ta2])
    t5 = b.op("vector", lambda v: v.scalar_tensor_tensor(
        out=cb, in0=a2, scalar=FP4_TH2[5], in1=cb, op0=ALU.is_gt,
        op1=ALU.add), deps=[t4])
    t6 = b.op("vector", lambda v: v.tensor_scalar(
        out=cc, in0=a2, scalar1=FP4_TH2[6], scalar2=None, op0=ALU.is_gt),
        deps=[ta2])
    tm1 = b.op("vector", lambda v: v.scalar_tensor_tensor(
        out=lvl, in0=lvl, scalar=0.5, in1=cb, op0=ALU.mult, op1=ALU.add),
        deps=[tacc, t5])
    tm2 = b.op("vector", lambda v: v.scalar_tensor_tensor(
        out=lvl, in0=cc, scalar=2.0, in1=lvl, op0=ALU.mult, op1=ALU.add),
        deps=[tm1, t6])
    tsg = b.op("scalar", lambda s: s.sign(out=sg, in_=src), deps=list(deps) + [ts_])
    tls = b.op("vector", lambda v: v.tensor_mul(out=lvl, in0=lvl, in1=sg),
               deps=[tm2, tsg])
    lvlv = lvl.rearrange("p (bb w) -> p bb w", w=16)
    xqv = xq_bf.rearrange("p (bb w) -> p bb w", w=16)
    tq = b.op("vector", lambda v: v.tensor_mul(
        out=xqv, in0=lvlv,
        in1=s_.unsqueeze(2).broadcast_to([128, nb, 16])), deps=[tls, ts_])
    return tq


def emit_transpose(b, pT, src_bf, dst_bf, eye, nchunk, deps, psum_free=None):
    d = list(deps)
    free = psum_free
    tc = None
    for g0 in range(0, nchunk, 8):
        gn = min(8, nchunk - g0)
        dd = list(d)
        if free is not None:
            dd.append(free)
        pe_last = None
        for c in range(gn):
            cc = g0 + c
            pe_last = b.op("tensor", lambda t, cc=cc, c=c: t.transpose(
                out=pT[:, c * 128:(c + 1) * 128],
                in_=src_bf[:, cc * 128:(cc + 1) * 128], identity=eye),
                deps=dd if c == 0 else ())
        tc = b.op("vector", lambda v, g0=g0, gn=gn: v.tensor_copy(
            out=dst_bf[:, g0 * 128:(g0 + gn) * 128], in_=pT[:, 0:gn * 128]),
            deps=[pe_last])
        free = tc
    return tc


def strips_of(width, mx=512):
    out, o = [], 0
    while o < width:
        w = min(mx, width - o)
        out.append((o, w))
        o += w
    return out


def emit_preload(b, func, deps=()):
    return b.op("scalar", lambda s: s.preload_act_table(func), deps=deps)


def get_const(b, g, val):
    d = g.setdefault("_consts", {})
    val = float(val)
    if val not in d:
        idx = len(d)
        ap = g["cbuf"][:, idx:idx + 1]
        tok = b.op("vector", lambda v, ap=ap: v.memset(ap, val))
        d[val] = (ap, tok)
    return d[val]



FFN_SECS = [(0, 6), (6, 6), (12, 5), (17, 5)]
FFN_SEC_OFF = [0, 6144, 12288, 17408]
FFN_SEC_OF_R = {}
for _si, (_c0, _nch) in enumerate(FFN_SECS):
    for _rr in range(_c0, _c0 + _nch):
        FFN_SEC_OF_R[_rr] = (_si, _rr - _c0)


def alloc_ffn_buffers(nc, g):
    g["X"] = nc.alloc_sbuf_tensor("Xsb", [128, NT * D], F32)
    g["xqT"] = nc.alloc_sbuf_tensor("xqT", [128, NT * D], BF16)
    g["acc"] = nc.alloc_sbuf_tensor("ffn_acc", [128, 8 * D], F32)
    g["wg_sb"] = nc.alloc_sbuf_tensor("wg_sb", [128, 8 * 768], BF16)
    g["wu_sb"] = nc.alloc_sbuf_tensor("wu_sb", [128, 8 * 768], BF16)
    g["wd_sb"] = nc.alloc_sbuf_tensor("wd_sb", [128, 6 * 1024], BF16)
    g["sg_sb"] = nc.alloc_sbuf_tensor("sg_sb", [128, 768], F32)
    g["h_sb2"] = nc.alloc_sbuf_tensor("h_sb2", [128, 2 * 768], F32)
    g["hq"] = nc.alloc_sbuf_tensor("hq", [128, 768], BF16)
    g["hqT"] = nc.alloc_sbuf_tensor("hqT", [128, 768], BF16)
    g["xn_bf"] = nc.alloc_sbuf_tensor("xn_bf", [128, D], BF16)
    g["xq_bf"] = nc.alloc_sbuf_tensor("xq_bf", [128, D], BF16)
    KQ = 1408
    g["qu"] = {k: nc.alloc_sbuf_tensor("qu_" + k, [128, KQ], BF16)
               for k in ("x2", "a2", "m0", "m1", "lvl", "sg", "is2e", "se")}
    g["qst"] = {k: nc.alloc_sbuf_tensor("qst_" + k, [128, KQ // 16], F32)
                for k in ("bm2", "L", "is2", "s")}
    g["lnst"] = {k: nc.alloc_sbuf_tensor("ln_" + k, [128, NT], F32)
                 for k in ("s1", "s2", "m", "v", "rstd", "mr")}
    g["cbuf"] = nc.alloc_sbuf_tensor("cbuf", [128, 16], F32)
    g["junk"] = g["qu"]["x2"]
    g["wq_stage"] = g["acc"][:, 0:I_FFN]
    g["wq_q"] = g["acc"][:, 2816:3520].bitcast(BF16)
    g["wq_tr"] = g["acc"][:, 3520:4224].bitcast(BF16)


def emit_ffn_weight_quant(b, g):
    eye = g["eye_sb"][:]
    done = []
    prev_tq = None
    prev_tr = None
    prev_store = None
    for src_ap, nrows, K, scratch in (
            (g["wg_d"], I_FFN, D, g["wgT_q"]),
            (g["wu_d"], I_FFN, D, g["wuT_q"]),
            (g["wd_d"], D, I_FFN, g["wdT_q"])):
        ntile = nrows // 128
        outdim = nrows
        for r in range(ntile):
            dtile = b.dma("sync", out=g["wq_stage"][:, 0:K],
                          in_=src_ap[r * 128:(r + 1) * 128, :],
                          deps=[prev_tq])
            for (o, w) in strips_of(K, mx=1408):
                srcs = g["wq_stage"][:, o:o + w]
                tq = emit_quant(b, g, g["qu"], g["qst"], srcs, g["wq_q"][:, 0:w],
                                w, [dtile, prev_tq, prev_tr])
                prev_tq = tq
                ttr = emit_transpose(b, g["pT"], g["wq_q"][:, 0:w],
                                     g["wq_tr"][:, 0:w], eye, w // 128,
                                     [tq, prev_store], psum_free=prev_tr)
                prev_tr = ttr
                if K == D:
                    si, fr = FFN_SEC_OF_R[r]
                    i_s = 128 * FFN_SECS[si][1]
                    dst = scratch.ap()[:, FFN_SEC_OFF[si]:
                                       FFN_SEC_OFF[si] + 8 * i_s].rearrange(
                        "p (cc f) -> p cc f", f=i_s)[
                        :, :, fr * 128:(fr + 1) * 128]
                else:
                    dst = scratch.ap().rearrange("p (cc f) -> p cc f",
                                                 f=1024)[
                        :, o // 128:(o + w) // 128, r * 128:(r + 1) * 128]
                prev_store = b.dma("sync", out=dst,
                                   in_=g["wq_tr"][:, 0:w].rearrange(
                                       "p (cc f) -> p cc f", f=128),
                                   deps=[ttr])
        done.append(prev_store)
    return done, prev_tq, prev_tr


def emit_ffn_phaseA(b, g, prev_tq, prev_tr):
    st = g["lnst"]
    stat_toks = []
    s2_toks = []
    for i in range(NT):
        Xi = g["X"][:, i * D:(i + 1) * D]
        t1, t2 = emit_ln_stats(b, Xi, st["s1"][:, i:i + 1],
                               st["s2"][:, i:i + 1], g["junk"][:, 0:D],
                               deps=[prev_tq] if i == 0 else ())
        stat_toks += [t1, t2]
        s2_toks.append(t2)
    tmr, trs = emit_ln_batch(b, g, st, NT, stat_toks)
    out_toks = []
    for i in range(NT):
        Xi = g["X"][:, i * D:(i + 1) * D]
        tap = emit_ln_apply(b, Xi, g["xn_bf"][:], st["rstd"][:, i:i + 1],
                            st["mr"][:, i:i + 1], [tmr, trs, prev_tq])
        tq = emit_quant(b, g, g["qu"], g["qst"], g["xn_bf"][:], g["xq_bf"][:],
                        D, [tap, prev_tr])
        prev_tq = tq
        ttr = emit_transpose(b, g["pT"], g["xq_bf"][:],
                             g["xqT"][:, i * D:(i + 1) * D], g["eye_sb"][:],
                             8, [tq], psum_free=prev_tr)
        prev_tr = ttr
        out_toks.append(ttr)
    return out_toks, s2_toks, prev_tq, prev_tr


def emit_ffn_phaseB(b, g, xqT_toks, s2_toks, wq_done, prev_tq, prev_tr, taps=None,
                    nhalves=2, nsecs=4, ntiles=8):
    res_toks = []
    acc_reader = {}
    acc_writer = {}
    st_ = {"prev_pE_evac": None, "prev_silu": None, "prev_h": None,
           "prev_w_last_use": None, "prev_tq": prev_tq, "prev_tr": prev_tr}
    h_free = {0: None, 1: None}
    for half in range(nhalves):
        for si, (c0, nch) in enumerate(FFN_SECS[:nsecs]):
            i_s = nch * 128
            wdeps = list(wq_done) + [st_["prev_w_last_use"]]
            tg = b.dma("sync", out=g["wg_sb"][:, 0:8 * i_s],
                       in_=g["wgT_q"].ap()[:, FFN_SEC_OFF[si]:
                                           FFN_SEC_OFF[si] + 8 * i_s],
                       deps=wdeps)
            tu = b.dma("sync", out=g["wu_sb"][:, 0:8 * i_s],
                       in_=g["wuT_q"].ap()[:, FFN_SEC_OFF[si]:
                                           FFN_SEC_OFF[si] + 8 * i_s],
                       deps=wdeps)
            td = b.dma("scalar", out=g["wd_sb"][:, 0:nch * 1024],
                       in_=g["wdT_q"].ap()[:, c0 * 1024:(c0 + nch) * 1024],
                       deps=wdeps)

            def stageA(tj, half=half, si=si, c0=c0, nch=nch, i_s=i_s,
                       tg=tg, tu=tu):
                tile = half * 8 + tj
                slot = tj % 2
                hbuf = g["h_sb2"][:, slot * 768:slot * 768 + i_s]
                lhs = [g["xqT"][:, tile * D + c * 128:
                                tile * D + (c + 1) * 128] for c in range(8)]
                xdep = xqT_toks[tile]
                h_toks = []
                for (o, w) in strips_of(i_s):
                    mm = None
                    for c in range(8):
                        mm = b.op("tensor", lambda t, oo=g["pA"][:, 0:w],
                                  ll=lhs[c],
                                  rr=g["wg_sb"][:, c * i_s + o:
                                                c * i_s + o + w],
                                  st=(c == 0), sp=(c == 7): t.matmul(
                            out=oo, lhsT=ll, rhs=rr, start=st, stop=sp),
                            deps=[tg, xdep, st_["prev_silu"]]
                            if c == 0 else ())
                    st_["prev_silu"] = b.op("scalar", lambda s,
                                            oo=g["sg_sb"][:, o:o + w],
                                            ii=g["pA"][:, 0:w]: s.activation(
                        out=oo, in_=ii, func=AF.Silu),
                        deps=[mm, st_["prev_tq"]])
                    mm = None
                    for c in range(8):
                        mm = b.op("tensor", lambda t, oo=g["pB"][:, 0:w],
                                  ll=lhs[c],
                                  rr=g["wu_sb"][:, c * i_s + o:
                                                c * i_s + o + w],
                                  st=(c == 0), sp=(c == 7): t.matmul(
                            out=oo, lhsT=ll, rhs=rr, start=st, stop=sp),
                            deps=[tu, xdep, st_["prev_h"]] if c == 0 else ())
                    st_["prev_h"] = b.op("vector", lambda v,
                                         oo=hbuf[:, o:o + w],
                                         i0=g["pB"][:, 0:w],
                                         i1=g["sg_sb"][:, o:o + w]:
                                         v.tensor_mul(out=oo, in0=i0, in1=i1),
                        deps=[mm, st_["prev_silu"], h_free[slot]])
                    h_toks.append(st_["prev_h"])
                return h_toks

            def stageB(tj, h_toks, half=half, si=si, c0=c0, nch=nch,
                       i_s=i_s, td=td):
                tile = half * 8 + tj
                slot = tj % 2
                hbuf = g["h_sb2"][:, slot * 768:slot * 768 + i_s]
                if taps is not None and tile == 0 and si == 0:
                    taps.append(b.dma("gpsimd", out=g["h_tap"],
                                      in_=hbuf, deps=h_toks))
                    taps.append(b.dma("gpsimd", out=g["sg_tap"],
                                      in_=g["sg_sb"][:, 0:i_s], deps=h_toks))
                    h_toks = h_toks + [taps[-1], taps[-2]]
                tqh = emit_quant(b, g, g["qu"], g["qst"], hbuf,
                                 g["hq"][:, 0:i_s], i_s,
                                 h_toks + [st_["prev_tq"], st_["prev_tr"]])
                st_["prev_tq"] = tqh
                h_free[slot] = tqh
                if taps is not None and tile == 0 and si == 0:
                    taps.append(b.dma("gpsimd", out=g["hq_tap"],
                                      in_=g["hq"][:, 0:i_s], deps=[tqh]))
                thT = emit_transpose(b, g["pT"], g["hq"][:, 0:i_s],
                                     g["hqT"][:, 0:i_s], g["eye_sb"][:],
                                     nch, [tqh, st_["prev_pE_evac"]],
                                     psum_free=st_["prev_tr"])
                st_["prev_tr"] = thT
                mm_last = None
                for (o, w) in ((0, 512), (512, 512)):
                    for c in range(nch):
                        mm_last = b.op("tensor", lambda t,
                                       oo=g["pE"][:, o:o + w],
                                       ll=g["hqT"][:, c * 128:(c + 1) * 128],
                                       rr=g["wd_sb"][:, c * 1024 + o:
                                                     c * 1024 + o + w],
                                       st=(c == 0), sp=(c == nch - 1):
                                       t.matmul(out=oo, lhsT=ll, rhs=rr,
                                                start=st, stop=sp),
                            deps=[thT, td, st_["prev_pE_evac"]]
                            if (c == 0 and o == 0) else ())
                st_["prev_w_last_use"] = mm_last
                accv = g["acc"][:, tj * D:(tj + 1) * D]
                edeps = [mm_last, acc_writer.get(tile), acc_reader.get(tj)]
                if si == 0:
                    tev = b.op("scalar", lambda s, accv=accv: s.activation(
                        out=accv, in_=g["pE"][:], func=AF.Identity,
                        scale=0.5), deps=edeps)
                else:
                    tev = b.op("vector",
                               lambda v, accv=accv: v.scalar_tensor_tensor(
                                   out=accv, in0=g["pE"][:], scalar=0.5,
                                   in1=accv, op0=ALU.mult, op1=ALU.add),
                               deps=edeps)
                acc_writer[tile] = tev
                st_["prev_pE_evac"] = tev
                if taps is not None and tile == 0:
                    taps.append(b.dma("gpsimd", out=g["acc_tap"][si],
                                      in_=accv, deps=[tev]))
                    st_["prev_pE_evac"] = taps[-1]
                    acc_writer[tile] = taps[-1]
                if si == nsecs - 1:
                    Xi = g["X"][:, tile * D:(tile + 1) * D]
                    tres = b.op("vector",
                                lambda v, Xi=Xi, accv=accv: v.tensor_add(
                                    out=Xi, in0=Xi, in1=accv),
                                deps=[tev, s2_toks[tile]])
                    acc_reader[tj] = tres
                    res_toks.append(tres)

            pend = stageA(0)
            for tj in range(ntiles):
                nxt = stageA(tj + 1) if tj + 1 < ntiles else None
                stageB(tj, pend)
                pend = nxt
    return res_toks, st_["prev_tq"], st_["prev_tr"]


def emit_ffn0(b, g, quantize=True, wq_state=None):
    if quantize:
        wq_done, ptq, ptr = emit_ffn_weight_quant(b, g)
    else:
        wq_done, ptq, ptr = wq_state
    xqT_toks, s2_toks, ptq, ptr = emit_ffn_phaseA(b, g, ptq, ptr)
    import os
    res, ptq, ptr = emit_ffn_phaseB(b, g, xqT_toks, s2_toks, wq_done, ptq, ptr,
                                    taps=g.get("_taps"),
                                    nhalves=int(os.environ.get("FFN_NH", "2")),
                                    nsecs=int(os.environ.get("FFN_NS", "4")),
                                    ntiles=int(os.environ.get("FFN_NT", "8")))
    return res, (wq_done, ptq, ptr)



def build_ffn0_program(debug=False, wq_only=False, export_wq=False,
                       prequant=False):
    nc = bass.Bass("TRN2", target_bir_lowering=False)
    b = Bld(nc)
    g = {"nc": nc}
    alloc_ffn_buffers(nc, g)
    g["eye_sb"] = nc.alloc_sbuf_tensor("eye_sb", [128, 128], BF16)
    x_in = nc.dram_tensor("x", [T, D], F32, kind="ExternalInput").ap()
    y_out = nc.dram_tensor("y", [T, D], F32, kind="ExternalOutput").ap()
    if not prequant:
        g["wg_d"] = nc.dram_tensor("w_gate", [I_FFN, D], F32,
                                   kind="ExternalInput").ap()
        g["wu_d"] = nc.dram_tensor("w_up", [I_FFN, D], F32,
                                   kind="ExternalInput").ap()
        g["wd_d"] = nc.dram_tensor("w_down", [D, I_FFN], F32,
                                   kind="ExternalInput").ap()
    eye_d = nc.dram_tensor("eye", [128, 128], BF16, kind="ExternalInput").ap()
    if prequant:
        kind = "ExternalInput"
    else:
        kind = "ExternalOutput" if (debug or export_wq) else "Internal"
    g["wgT_q"] = nc.dram_tensor("wgT_q", [128, 22528], BF16, kind=kind)
    g["wuT_q"] = nc.dram_tensor("wuT_q", [128, 22528], BF16, kind=kind)
    g["wdT_q"] = nc.dram_tensor("wdT_q", [128, 22528], BF16, kind=kind)
    if debug:
        g["xqT_dump"] = nc.dram_tensor("xqT_dump", [128, NT * D], BF16,
                                       kind="ExternalOutput").ap()
        g["xln_tap_d"] = nc.dram_tensor("xln_tap", [128, D], BF16,
                                        kind="ExternalOutput").ap() if False else None
        g["acc_tap"] = [nc.dram_tensor("acc_tap%d" % i, [128, D], F32,
                                       kind="ExternalOutput").ap() for i in range(4)]
        g["h_tap"] = nc.dram_tensor("h_tap", [128, 768], F32, kind="ExternalOutput").ap()
        g["sg_tap"] = nc.dram_tensor("sg_tap", [128, 768], F32, kind="ExternalOutput").ap()
        g["hq_tap"] = nc.dram_tensor("hq_tap", [128, 768], BF16, kind="ExternalOutput").ap()
        g["_taps"] = []

    with (
        nc.psum_tensor("pA", [128, 512], F32) as pA,
        nc.psum_tensor("pB", [128, 512], F32) as pB,
        nc.psum_tensor("pE", [128, 1024], F32) as pE,
        nc.psum_tensor("pT", [128, 1024], BF16) as pT,
    ):
        g["pA"], g["pB"], g["pE"], g["pT"] = pA, pB, pE, pT
        teye = b.dma("gpsimd", out=g["eye_sb"][:], in_=eye_d)
        tx_toks = []
        for hh in range(2):
            tx_toks.append(b.dma("gpsimd",
                           out=g["X"][:, hh * 8 * D:(hh + 1) * 8 * D].rearrange(
                               "p (i f) -> p i f", f=D),
                           in_=x_in.rearrange("(i p) f -> p i f", p=128)
                           [:, hh * 8:(hh + 1) * 8, :]))
        tx = tx_toks
        for e in ("vector", "scalar", "tensor"):
            b.wait(e, [teye] + tx)
        if wq_only:
            wq_done, ptq, ptr = emit_ffn_weight_quant(b, g)
            res = wq_done
        elif prequant:
            res, _ = emit_ffn0(b, g, quantize=False,
                               wq_state=([], None, None))
        else:
            res, _ = emit_ffn0(b, g, quantize=True)
        if debug:
            res = list(res)
            res.append(b.dma("gpsimd", out=g["xqT_dump"], in_=g["xqT"][:],
                             deps=list(res)))
        tst = []
        for hh in range(2):
            tst.append(b.dma("gpsimd",
                       out=y_out.rearrange("(i p) f -> p i f", p=128)
                       [:, hh * 8:(hh + 1) * 8, :],
                       in_=g["X"][:, hh * 8 * D:(hh + 1) * 8 * D].rearrange(
                           "p (i f) -> p i f", f=D),
                       deps=res))
        b.wait("gpsimd", tst)
        nc_built = b.build()
    return nc_built





def alloc_attn0_buffers(nc, g):
    g["q_tm"] = g["junk"]
    g["xln"] = nc.alloc_sbuf_tensor("xln", [128, NT * D], BF16)
    g["xlnT"] = nc.alloc_sbuf_tensor("xlnT", [128, 8 * T], BF16)
    g["kT"] = nc.alloc_sbuf_tensor("kT", [128, 8 * T], BF16)
    g["qTi"] = nc.alloc_sbuf_tensor("qTi", [128, 8 * 128], BF16)
    g["WqT_bf"] = nc.alloc_sbuf_tensor("WqT_bf", [128, 8 * 1024], BF16)
    g["Wstage"] = nc.alloc_sbuf_tensor("Wstage", [128, 1024], F32)
    g["Wk_bf"] = nc.alloc_sbuf_tensor("Wk_bf", [128, 2 * 1024], BF16)
    g["S_row"] = nc.alloc_sbuf_tensor("S_row", [128, T], BF16)
    g["P_bf"] = g["Wstage"][:].bitcast(BF16)
    g["PT_sb"] = nc.alloc_sbuf_tensor("PT_sb", [128, 256], BF16)
    g["s_bc"] = nc.alloc_sbuf_tensor("s_bc", [128, D], F32)
    g["o_sb"] = g["Wk_bf"][:].bitcast(F32)
    g["att_st"] = {k: nc.alloc_sbuf_tensor("att_" + k, [128, NT], F32)
                   for k in ("m4a", "m1", "bm", "l", "rl")}
    g["srow_s"] = nc.alloc_sbuf_tensor("srow_s", [1, 1024], F32)
    g["srow_bf"] = nc.alloc_sbuf_tensor("srow_bf", [1, 1024], BF16)
    g["bq_row_bf"] = nc.alloc_sbuf_tensor("bq_row_bf", [1, 1024], BF16)
    g["bk_cols_sb"] = nc.alloc_sbuf_tensor("bk_cols_sb", [128, 8], F32)
    g["ones_row"] = nc.alloc_sbuf_tensor("ones_row", [1, 128], BF16)
    g["ones_col"] = nc.alloc_sbuf_tensor("ones_col", [128, 1], BF16)


def emit_attn0(b, g, tin):
    st = g["lnst"]
    stat_toks = []
    s2_toks = []
    for i in range(NT):
        Xi = g["X"][:, i * D:(i + 1) * D]
        t1, t2 = emit_ln_stats(b, Xi, st["s1"][:, i:i + 1], st["s2"][:, i:i + 1],
                               g["junk"][:, 0:D], deps=tin if i == 0 else ())
        stat_toks += [t1, t2]
        s2_toks.append(t2)
    tmr, trs = emit_ln_batch(b, g, st, NT, stat_toks)
    prev_tr = g.get("attn_ptr")
    xln_toks = []
    xlnT_toks = []
    cs_last = None
    for i in range(NT):
        Xi = g["X"][:, i * D:(i + 1) * D]
        xln_i = g["xln"][:, i * D:(i + 1) * D]
        tap = emit_ln_apply(b, Xi, xln_i, st["rstd"][:, i:i + 1],
                            st["mr"][:, i:i + 1], [tmr, trs])
        xln_toks.append(tap)
        d = [tap]
        if prev_tr is not None:
            d.append(prev_tr)
        pe_last = None
        for c in range(8):
            pe_last = b.op("tensor", lambda t,
                           oo=g["pT"][:, c * 128:(c + 1) * 128],
                           ii=xln_i[:, c * 128:(c + 1) * 128],
                           ee=g["eye_sb"][:]: t.transpose(out=oo, in_=ii,
                                                          identity=ee),
                           deps=d if c == 0 else ())
        tc = b.op("vector", lambda v, i=i: v.tensor_copy(
            out=g["xlnT"][:].rearrange("p (c t) -> p c t", t=T)
            [:, :, i * 128:(i + 1) * 128],
            in_=g["pT"][:].rearrange("p (c f) -> p c f", f=128)), deps=[pe_last])
        prev_tr = tc
        xlnT_toks.append(tc)
        for (o, w) in ((0, 512), (512, 512)):
            cs_last = b.op("tensor", lambda t, oo=g["pF"][:, o:o + w],
                           ll=g["ones_col"][:], rr=xln_i[:, o:o + w],
                           st_=(i == 0), sp=(i == NT - 1): t.matmul(
                out=oo, lhsT=ll, rhs=rr, start=st_, stop=sp),
                deps=[tap] if (o == 0 and i == 0) else ())
    g["attn_ptr"] = prev_tr
    tse = b.op("vector", lambda v: v.tensor_scalar(
        out=g["srow_s"][:], in0=g["pF"][:, 0:1024], scalar1=-2.0 / T,
        scalar2=None, op0=ALU.mult), deps=[cs_last])
    tsb = b.op("vector", lambda v: v.tensor_copy(out=g["srow_bf"][:],
                                                 in_=g["srow_s"][:]),
               deps=[tse])
    sbc_toks = []
    for o in range(0, 1024, 256):
        tmmb = b.op("tensor", lambda t, oo=g["pS"][:, 0:256],
                    ll=g["ones_row"][:], rr=g["srow_bf"][:, o:o + 256]: t.matmul(
            out=oo, lhsT=ll, rhs=rr, start=True, stop=True),
            deps=[tsb, tse] + sbc_toks[-1:])
        sbc_toks.append(b.op("vector", lambda v, o=o: v.tensor_copy(
            out=g["s_bc"][:, o:o + 256], in_=g["pS"][:, 0:256]), deps=[tmmb]))
    kT_toks = []
    prev_kevac = sbc_toks[-1]
    wk_prev_use = g["tWq"]
    for ncb in range(8):
        dW = b.dma("sync", out=g["Wstage"][:, 0:1024].rearrange(
            "p (c f) -> p c f", f=128),
            in_=g["WkT_d"].rearrange("(c p) n -> p c n", p=128)
            [:, :, ncb * 128:(ncb + 1) * 128], deps=[wk_prev_use])
        slot = (ncb % 2) * 1024
        tcast = b.op("vector", lambda v, slot=slot: v.tensor_copy(
            out=g["Wk_bf"][:, slot:slot + 1024], in_=g["Wstage"][:, 0:1024]),
            deps=[dW, wk_prev_use])
        for js in range(4):
            mm = None
            for dc in range(8):
                mm = b.op("tensor", lambda t, oo=g["pS"][:, 0:512],
                          ll=g["Wk_bf"][:, slot + dc * 128: slot + (dc + 1) * 128],
                          rr=g["xlnT"][:].rearrange("p (c t) -> p c t", t=T)
                          [:, dc, js * 512:(js + 1) * 512],
                          st_=(dc == 0), sp=(dc == 7): t.matmul(
                    out=oo, lhsT=ll, rhs=rr, start=st_, stop=sp),
                    deps=[tcast, xlnT_toks[-1], prev_kevac]
                    if dc == 0 else ())
            prev_kevac = b.op("scalar", lambda s, ncb=ncb, js=js: s.activation(
                out=g["kT"][:].rearrange("p (c t) -> p c t", t=T)
                [:, ncb, js * 512:(js + 1) * 512],
                in_=g["pS"][:, 0:512], func=AF.Identity,
                bias=g["bk_cols_sb"][:, ncb:ncb + 1]), deps=[mm])
            kT_toks.append(prev_kevac)
        wk_prev_use = mm
    res_toks = []
    ast = g["att_st"]
    prev_oevac = None
    prev_sevac = None
    prev_qevac = None
    prev_smm_last = None
    prev_pv_tcopy = None
    prev_tqT = None
    import os as _os
    nt_c = int(_os.environ.get("ATTN_NT", str(NT)))
    for i in range(nt_c):
        xln_i = g["xln"][:, i * D:(i + 1) * D]
        qev = []
        for (o, w) in ((0, 512), (512, 512)):
            mm = None
            for dc in range(8):
                mm = b.op("tensor", lambda t, oo=g["pA"][:, 0:w],
                          ll=g["xlnT"][:].rearrange("p (c t) -> p c t", t=T)
                          [:, dc, i * 128:(i + 1) * 128],
                          rr=g["WqT_bf"][:].rearrange("p (c n) -> p c n", n=1024)
                          [:, dc, o:o + w],
                          st_=(dc == 0), sp=False: t.matmul(
                    out=oo, lhsT=ll, rhs=rr, start=st_, stop=sp),
                    deps=[g["tWq"], xlnT_toks[i], prev_qevac, prev_tqT]
                    if dc == 0 else ())
            mm = b.op("tensor", lambda t, oo=g["pA"][:, 0:256],
                      ll=g["ones_row"][:], rr=g["bq_row_bf"][:, o:o + 256]:
                      t.matmul(out=oo, lhsT=ll, rhs=rr, start=False, stop=False),
                      deps=[g["tbq"]])
            mm = b.op("tensor", lambda t, oo=g["pA"][:, 256:512],
                      ll=g["ones_row"][:], rr=g["bq_row_bf"][:, o + 256:o + 512]:
                      t.matmul(out=oo, lhsT=ll, rhs=rr, start=False, stop=True),
                      deps=())
            prev_qevac = b.op("scalar", lambda s, o=o, w=w: s.activation(
                out=g["q_tm"][:, o:o + w], in_=g["pA"][:, 0:w],
                func=AF.Identity), deps=[mm])
            qev.append(prev_qevac)
        tqT = emit_transpose(b, g["pT"], g["q_tm"][:], g["qTi"][:],
                             g["eye_sb"][:], 8, qev + [prev_smm_last],
                             psum_free=g["attn_ptr"])
        g["attn_ptr"] = tqT
        prev_tqT = tqT
        sm_toks = []
        for js in range(4):
            mm = None
            for dc in range(8):
                mm = b.op("tensor", lambda t, oo=g["pS"][:, 0:512],
                          ll=g["qTi"][:, dc * 128:(dc + 1) * 128],
                          rr=g["kT"][:].rearrange("p (c t) -> p c t", t=T)
                          [:, dc, js * 512:(js + 1) * 512],
                          st_=(dc == 0), sp=(dc == 7): t.matmul(
                    out=oo, lhsT=ll, rhs=rr, start=st_, stop=sp),
                    deps=[tqT, kT_toks[-1], prev_sevac] if dc == 0 else ())
            prev_smm_last = mm
            prev_sevac = b.op("scalar", lambda s, js=js: s.activation(
                out=g["S_row"][:, js * 512:(js + 1) * 512],
                in_=g["pS"][:, 0:512], func=AF.Identity), deps=[mm])
            sm_toks.append(b.op("vector", lambda v, js=js, i=i: v.reduce_max(
                out=ast["m4a"][:, js:js + 1]
                if False else g["m4buf"][:, js:js + 1],
                in_=g["S_row"][:, js * 512:(js + 1) * 512], axis=AX.X),
                deps=[prev_sevac]))
        tm1 = b.op("vector", lambda v, i=i: v.reduce_max(
            out=ast["m1"][:, i:i + 1], in_=g["m4buf"][:, 0:4], axis=AX.X),
            deps=sm_toks)
        tbm = b.op("vector", lambda v, i=i: v.tensor_scalar(
            out=ast["bm"][:, i:i + 1], in0=ast["m1"][:, i:i + 1],
            scalar1=-1.0 / 32.0, scalar2=None, op0=ALU.mult), deps=[tm1])
        tP = b.op("scalar", lambda s, i=i: s.activation(
            out=g["P_bf"][:], in_=g["S_row"][:], func=AF.Exp, scale=1.0 / 32.0,
            bias=ast["bm"][:, i:i + 1]),
            deps=[tbm, prev_sevac, prev_pv_tcopy])
        tl = b.op("vector", lambda v, i=i: v.reduce_sum(
            out=ast["l"][:, i:i + 1], in_=g["P_bf"][:], axis=AX.X), deps=[tP])
        trl = b.op("vector", lambda v, i=i: v.reciprocal(
            out=ast["rl"][:, i:i + 1], in_=ast["l"][:, i:i + 1]), deps=[tl])
        mmv = None
        tcopy = [None, None]
        mm_slot = [None, None]
        for jb in range(NT):
            slot = jb % 2
            dtr = [tP]
            if tcopy[slot] is not None:
                dtr.append(tcopy[slot])
            if prev_oevac is not None:
                dtr.append(prev_oevac)
            pe_t = b.op("tensor", lambda t,
                        oo=g["pT"][:, slot * 128:(slot + 1) * 128],
                        ii=g["P_bf"][:, jb * 128:(jb + 1) * 128],
                        ee=g["eye_sb"][:]: t.transpose(out=oo, in_=ii,
                                                       identity=ee),
                        deps=dtr)
            tcopy[slot] = b.op("vector", lambda v, slot=slot: v.tensor_copy(
                out=g["PT_sb"][:, slot * 128:(slot + 1) * 128],
                in_=g["pT"][:, slot * 128:(slot + 1) * 128]),
                deps=[pe_t, mm_slot[slot]])
            for (o, w) in ((0, 512), (512, 512)):
                mmv = b.op("tensor", lambda t, oo=g["pE"][:, o:o + w],
                           ll=g["PT_sb"][:, slot * 128:(slot + 1) * 128],
                           rr=g["xln"][:, jb * D + o: jb * D + o + w],
                           st_=(jb == 0), sp=(jb == NT - 1): t.matmul(
                    out=oo, lhsT=ll, rhs=rr, start=st_, stop=sp),
                    deps=[tcopy[slot], xln_toks[jb]] if o == 0 else ())
            mm_slot[slot] = mmv
        prev_pv_tcopy = tcopy[(NT - 1) % 2]
        g["attn_ptr"] = prev_pv_tcopy
        tev = b.op("vector", lambda v, i=i: v.tensor_scalar(
            out=g["o_sb"][:], in0=g["pE"][:], scalar1=ast["rl"][:, i:i + 1],
            scalar2=None, op0=ALU.mult), deps=[mmv, trl])
        prev_oevac = tev
        Xi = g["X"][:, i * D:(i + 1) * D]
        nres = int(_os.environ.get("ATTN_RES", "3"))
        tr = b.op("vector", lambda v, Xi=Xi: v.tensor_add(
            out=Xi, in0=Xi, in1=g["o_sb"][:]), deps=[tev, s2_toks[i]])
        if nres >= 2:
            alt = _os.environ.get("ATTN_ALT")
            if alt == "o2":
                tr = b.op("vector", lambda v, Xi=Xi: v.tensor_add(
                    out=Xi, in0=Xi, in1=g["o_sb"][:]), deps=[tr])
            elif alt == "tapxln":
                tp = b.dma("gpsimd", out=g["xln_tap_d"], in_=xln_i, deps=[tr])
                tr = b.op("vector", lambda v, Xi=Xi, xln_i=xln_i: v.tensor_add(
                    out=Xi, in0=Xi, in1=xln_i), deps=[tr, tp])
            else:
                tr = b.op("vector", lambda v, Xi=Xi, xln_i=xln_i: v.tensor_add(
                    out=Xi, in0=Xi, in1=xln_i), deps=[tr])
        if nres >= 3:
            tr = b.op("vector", lambda v, Xi=Xi: v.tensor_add(
                out=Xi, in0=Xi, in1=g["s_bc"][:]), deps=[tr] + sbc_toks)
        res_toks.append(tr)
    return res_toks


def build_attn0_program(debug=False):
    nc = bass.Bass("TRN2", target_bir_lowering=False)
    b = Bld(nc)
    g = {"nc": nc}
    g["X"] = nc.alloc_sbuf_tensor("Xsb", [128, NT * D], F32)
    g["lnst"] = {k: nc.alloc_sbuf_tensor("ln_" + k, [128, NT], F32)
                 for k in ("s1", "s2", "m", "v", "rstd", "mr")}
    g["junk"] = nc.alloc_sbuf_tensor("junk", [128, D], BF16)
    g["cbuf"] = nc.alloc_sbuf_tensor("cbuf", [128, 16], F32)
    g["m4buf"] = nc.alloc_sbuf_tensor("m4buf", [128, 4], F32)
    alloc_attn0_buffers(nc, g)
    g["eye_sb"] = nc.alloc_sbuf_tensor("eye_sb", [128, 128], BF16)
    x_in = nc.dram_tensor("x", [T, D], F32, kind="ExternalInput").ap()
    y_out = nc.dram_tensor("y", [T, D], F32, kind="ExternalOutput").ap()
    g["WqT_d"] = nc.dram_tensor("WqT", [D, D], F32, kind="ExternalInput").ap()
    g["WkT_d"] = nc.dram_tensor("WkT", [D, D], F32, kind="ExternalInput").ap()
    bqr = nc.dram_tensor("bq_row", [1, D], F32, kind="ExternalInput").ap()
    bkc = nc.dram_tensor("bk_cols", [128, 8], F32, kind="ExternalInput").ap()
    eye_d = nc.dram_tensor("eye", [128, 128], BF16, kind="ExternalInput").ap()
    onr = nc.dram_tensor("ones_row_d", [1, 128], BF16, kind="ExternalInput").ap()
    onc_d = nc.dram_tensor("ones_col_d", [128, 1], BF16, kind="ExternalInput").ap()

    dbg_outs = {}
    if debug:
        g_xln_tap = nc.dram_tensor("xln_tap", [128, D], BF16, kind="ExternalOutput").ap()
        for nm, sh, dt in (("d_xln", [128, NT * D], BF16), ("d_kT", [128, 8 * T], BF16),
                           ("d_S", [128, T], F32), ("d_P", [128, T], BF16),
                           ("d_l", [128, NT], F32), ("d_rl", [128, NT], F32),
                           ("d_sbc", [128, D], F32), ("d_o", [128, D], F32),
                           ("d_qTi", [128, 1024], BF16), ("d_xlnT", [128, 8 * T], BF16)):
            dbg_outs[nm] = nc.dram_tensor(nm, sh, dt, kind="ExternalOutput").ap()
    with (
        nc.psum_tensor("pA", [128, 512], F32) as pA,
        nc.psum_tensor("pS", [128, 512], F32) as pS,
        nc.psum_tensor("pE", [128, 1024], F32) as pE,
        nc.psum_tensor("pT", [128, 1024], BF16) as pT,
        nc.psum_tensor("pF", [1, 1024], F32) as pF,
    ):
        g["pA"], g["pS"], g["pE"], g["pT"], g["pF"] = pA, pS, pE, pT, pF
        if debug:
            g["xln_tap_d"] = g_xln_tap
        teye = b.dma("gpsimd", out=g["eye_sb"][:], in_=eye_d)
        tonr = b.dma("gpsimd", out=g["ones_row"][:], in_=onr)
        tonc = b.dma("gpsimd", out=g["ones_col"][:], in_=onc_d)
        tbkc = b.dma("gpsimd", out=g["bk_cols_sb"][:], in_=bkc)
        tx_toks = []
        for hh in range(2):
            tx_toks.append(b.dma("gpsimd",
                           out=g["X"][:, hh * 8 * D:(hh + 1) * 8 * D].rearrange(
                               "p (i f) -> p i f", f=D),
                           in_=x_in.rearrange("(i p) f -> p i f", p=128)
                           [:, hh * 8:(hh + 1) * 8, :]))
        tx = tx_toks
        tWq_parts = []
        prev_cast = None
        for c in range(8):
            dW = b.dma("sync", out=g["Wstage"][:, 0:1024],
                       in_=g["WqT_d"][c * 128:(c + 1) * 128, :],
                       deps=[prev_cast])
            prev_cast = b.op("vector", lambda v, c=c: v.tensor_copy(
                out=g["WqT_bf"][:].rearrange("p (c n) -> p c n", n=1024)[:, c, :],
                in_=g["Wstage"][:, 0:1024]), deps=[dW])
            tWq_parts.append(prev_cast)
        g["tWq"] = tWq_parts[-1]
        dbq = b.dma("sync", out=g["srow_s"][:], in_=bqr)
        g["tbq"] = b.op("vector", lambda v: v.tensor_copy(
            out=g["bq_row_bf"][:], in_=g["srow_s"][:]), deps=[dbq])
        for e in ("vector", "scalar", "tensor"):
            b.wait(e, [teye, tonr, tonc, tbkc] + tx)
        res = emit_attn0(b, g, tx + [g["tbq"]])
        if debug:
            for nm, src in (("d_xln", g["xln"][:]), ("d_kT", g["kT"][:]),
                            ("d_S", g["S_row"][:]), ("d_P", g["P_bf"][:]),
                            ("d_l", g["att_st"]["l"][:]), ("d_rl", g["att_st"]["rl"][:]),
                            ("d_sbc", g["s_bc"][:]), ("d_o", g["o_sb"][:]),
                            ("d_qTi", g["qTi"][:]), ("d_xlnT", g["xlnT"][:])):
                res = res + [b.dma("scalar", out=dbg_outs[nm], in_=src, deps=res[-1:])]
        tst = []
        for hh in range(2):
            tst.append(b.dma("gpsimd",
                       out=y_out.rearrange("(i p) f -> p i f", p=128)
                       [:, hh * 8:(hh + 1) * 8, :],
                       in_=g["X"][:, hh * 8 * D:(hh + 1) * 8 * D].rearrange(
                           "p (i f) -> p i f", f=D),
                       deps=res))
        b.wait("gpsimd", tst)
        nc_built = b.build()
    return nc_built





def alloc_moe_buffers(nc, g):
    g["xnT_h"] = nc.alloc_sbuf_tensor("xnT_h", [128, 8 * D], BF16)
    g["macc"] = nc.alloc_sbuf_tensor("macc", [128, 8 * D], F32)
    g["ew_bf"] = nc.alloc_sbuf_tensor("ew_bf", [128, 8 * D], BF16)
    g["ew_stage"] = nc.alloc_sbuf_tensor("ew_stage", [128, D], F32)
    g["rw_bf"] = nc.alloc_sbuf_tensor("rw_bf", [128, 64], BF16)
    g["eb_bf"] = nc.alloc_sbuf_tensor("eb_bf", [8, D], BF16)
    g["rb_row"] = nc.alloc_sbuf_tensor("rb_row", [1, 8], BF16)
    g["g8"] = nc.alloc_sbuf_tensor("g8", [128, 64], F32)
    g["g8bf"] = nc.alloc_sbuf_tensor("g8bf", [128, 8], BF16)
    g["gT_all"] = nc.alloc_sbuf_tensor("gT_all", [8, 8 * 128], BF16)
    g["e8"] = nc.alloc_sbuf_tensor("e8", [128, 8], F32)
    g["mask8"] = nc.alloc_sbuf_tensor("mask8", [128, 8], F32)
    g["mhave"] = nc.alloc_sbuf_tensor("mhave", [128, 2], F32)
    g["mst"] = {k: nc.alloc_sbuf_tensor("mst_" + k, [128, NT], F32)
                for k in ("m8", "bm8", "l8", "gate")}
    g["xn_bf2"] = nc.alloc_sbuf_tensor("xn_bf2", [128, D], BF16)


def emit_moe(b, g, tin):
    import os as _os
    ser_a = bool(_os.environ.get("MOE_SER_A"))
    ser_b = bool(_os.environ.get("MOE_SER_B"))
    st = g["lnst"]
    res_toks = []
    prev_tr = g.get("moe_ptr")
    prev_ep_evac = None
    prev_racc = {}
    for half in range(2):
        b.serial = ser_a
        b._last = None
        stat_toks = []
        s2_toks = {}
        for tj in range(8):
            i = half * 8 + tj
            Xi = g["X"][:, i * D:(i + 1) * D]
            t1, t2 = emit_ln_stats(b, Xi, st["s1"][:, tj:tj + 1],
                                   st["s2"][:, tj:tj + 1], g["junk"][:, 0:D],
                                   deps=(list(tin) + [g.get("moe_lnfree")])
                                   if tj == 0 else ())
            stat_toks += [t1, t2]
            s2_toks[i] = t2
        tmr, trs = emit_ln_batch(b, g, st, 8, stat_toks)
        g["moe_lnfree"] = tmr
        xnT_toks = {}
        gdone = {}
        prev_small = None
        for tj in range(8):
            i = half * 8 + tj
            Xi = g["X"][:, i * D:(i + 1) * D]
            tap = emit_ln_apply(b, Xi, g["xn_bf2"][:], st["rstd"][:, tj:tj + 1],
                                st["mr"][:, tj:tj + 1],
                                [tmr, trs, prev_tr, prev_ep_evac])
            ttr = emit_transpose(b, g["pT"], g["xn_bf2"][:],
                                 g["xnT_h"][:, tj * D:(tj + 1) * D],
                                 g["eye_sb"][:], 8, [tap], psum_free=prev_tr)
            prev_tr = ttr
            xnT_toks[i] = ttr
            mm = None
            for dc in range(8):
                mm = b.op("tensor", lambda t, oo=g["pA"][:, 0:8],
                          ll=g["xnT_h"][:, tj * D + dc * 128:tj * D + (dc + 1) * 128],
                          rr=g["rw_bf"][:].rearrange("p (c e) -> p c e", e=8)[:, dc, :],
                          st_=(dc == 0), sp=False: t.matmul(
                    out=oo, lhsT=ll, rhs=rr, start=st_, stop=sp),
                    deps=[ttr, g["t_rw"]] + list(prev_racc.get(half, ()))
                    if dc == 0 else ())
            mm = b.op("tensor", lambda t, oo=g["pA"][:, 0:8],
                      ll=g["ones_row"][:], rr=g["rb_row"][:]: t.matmul(
                out=oo, lhsT=ll, rhs=rr, start=False, stop=True),
                deps=[g["t_rb"]])
            tm8 = b.op("vector", lambda v, tj=tj: v.reduce_max(
                out=g["mst"]["m8"][:, tj:tj + 1], in_=g["pA"][:, 0:8],
                axis=AX.X), deps=[mm, prev_small])
            tmask = b.op("vector", lambda v, tj=tj: v.tensor_scalar(
                out=g["mask8"][:], in0=g["pA"][:, 0:8],
                scalar1=g["mst"]["m8"][:, tj:tj + 1], scalar2=None,
                op0=ALU.is_ge), deps=[tm8])
            tbm8 = b.op("vector", lambda v, tj=tj: v.tensor_scalar(
                out=g["mst"]["bm8"][:, tj:tj + 1],
                in0=g["mst"]["m8"][:, tj:tj + 1], scalar1=-1.0, scalar2=None,
                op0=ALU.mult), deps=[tm8])
            te8 = b.op("scalar", lambda s, tj=tj: s.activation(
                out=g["e8"][:], in_=g["pA"][:, 0:8], func=AF.Exp,
                bias=g["mst"]["bm8"][:, tj:tj + 1]), deps=[tbm8, tmask])
            te8s = b.op("scalar", lambda s: s.activation(
                out=g["e8"][:], in_=g["e8"][:], func=AF.Identity),
                deps=[te8])
            prev_racc[half] = (tmask, te8s)
            tl8 = b.op("vector", lambda v, tj=tj: v.reduce_sum(
                out=g["mst"]["l8"][:, tj:tj + 1], in_=g["e8"][:], axis=AX.X),
                deps=[te8s])
            tgate = b.op("vector", lambda v, tj=tj: v.reciprocal(
                out=g["mst"]["gate"][:, tj:tj + 1],
                in_=g["mst"]["l8"][:, tj:tj + 1]), deps=[tl8])
            th = b.op("vector", lambda v: v.memset(g["mhave"][:, 0:1], 0.0),
                      deps=[tgate])
            for e in range(8):
                tinv = b.op("vector", lambda v: v.tensor_scalar(
                    out=g["mhave"][:, 1:2], in0=g["mhave"][:, 0:1],
                    scalar1=-1.0, scalar2=1.0, op0=ALU.mult, op1=ALU.add),
                    deps=[th])
                tkeep = b.op("vector", lambda v, e=e, tj=tj: v.tensor_mul(
                    out=g["g8"][:, tj * 8 + e:tj * 8 + e + 1],
                    in0=g["mask8"][:, e:e + 1], in1=g["mhave"][:, 1:2]),
                    deps=[tinv, tmask])
                th = b.op("vector", lambda v, e=e, tj=tj: v.tensor_add(
                    out=g["mhave"][:, 0:1], in0=g["mhave"][:, 0:1],
                    in1=g["g8"][:, tj * 8 + e:tj * 8 + e + 1]), deps=[tkeep])
            tgs = b.op("vector", lambda v, tj=tj: v.tensor_scalar(
                out=g["g8"][:, tj * 8:(tj + 1) * 8],
                in0=g["g8"][:, tj * 8:(tj + 1) * 8],
                scalar1=g["mst"]["gate"][:, tj:tj + 1], scalar2=0.5,
                op0=ALU.mult, op1=ALU.mult), deps=[th, tgate])
            tgb = b.op("vector", lambda v, tj=tj: v.tensor_copy(
                out=g["g8bf"][:], in_=g["g8"][:, tj * 8:(tj + 1) * 8]),
                deps=[tgs, prev_tr])
            pe_g = b.op("tensor", lambda t, oo=g["pT"][0:8, 0:128],
                        ii=g["g8bf"][:], ee=g["eye_sb"][:]: t.transpose(
                out=oo, in_=ii, identity=ee), deps=[tgb, prev_tr])
            tgT = b.op("vector", lambda v, tj=tj: v.tensor_copy(
                out=g["gT_all"][0:8, tj * 128:(tj + 1) * 128],
                in_=g["pT"][0:8, 0:128]), deps=[pe_g])
            prev_tr = tgT
            gdone[i] = (tgT, tgs)
            prev_small = tgs
        b.serial = ser_b
        b._last = None
        acc_init = {}
        for tj in range(8):
            i = half * 8 + tj
            mm = None
            for (o, w) in ((0, 512), (512, 512)):
                mm = b.op("tensor", lambda t, oo=g["pE"][:, o:o + w],
                          ll=g["gT_all"][0:8, tj * 128:(tj + 1) * 128],
                          rr=g["eb_bf"][0:8, o:o + w]: t.matmul(
                    out=oo, lhsT=ll, rhs=rr, start=True, stop=True),
                    deps=[gdone[i][0], g["t_eb"], prev_ep_evac]
                    if o == 0 else ())
            tev = b.op("vector", lambda v, tj=tj: v.tensor_copy(
                out=g["macc"][:, tj * D:(tj + 1) * D], in_=g["pE"][:]),
                deps=[mm, acc_init.get(tj)])
            acc_init[tj] = tev
            prev_ep_evac = tev
        prev_use_ew = g.get("t_ew_last")
        for e in range(8):
            tcasts = []
            prev_cast = g.get("moe_cast_prev")
            for dc in range(8):
                dW = b.dma("sync", out=g["ew_stage"][:],
                           in_=g["ew_d"][e * 1024 + dc * 128:
                                         e * 1024 + (dc + 1) * 128, :],
                           deps=[prev_cast, prev_use_ew])
                prev_cast = b.op("vector", lambda v, dc=dc: v.tensor_copy(
                    out=g["ew_bf"][:, dc * D:(dc + 1) * D],
                    in_=g["ew_stage"][:]), deps=[dW])
                tcasts.append(prev_cast)
            g["moe_cast_prev"] = prev_cast
            for tj in range(8):
                i = half * 8 + tj
                mm = None
                for (o, w) in ((0, 512), (512, 512)):
                    for dc in range(8):
                        mm = b.op("tensor", lambda t, oo=g["pE"][:, o:o + w],
                                  ll=g["xnT_h"][:, tj * D + dc * 128:
                                                tj * D + (dc + 1) * 128],
                                  rr=g["ew_bf"][:, dc * D + o:dc * D + o + w],
                                  st_=(dc == 0), sp=(dc == 7): t.matmul(
                            out=oo, lhsT=ll, rhs=rr, start=st_, stop=sp),
                            deps=[tcasts[-1], xnT_toks[i], prev_ep_evac]
                            if (dc == 0 and o == 0) else ())
                prev_use_ew = mm
                tev = b.op("vector", lambda v, tj=tj, e=e, i=i:
                           v.scalar_tensor_tensor(
                    out=g["macc"][:, tj * D:(tj + 1) * D], in0=g["pE"][:],
                    scalar=g["g8"][:, tj * 8 + e:tj * 8 + e + 1],
                    in1=g["macc"][:, tj * D:(tj + 1) * D],
                    op0=ALU.mult, op1=ALU.add),
                    deps=[mm, gdone[i][1], acc_init[tj]])
                acc_init[tj] = tev
                prev_ep_evac = tev
                if e == 7:
                    Xi = g["X"][:, i * D:(i + 1) * D]
                    tres = b.op("vector", lambda v, Xi=Xi, tj=tj: v.tensor_add(
                        out=Xi, in0=Xi,
                        in1=g["macc"][:, tj * D:(tj + 1) * D]),
                        deps=[tev, s2_toks[i]])
                    res_toks.append(tres)
        g["t_ew_last"] = prev_use_ew
    g["moe_ptr"] = prev_tr
    b.serial = False
    return res_toks


def build_moe_program(debug=False):
    nc = bass.Bass("TRN2", target_bir_lowering=False)
    b = Bld(nc)
    g = {"nc": nc}
    g["X"] = nc.alloc_sbuf_tensor("Xsb", [128, NT * D], F32)
    g["lnst"] = {k: nc.alloc_sbuf_tensor("ln_" + k, [128, NT], F32)
                 for k in ("s1", "s2", "m", "v", "rstd", "mr")}
    g["junk"] = nc.alloc_sbuf_tensor("junk", [128, D], BF16)
    g["cbuf"] = nc.alloc_sbuf_tensor("cbuf", [128, 16], F32)
    alloc_moe_buffers(nc, g)
    g["eye_sb"] = nc.alloc_sbuf_tensor("eye_sb", [128, 128], BF16)
    g["ones_row"] = nc.alloc_sbuf_tensor("ones_row", [1, 128], BF16)
    x_in = nc.dram_tensor("x", [T, D], F32, kind="ExternalInput").ap()
    y_out = nc.dram_tensor("y", [T, D], F32, kind="ExternalOutput").ap()
    g["ew_d"] = nc.dram_tensor("expert_w", [8 * D, D], F32,
                               kind="ExternalInput").ap()
    g["rw_d"] = nc.dram_tensor("router_w", [D, 8], F32,
                               kind="ExternalInput").ap()
    eb_d = nc.dram_tensor("expert_b", [8, D], F32, kind="ExternalInput").ap()
    rb_d = nc.dram_tensor("router_b", [1, 8], F32, kind="ExternalInput").ap()
    eye_d = nc.dram_tensor("eye", [128, 128], BF16, kind="ExternalInput").ap()
    onr = nc.dram_tensor("ones_row_d", [1, 128], BF16,
                         kind="ExternalInput").ap()
    with (
        nc.psum_tensor("pA", [128, 512], F32) as pA,
        nc.psum_tensor("pE", [128, 1024], F32) as pE,
        nc.psum_tensor("pT", [128, 1024], BF16) as pT,
    ):
        g["pA"], g["pE"], g["pT"] = pA, pE, pT
        if debug:
            g["dbg_moe"] = {nm: nc.dram_tensor(nm, sh, dt, kind="ExternalOutput").ap()
                            for nm, sh, dt in (("d_g8", [128, 64], F32),
                                               ("d_rw", [128, 64], BF16),
                                               ("d_logit", [128, 8], F32),
                                               ("d_macc", [128, 8 * D], F32),
                                               ("d_xnT", [128, 8 * D], BF16),
                                               ("d_gT", [8, 8 * 128], BF16))}
        teye = b.dma("gpsimd", out=g["eye_sb"][:], in_=eye_d)
        tonr = b.dma("gpsimd", out=g["ones_row"][:], in_=onr)
        tx_toks = []
        for hh in range(2):
            tx_toks.append(b.dma("gpsimd",
                           out=g["X"][:, hh * 8 * D:(hh + 1) * 8 * D].rearrange(
                               "p (i f) -> p i f", f=D),
                           in_=x_in.rearrange("(i p) f -> p i f", p=128)
                           [:, hh * 8:(hh + 1) * 8, :]))
        tx = tx_toks
        drw = b.dma("sync", out=g["macc"][:, 0:64].rearrange(
                        "p (c e) -> p c e", e=8),
                    in_=g["rw_d"].rearrange("(c p) e -> p c e", p=128))
        g["t_rw"] = b.op("vector", lambda v: v.tensor_copy(
            out=g["rw_bf"][:], in_=g["macc"][:, 0:64]), deps=[drw])
        deb = b.dma("sync", out=g["macc"][0:8, 64:64 + D], in_=eb_d)
        g["t_eb"] = b.op("vector", lambda v: v.tensor_copy(
            out=g["eb_bf"][:], in_=g["macc"][0:8, 64:64 + D]), deps=[deb])
        drb = b.dma("sync", out=g["macc"][0:1, 1088:1096], in_=rb_d)
        g["t_rb"] = b.op("vector", lambda v: v.tensor_copy(
            out=g["rb_row"][:], in_=g["macc"][0:1, 1088:1096]), deps=[drb])
        for e in ("vector", "scalar", "tensor"):
            b.wait(e, [teye, tonr, g["t_rw"], g["t_eb"], g["t_rb"]] + tx)
        res = emit_moe(b, g, tx)
        if debug:
            dd = g["dbg_moe"]
            res = list(res)
            for nm, src in (("d_g8", g["g8"][:]), ("d_rw", g["rw_bf"][:]),
                            ("d_macc", g["macc"][:]), ("d_xnT", g["xnT_h"][:]),
                            ("d_gT", g["gT_all"][:])):
                res.append(b.dma("scalar", out=dd[nm], in_=src, deps=res[-1:]))
        tst = []
        for hh in range(2):
            tst.append(b.dma("gpsimd",
                       out=y_out.rearrange("(i p) f -> p i f", p=128)
                       [:, hh * 8:(hh + 1) * 8, :],
                       in_=g["X"][:, hh * 8 * D:(hh + 1) * 8 * D].rearrange(
                           "p (i f) -> p i f", f=D),
                       deps=res))
        b.wait("gpsimd", tst)
        nc_built = b.build()
    return nc_built





def alloc_l1a_buffers(nc, g):
    g["xlnT"] = nc.alloc_sbuf_tensor("xlnT", [128, 8 * T], BF16)
    g["Wstage"] = nc.alloc_sbuf_tensor("Wstage", [128, 1024], F32)
    g["Wblk"] = nc.alloc_sbuf_tensor("Wblk", [128, 2 * 1024], BF16)
    g["xn_bf2"] = nc.alloc_sbuf_tensor("xn_bf2", [128, D], BF16)
    g["res_sb"] = nc.alloc_sbuf_tensor("res_sb", [128, 512], F32)
    g["rn2_sb"] = nc.alloc_sbuf_tensor("rn2_sb", [1, T], F32)
    g["xn2_row"] = nc.alloc_sbuf_tensor("xn2_row", [1, T], F32)
    g["gp_small"] = nc.alloc_sbuf_tensor("gp_small", [128, 4], F32)
    g["bproj_cols"] = nc.alloc_sbuf_tensor("bproj_cols", [128, 8], F32)
    g["res_bf"] = nc.alloc_sbuf_tensor("res_bf", [128, 512], BF16)
    g["rat_bf"] = nc.alloc_sbuf_tensor("rat_bf", [128, NT], BF16)


def emit_l1_xlnT(b, g, tin, keep_v=False):
    st = g["lnst"]
    stat_toks = []
    s2_toks = []
    for i in range(NT):
        Xi = g["X"][:, i * D:(i + 1) * D]
        t1, t2 = emit_ln_stats(b, Xi, st["s1"][:, i:i + 1], st["s2"][:, i:i + 1],
                               g["junk"][:, 0:D], deps=tin if i == 0 else ())
        stat_toks += [t1, t2]
        s2_toks.append(t2)
    tmr, trs = emit_ln_batch(b, g, st, NT, stat_toks)
    prev_tr = None
    xlnT_toks = []
    for i in range(NT):
        Xi = g["X"][:, i * D:(i + 1) * D]
        tap = emit_ln_apply(b, Xi, g["xn_bf2"][:], st["rstd"][:, i:i + 1],
                            st["mr"][:, i:i + 1], [tmr, trs])
        pe_last = None
        for c in range(8):
            pe_last = b.op("tensor", lambda t,
                           oo=g["pT"][:, c * 128:(c + 1) * 128],
                           ii=g["xn_bf2"][:, c * 128:(c + 1) * 128],
                           ee=g["eye_sb"][:]: t.transpose(out=oo, in_=ii,
                                                          identity=ee),
                           deps=[tap, prev_tr] if c == 0 else ())
        tc = b.op("vector", lambda v, i=i: v.tensor_copy(
            out=g["xlnT"][:].rearrange("p (c t) -> p c t", t=T)
            [:, :, i * 128:(i + 1) * 128],
            in_=g["pT"][:].rearrange("p (c f) -> p c f", f=128)), deps=[pe_last])
        prev_tr = tc
        xlnT_toks.append(tc)
    return xlnT_toks, s2_toks, prev_tr, (tmr, trs)


def emit_wcolblock(b, g, w_dram, ncb, deps=(), slot=0):
    dW = b.dma("sync", out=g["Wstage"][:, 0:1024].rearrange(
        "p (c f) -> p c f", f=128),
        in_=w_dram.rearrange("(c p) n -> p c n", p=128)
        [:, :, ncb * 128:(ncb + 1) * 128], deps=deps)
    return b.op("vector", lambda v: v.tensor_copy(
        out=g["Wblk"][:, slot * 1024:(slot + 1) * 1024],
        in_=g["Wstage"][:, 0:1024]), deps=[dW])


def emit_l1a(b, g, tin):
    xlnT_toks, s2_toks, ptr, _ = emit_l1_xlnT(b, g, tin)
    prev_cast = None
    prev_use = None
    rn_last = None
    prev_sq = None
    for ncb in range(8):
        tcast = emit_wcolblock(b, g, g["WprojT_d"], ncb,
                               deps=[prev_cast, prev_use], slot=ncb % 2)
        prev_cast = tcast
        slot = ncb % 2
        for gs in range(4):
            mm = None
            for dc in range(8):
                mm = b.op("tensor", lambda t, oo=g["pS"][:, 0:512],
                          ll=g["Wblk"][:, slot * 1024 + dc * 128:
                                       slot * 1024 + (dc + 1) * 128],
                          rr=g["xlnT"][:].rearrange("p (c t) -> p c t", t=T)
                          [:, dc, gs * 512:(gs + 1) * 512],
                          st_=(dc == 0), sp=(dc == 7): t.matmul(
                    out=oo, lhsT=ll, rhs=rr, start=st_, stop=sp),
                    deps=[tcast, xlnT_toks[-1], prev_sq] if dc == 0 else ())
            prev_use = mm
            tres = b.op("vector", lambda v, ncb=ncb, gs=gs:
                        v.scalar_tensor_tensor(
                out=g["res_sb"][:], in0=g["pS"][:, 0:512],
                scalar=g["bproj_cols"][:, ncb:ncb + 1],
                in1=g["xlnT"][:].rearrange("p (c t) -> p c t", t=T)
                [:, ncb, gs * 512:(gs + 1) * 512],
                op0=ALU.add, op1=ALU.subtract), deps=[mm, g["t_bproj"]])
            tsq = b.op("vector", lambda v: v.tensor_mul(
                out=g["res_bf"][:], in0=g["res_sb"][:], in1=g["res_sb"][:]),
                deps=[tres])
            prev_sq = tsq
            rn_last = b.op("tensor", lambda t, gs=gs,
                           oo=g["pF"][0:1, gs * 512:(gs + 1) * 512],
                           ll=g["ones_col"][:], rr=g["res_bf"][:],
                           st_=(ncb == 0), sp=(ncb == 7): t.matmul(
                out=oo, lhsT=ll, rhs=rr, start=st_, stop=sp),
                deps=[tsq])
    tev = b.op("vector", lambda v: v.tensor_copy(
        out=g["rn2_sb"][:], in_=g["pF"][0:1, 0:T]), deps=[rn_last])
    xn_last = None
    prev_sq2 = tev
    for dc in range(8):
        for gs in range(4):
            tsq = b.op("vector", lambda v, dc=dc, gs=gs: v.tensor_mul(
                out=g["res_bf"][:],
                in0=g["xlnT"][:].rearrange("p (c t) -> p c t", t=T)
                [:, dc, gs * 512:(gs + 1) * 512],
                in1=g["xlnT"][:].rearrange("p (c t) -> p c t", t=T)
                [:, dc, gs * 512:(gs + 1) * 512]), deps=[prev_sq2])
            prev_sq2 = tsq
            xn_last = b.op("tensor", lambda t, gs=gs,
                           oo=g["pF"][0:1, gs * 512:(gs + 1) * 512],
                           ll=g["ones_col"][:], rr=g["res_bf"][:],
                           st_=(dc == 0), sp=(dc == 7): t.matmul(
                out=oo, lhsT=ll, rhs=rr, start=st_, stop=sp),
                deps=[tsq, tev] if dc == 0 and gs == 0 else [tsq])
    tev2 = b.op("vector", lambda v: v.tensor_copy(
        out=g["xn2_row"][:], in_=g["pF"][0:1, 0:T]), deps=[xn_last])
    tsr = b.op("scalar", lambda s: s.activation(out=g["rn2_sb"][:],
                                                in_=g["rn2_sb"][:],
                                                func=AF.Ln), deps=[tev])
    tsr2 = b.op("scalar", lambda s: s.activation(out=g["rn2_sb"][:],
                                                 in_=g["rn2_sb"][:],
                                                 func=AF.Exp, scale=0.5),
                deps=[tsr])
    tsx = b.op("scalar", lambda s: s.activation(out=g["xn2_row"][:],
                                                in_=g["xn2_row"][:],
                                                func=AF.Ln), deps=[tev2])
    tsx2 = b.op("scalar", lambda s: s.activation(out=g["xn2_row"][:],
                                                 in_=g["xn2_row"][:],
                                                 func=AF.Exp, scale=0.5),
                deps=[tsx])
    tsx3 = b.op("vector", lambda v: v.tensor_scalar(
        out=g["xn2_row"][:], in0=g["xn2_row"][:], scalar1=1e-8, scalar2=None,
        op0=ALU.add), deps=[tsx2])
    trc = b.op("vector", lambda v: v.reciprocal(out=g["xn2_row"][:],
                                                in_=g["xn2_row"][:]),
               deps=[tsx3])
    trt = b.op("vector", lambda v: v.tensor_mul(
        out=g["rn2_sb"][:], in0=g["rn2_sb"][:], in1=g["xn2_row"][:]),
        deps=[trc, tsr2])
    tps = b.op("vector", lambda v: v.reduce_sum(
        out=g["gp_small"][0:1, 0:1], in_=g["rn2_sb"][:], axis=AX.X),
        deps=[trt])
    return tps


def build_l1a_program():
    nc = bass.Bass("TRN2", target_bir_lowering=False)
    b = Bld(nc)
    g = {"nc": nc}
    g["X"] = nc.alloc_sbuf_tensor("Xsb", [128, NT * D], F32)
    g["lnst"] = {k: nc.alloc_sbuf_tensor("ln_" + k, [128, NT], F32)
                 for k in ("s1", "s2", "m", "v", "rstd", "mr")}
    g["junk"] = nc.alloc_sbuf_tensor("junk", [128, D], BF16)
    g["cbuf"] = nc.alloc_sbuf_tensor("cbuf", [128, 16], F32)
    alloc_l1a_buffers(nc, g)
    g["eye_sb"] = nc.alloc_sbuf_tensor("eye_sb", [128, 128], BF16)
    g["ones_col"] = nc.alloc_sbuf_tensor("ones_col", [128, 1], BF16)
    x_in = nc.dram_tensor("x", [T, D], F32, kind="ExternalInput").ap()
    gp_out = nc.dram_tensor("gp", [1, 1], F32, kind="ExternalOutput").ap()
    g["WprojT_d"] = nc.dram_tensor("WprojT", [D, D], F32,
                                   kind="ExternalInput").ap()
    bpc = nc.dram_tensor("bproj_cols_d", [128, 8], F32,
                         kind="ExternalInput").ap()
    eye_d = nc.dram_tensor("eye", [128, 128], BF16, kind="ExternalInput").ap()
    onc_d = nc.dram_tensor("ones_col_d", [128, 1], BF16,
                           kind="ExternalInput").ap()
    with (
        nc.psum_tensor("pS", [128, 512], F32) as pS,
        nc.psum_tensor("pT", [128, 1024], BF16) as pT,
        nc.psum_tensor("pF", [1, 2048], F32) as pF,
    ):
        g["pS"], g["pT"], g["pF"] = pS, pT, pF
        teye = b.dma("gpsimd", out=g["eye_sb"][:], in_=eye_d)
        tonc = b.dma("gpsimd", out=g["ones_col"][:], in_=onc_d)
        tbp = b.dma("gpsimd", out=g["bproj_cols"][:], in_=bpc)
        g["t_bproj"] = tbp
        tx_toks = []
        for hh in range(2):
            tx_toks.append(b.dma("gpsimd",
                           out=g["X"][:, hh * 8 * D:(hh + 1) * 8 * D].rearrange(
                               "p (i f) -> p i f", f=D),
                           in_=x_in.rearrange("(i p) f -> p i f", p=128)
                           [:, hh * 8:(hh + 1) * 8, :]))
        for e in ("vector", "scalar", "tensor"):
            b.wait(e, [teye, tonc, tbp] + tx_toks)
        tps = emit_l1a(b, g, tx_toks)
        tst = b.dma("gpsimd", out=gp_out, in_=g["gp_small"][0:1, 0:1],
                    deps=[tps])
        b.wait("gpsimd", [tst])
        nc_built = b.build()
    return nc_built





L1_NEG = -1.0e9
GAUGE_TH = 0.3
ENT_TH = 0.15


def alloc_l1b_buffers(nc, g):
    g["xstage"] = nc.alloc_sbuf_tensor("xstage", [128, 2 * 1024], F32)
    g["xlnT"] = nc.alloc_sbuf_tensor("xlnT", [128, 8 * T], BF16)
    g["dT"] = nc.alloc_sbuf_tensor("dT", [128, 8 * T], BF16)
    g["knT"] = nc.alloc_sbuf_tensor("knT", [128, 8 * T], BF16)
    g["V"] = nc.alloc_sbuf_tensor("Vsb", [128, NT * D], BF16)
    g["Wfull"] = nc.alloc_sbuf_tensor("Wfull", [128, 8 * 1024], BF16)
    g["Wstage"] = nc.alloc_sbuf_tensor("Wstage", [128, 1024], F32)
    g["scr4k"] = nc.alloc_sbuf_tensor("scr4k", [128, 1024], F32)
    g["scr2k"] = nc.alloc_sbuf_tensor("scr2k", [128, 512], F32)
    g["qn_bf"] = nc.alloc_sbuf_tensor("qn_bf", [128, 1024], BF16)
    g["qbf2"] = nc.alloc_sbuf_tensor("qbf2", [128, 1024], BF16)
    g["u384"] = nc.alloc_sbuf_tensor("u384", [128, 384], F32)
    g["P384"] = nc.alloc_sbuf_tensor("P384", [128, 384], BF16)
    g["PT3"] = nc.alloc_sbuf_tensor("PT3", [128, 384], BF16)
    g["mask_mid"] = nc.alloc_sbuf_tensor("mask_mid", [128, 384], BF16)
    g["mask_first"] = nc.alloc_sbuf_tensor("mask_first", [128, 256], BF16)
    g["lbuf"] = nc.alloc_sbuf_tensor("lbuf", [128, 8], F32)
    g["f8"] = nc.alloc_sbuf_tensor("f8", [128, 8], F32)
    g["n2_8"] = nc.alloc_sbuf_tensor("n2_8", [128, 8], F32)
    g["mx8"] = nc.alloc_sbuf_tensor("mx8", [128, 8], F32)
    g["n_all"] = nc.alloc_sbuf_tensor("n_all", [128, 128], F32)
    g["in_all"] = nc.alloc_sbuf_tensor("in_all", [128, 128], F32)
    g["kn_all"] = nc.alloc_sbuf_tensor("kn_all", [128, 128], F32)
    g["kin_all"] = nc.alloc_sbuf_tensor("kin_all", [128, 128], F32)
    g["hacc"] = nc.alloc_sbuf_tensor("hacc", [128, 64], F32)
    g["hcol"] = nc.alloc_sbuf_tensor("hcol", [128, 4], F32)
    g["hcol_bf"] = nc.alloc_sbuf_tensor("hcol_bf", [128, 1], BF16)
    g["Hrows"] = nc.alloc_sbuf_tensor("Hrows", [1, 64], F32)
    g["prow"] = nc.alloc_sbuf_tensor("prow", [1, 1024], F32)
    g["brow"] = nc.alloc_sbuf_tensor("brow", [1, 1024], F32)
    g["pmax8"] = nc.alloc_sbuf_tensor("pmax8", [1, 8], F32)
    g["w2bc"] = nc.alloc_sbuf_tensor("w2bc", [128, 512], BF16)
    g["cvacc"] = nc.alloc_sbuf_tensor("cvacc", [128, 4], F32)
    g["g128"] = nc.alloc_sbuf_tensor("g128", [128, 2], F32)
    g["nb2c"] = nc.alloc_sbuf_tensor("nb2c", [128, 1], F32)
    g["mrow"] = nc.alloc_sbuf_tensor("mrow", [1, 16], F32)
    g["mrow_bf"] = nc.alloc_sbuf_tensor("mrow_bf", [1, 16], BF16)
    g["gp_row"] = nc.alloc_sbuf_tensor("gp_row", [1, 8], F32)
    g["bias_row"] = nc.alloc_sbuf_tensor("bias_row", [1, 1024], BF16)
    g["bproj_cols"] = nc.alloc_sbuf_tensor("bproj_cols", [128, 8], F32)
    g["ones_row"] = nc.alloc_sbuf_tensor("ones_row", [1, 128], BF16)
    g["ones_col"] = nc.alloc_sbuf_tensor("ones_col", [128, 1], BF16)


def emit_l1_xlnT_stream(b, g, x_dram, tin):
    st = g["lnst"]
    taps = {}
    prev_tr = None
    prev_pe = None
    xlnT_toks = []
    for i in range(NT):
        slot = i % 2
        stage = g["xstage"][:, slot * 1024:(slot + 1) * 1024]
        tX = b.dma("gpsimd", out=stage, in_=x_dram[i * 128:(i + 1) * 128, :],
                   deps=list(tin if i == 0 else []) + [taps.get(i - 2)])
        t1, t2 = emit_ln_stats(b, stage, st["s1"][:, i:i + 1],
                               st["s2"][:, i:i + 1], g["junk"][:, 0:D],
                               deps=[tX])
        tmr, trs = emit_ln_batch(b, g, st, 1, [t1, t2], off=i)
        tap = emit_ln_apply(b, stage, g["qn_bf"][:], st["rstd"][:, i:i + 1],
                            st["mr"][:, i:i + 1], [tmr, trs, prev_pe])
        taps[i] = tap
        pe_last = None
        for c in range(8):
            pe_last = b.op("tensor", lambda t,
                           oo=g["pT"][:, c * 128:(c + 1) * 128],
                           ii=g["qn_bf"][:, c * 128:(c + 1) * 128],
                           ee=g["eye_sb"][:]: t.transpose(out=oo, in_=ii,
                                                          identity=ee),
                           deps=[tap, prev_tr] if c == 0 else ())
        prev_pe = pe_last
        tc = b.op("vector", lambda v, i=i: v.tensor_copy(
            out=g["xlnT"][:].rearrange("p (c t) -> p c t", t=T)
            [:, :, i * 128:(i + 1) * 128],
            in_=g["pT"][:].rearrange("p (c f) -> p c f", f=128)),
            deps=[pe_last])
        prev_tr = tc
        xlnT_toks.append(tc)
    return xlnT_toks


def emit_wfull(b, g, w_dram, deps=(), ncols=1024, nchunks=8):
    toks = []
    prev = None
    for dc in range(nchunks):
        dW = b.dma("sync", out=g["Wstage"][:, 0:ncols],
                   in_=w_dram[dc * 128:(dc + 1) * 128, :],
                   deps=list(deps) + [prev, g.get("_wfull_prev_use")])
        prev = b.op("vector", lambda v, dc=dc, ncols=ncols: v.tensor_copy(
            out=g["Wfull"][:, dc * ncols:(dc + 1) * ncols],
            in_=g["Wstage"][:, 0:ncols]), deps=[dW])
        toks.append(prev)
    return toks[-1]


def tile_blocks(i):
    if i == 0:
        return [0, 1], "first"
    if i == NT - 1:
        return [NT - 2, NT - 1], "last"
    return [i - 1, i, i + 1], "mid"


def emit_row_entropy(b, g, outcol, deps):
    c10, c10t = get_const(b, g, 1e-10)
    prow = g["prow"][:]
    t1 = b.op("vector", lambda v: v.tensor_reduce(
        out=g["pmax8"][:], in_=prow.rearrange("o (h k) -> o h k", k=128),
        axis=AX.X, op=ALU.max), deps=deps)
    t2 = b.op("vector", lambda v: v.scalar_tensor_tensor(
        out=prow.rearrange("o (h k) -> o h k", k=128),
        in0=g["pmax8"][:].unsqueeze(2).broadcast_to([1, 8, 128]),
        scalar=-1.0, in1=prow.rearrange("o (h k) -> o h k", k=128),
        op0=ALU.mult, op1=ALU.add), deps=[t1])
    t3 = b.op("scalar", lambda s: s.activation(
        out=prow, in_=prow, func=AF.Exp), deps=[t2])
    t4 = b.op("vector", lambda v: v.tensor_reduce(
        out=g["pmax8"][:], in_=prow.rearrange("o (h k) -> o h k", k=128),
        axis=AX.X, op=ALU.add), deps=[t3])
    t5 = b.op("vector", lambda v: v.reciprocal(
        out=g["pmax8"][:], in_=g["pmax8"][:]), deps=[t4])
    t6 = b.op("vector", lambda v: v.tensor_mul(
        out=prow.rearrange("o (h k) -> o h k", k=128),
        in0=prow.rearrange("o (h k) -> o h k", k=128),
        in1=g["pmax8"][:].unsqueeze(2).broadcast_to([1, 8, 128])),
        deps=[t5])
    t7 = b.op("scalar", lambda s: s.activation(
        out=g["brow"][:], in_=prow, func=AF.Ln, bias=c10[0:1, 0:1]),
        deps=[t6, c10t])
    t8 = b.op("vector", lambda v: v.tensor_mul(
        out=g["brow"][:], in0=g["brow"][:], in1=prow), deps=[t7])
    t9 = b.op("vector", lambda v, outcol=outcol: v.tensor_reduce(
        out=outcol, in_=g["brow"][:].rearrange("o (h k) -> o h k", k=128),
        axis=AX.X, op=ALU.add), deps=[t8])
    return t9


def emit_l1b(b, g, tin):
    xlnT_toks = emit_l1_xlnT_stream(b, g, g["x_d"], tin)
    g["_l1b_ptfree"] = xlnT_toks[-1]
    CT = g["xlnT"][:].rearrange("p (c t) -> p c t", t=T)
    tgp = b.op("vector", lambda v: v.reduce_sum(
        out=g["mrow"][0:1, 8:9], in_=g["gp_row"][0:1, 0:8], axis=AX.X),
        deps=[g["t_gp"]])
    tgm = b.op("vector", lambda v: v.tensor_scalar(
        out=g["mrow"][0:1, 9:10], in0=g["mrow"][0:1, 8:9],
        scalar1=1.0 / 16384.0, scalar2=GAUGE_TH, op0=ALU.mult, op1=ALU.is_gt),
        deps=[tgp])
    tgmb = b.op("vector", lambda v: v.tensor_copy(
        out=g["mrow_bf"][0:1, 0:1], in_=g["mrow"][0:1, 9:10]), deps=[tgm])
    tg128 = b.op("tensor", lambda t: t.matmul(
        out=g["pS"][:, 0:1], lhsT=g["ones_row"][:],
        rhs=g["mrow_bf"][0:1, 0:1], start=True, stop=True),
        deps=[tgmb, g["t_ones"]])
    tmb = b.op("vector", lambda v: v.tensor_copy(
        out=g["g128"][:, 0:1], in_=g["pS"][:, 0:1]), deps=[tg128])
    twp = emit_wfull(b, g, g["WprojT_d"])
    prev_ev = None
    dT_toks = []
    for ncb in range(8):
        for gs in range(4):
            mm = None
            for dc in range(8):
                mm = b.op("tensor", lambda t, oo=g["pS"][:, 0:512],
                          ll=g["Wfull"][:, dc * 1024 + ncb * 128:
                                        dc * 1024 + (ncb + 1) * 128],
                          rr=CT[:, dc, gs * 512:(gs + 1) * 512],
                          st_=(dc == 0), sp=(dc == 7): t.matmul(
                    out=oo, lhsT=ll, rhs=rr, start=st_, stop=sp),
                    deps=[twp, xlnT_toks[-1], prev_ev, tmb] if dc == 0 else ())
            prev_ev = b.op("vector", lambda v, ncb=ncb, gs=gs:
                           v.scalar_tensor_tensor(
                out=g["dT"][:].rearrange("p (c t) -> p c t", t=T)
                [:, ncb, gs * 512:(gs + 1) * 512],
                in0=g["pS"][:, 0:512],
                scalar=g["bproj_cols"][:, ncb:ncb + 1],
                in1=CT[:, ncb, gs * 512:(gs + 1) * 512],
                op0=ALU.add, op1=ALU.subtract), deps=[mm, g["t_bproj"]])
            dT_toks.append(prev_ev)
    g["_wfull_prev_use"] = prev_ev
    twv = emit_wfull(b, g, g["WvT_d"])
    tbv = b.dma("sync", out=g["bias_row"][:], in_=g["bv_d"], deps=[])
    prev_vev = None
    for i in range(NT):
        for (o, w) in ((0, 512), (512, 512)):
            mm = None
            for dc in range(8):
                mm = b.op("tensor", lambda t, oo=g["pS"][:, 0:512],
                          ll=CT[:, dc, i * 128:(i + 1) * 128],
                          rr=g["Wfull"][:, dc * 1024 + o:dc * 1024 + o + w],
                          st_=(dc == 0), sp=False: t.matmul(
                    out=oo, lhsT=ll, rhs=rr, start=st_, stop=sp),
                    deps=[twv, xlnT_toks[i], prev_vev] if dc == 0 else ())
            for bo in (0, 256):
                mm = b.op("tensor", lambda t, bo=bo, o=o,
                          oo=g["pS"][:, bo:bo + 256],
                          ll=g["ones_row"][:],
                          rr=g["bias_row"][0:1, o + bo:o + bo + 256]:
                          t.matmul(out=oo, lhsT=ll, rhs=rr, start=False,
                                   stop=True),
                          deps=[tbv, g["t_ones"]] if bo == 0 else ())
            prev_vev = b.op("vector", lambda v, i=i, o=o: v.tensor_copy(
                out=g["V"][:, i * D + o:i * D + o + 512],
                in_=g["pS"][:, 0:512]), deps=[mm])
    g["_wfull_prev_use"] = prev_vev
    xqk_toks = []
    prev = None
    for dc in range(8):
        prev = b.op("vector", lambda v, dc=dc: v.scalar_tensor_tensor(
            out=CT[:, dc, :], in0=g["dT"][:].rearrange(
                "p (c t) -> p c t", t=T)[:, dc, :],
            scalar=g["g128"][:, 0:1], in1=CT[:, dc, :],
            op0=ALU.mult, op1=ALU.add),
            deps=[tmb, dT_toks[-1], prev, prev_vev])
        xqk_toks.append(prev)

    def qk_pipeline(w_dram, tbias, dst, dst_n, dst_in, csguard):
        tw = emit_wfull(b, g, w_dram)
        out_toks = []
        prev_q = None
        prev_tin = None
        cs_last = None
        for i in range(NT):
            for (o, w) in ((0, 512), (512, 512)):
                mm = None
                for dc in range(8):
                    mm = b.op("tensor", lambda t, oo=g["pS"][:, 0:512],
                              ll=CT[:, dc, i * 128:(i + 1) * 128],
                              rr=g["Wfull"][:, dc * 1024 + o:dc * 1024 + o + w],
                              st_=(dc == 0), sp=False: t.matmul(
                        out=oo, lhsT=ll, rhs=rr, start=st_, stop=sp),
                        deps=[tw, xqk_toks[-1], prev_q] if dc == 0 else ())
                for bo in (0, 256):
                    mm = b.op("tensor", lambda t, bo=bo, o=o,
                              oo=g["pS"][:, bo:bo + 256],
                              ll=g["ones_row"][:],
                              rr=g["bias_row"][0:1, o + bo:o + bo + 256]:
                              t.matmul(out=oo, lhsT=ll, rhs=rr, start=False,
                                       stop=True),
                              deps=[tbias, g["t_ones"]] if bo == 0 else ())
                prev_q = b.op("vector", lambda v, o=o: v.tensor_copy(
                    out=g["scr4k"][:, o:o + 512], in_=g["pS"][:, 0:512]),
                    deps=[mm])
            tqb = b.op("vector", lambda v: v.tensor_copy(
                out=g["qbf2"][:], in_=g["scr4k"][:]), deps=[prev_q])
            for (o, w) in ((0, 512), (512, 512)):
                cs_last = b.op("tensor", lambda t, o=o, w=w, i=i,
                               oo=g["pFq"][0:1, o:o + w],
                               ll=g["ones_col"][:],
                               rr=g["qbf2"][:, o:o + w],
                               st_=(i == 0), sp=(i == NT - 1): t.matmul(
                    out=oo, lhsT=ll, rhs=rr, start=st_, stop=sp),
                    deps=[tqb, g["t_onec"]] +
                    ([csguard] if (i == 0 and o == 0) else []))
            tn2 = None
            for h in range(8):
                tn2 = b.op("vector", lambda v, h=h: v.scalar_tensor_tensor(
                    out=g["junk"][:, 0:128],
                    in0=g["scr4k"][:, h * 128:(h + 1) * 128], scalar=1.0,
                    in1=g["scr4k"][:, h * 128:(h + 1) * 128],
                    op0=ALU.mult, op1=ALU.mult,
                    accum_out=g["n2_8"][:, h:h + 1]),
                    deps=[prev_q, prev_tin] if h == 0 else [prev_q])
            tcl = b.op("vector", lambda v: v.tensor_scalar(
                out=g["n2_8"][:], in0=g["n2_8"][:], scalar1=1e-24,
                scalar2=None, op0=ALU.max), deps=[tn2])
            tln = b.op("scalar", lambda s: s.activation(
                out=g["n2_8"][:], in_=g["n2_8"][:], func=AF.Ln), deps=[tcl])
            tn = b.op("scalar", lambda s, i=i, dst_n=dst_n: s.activation(
                out=dst_n[:, i * 8:(i + 1) * 8], in_=g["n2_8"][:],
                func=AF.Exp, scale=0.5), deps=[tln])
            tin_ = b.op("scalar", lambda s, i=i, dst_in=dst_in: s.activation(
                out=dst_in[:, i * 8:(i + 1) * 8], in_=g["n2_8"][:],
                func=AF.Exp, scale=-0.5), deps=[tn])
            prev_tin = tin_
            tqn = b.op("vector", lambda v, i=i, dst_in=dst_in: v.tensor_mul(
                out=g["qn_bf"][:].rearrange("p (h k) -> p h k", k=128),
                in0=g["scr4k"][:].rearrange("p (h k) -> p h k", k=128),
                in1=dst_in[:, i * 8:(i + 1) * 8].unsqueeze(2).broadcast_to(
                    [128, 8, 128])), deps=[tin_, tqb])
            pe_last = None
            for c in range(8):
                pe_last = b.op("tensor", lambda t, c=c,
                               oo=g["pT"][:, c * 128:(c + 1) * 128],
                               ii=g["qn_bf"][:, c * 128:(c + 1) * 128],
                               ee=g["eye_sb"][:]: t.transpose(
                    out=oo, in_=ii, identity=ee),
                    deps=[tqn, g["_l1b_ptfree"]] if c == 0 else ())
            tc = b.op("vector", lambda v, i=i, dst=dst: v.tensor_copy(
                out=dst.rearrange("p (c t) -> p c t", t=T)
                [:, :, i * 128:(i + 1) * 128],
                in_=g["pT"][:].rearrange("p (c f) -> p c f", f=128)),
                deps=[pe_last])
            g["_l1b_ptfree"] = tc
            out_toks.append(tc)
        g["_wfull_prev_use"] = out_toks[-1]
        return out_toks, cs_last

    tbq = b.dma("sync", out=g["bias_row"][:], in_=g["bq_d"], deps=[prev_vev])
    qnT_toks, q_cs = qk_pipeline(g["WqT_d"], tbq, g["dT"][:],
                                 g["n_all"], g["in_all"], None)
    tpq = b.op("vector", lambda v: v.tensor_copy(
        out=g["prow"][:], in_=g["pFq"][0:1, 0:1024]), deps=[q_cs])
    tq9 = emit_row_entropy(b, g, g["Hrows"][0:1, 16:24], [tpq])
    tbk = b.dma("sync", out=g["bias_row"][:], in_=g["bk_d"],
                deps=[qnT_toks[-1]])
    knT_toks, k_cs = qk_pipeline(g["WkT_d"], tbk, g["knT"][:],
                                 g["kn_all"], g["kin_all"], tpq)
    tpk = b.op("vector", lambda v: v.tensor_copy(
        out=g["prow"][:], in_=g["pFq"][0:1, 0:1024]), deps=[k_cs, tq9])
    tk9 = emit_row_entropy(b, g, g["Hrows"][0:1, 24:32], [tpk])
    g["_l1b_hqk"] = (tq9, tk9)
    return xlnT_toks, qnT_toks, knT_toks


def emit_l1b_attn(b, g, qnT_toks, knT_toks):
    qT = g["dT"][:].rearrange("p (c t) -> p c t", t=T)
    kT = g["knT"][:].rearrange("p (c t) -> p c t", t=T)
    o_buf = g["xlnT"][:]
    o_toks = []
    prev_u = None
    prev_P = None
    prev_PT = None
    prev_oev = None
    for i in range(NT):
        blocks, kind = tile_blocks(i)
        nb = len(blocks)
        W = nb * 128
        j0 = blocks[0] * 128
        mask = g["mask_first"][:, 0:W] if kind == "first" \
            else g["mask_mid"][:, 0:W]
        tl_last = None
        for h in range(8):
            mm = b.op("tensor", lambda t, h=h, i=i, j0=j0, W=W,
                      oo=g["pS"][:, 0:W],
                      ll=qT[:, h, i * 128:(i + 1) * 128],
                      rr=kT[:, h, j0:j0 + W]: t.matmul(
                out=oo, lhsT=ll, rhs=rr, start=True, stop=True),
                deps=[prev_u] + ([qnT_toks[-1], knT_toks[-1]]
                                 if (i == 0 and h == 0) else []))
            tu = b.op("vector", lambda v, W=W: v.tensor_scalar(
                out=g["u384"][:, 0:W], in0=g["pS"][:, 0:W], scalar1=-2.0,
                scalar2=2.0, op0=ALU.mult, op1=ALU.add), deps=[mm, prev_P])
            prev_u = tu
            tu2 = b.op("vector", lambda v, W=W: v.tensor_scalar(
                out=g["u384"][:, 0:W], in0=g["u384"][:, 0:W], scalar1=1e-30,
                scalar2=None, op0=ALU.max), deps=[tu])
            td1 = b.op("scalar", lambda s, W=W: s.activation(
                out=g["u384"][:, 0:W], in_=g["u384"][:, 0:W], func=AF.Ln),
                deps=[tu2])
            td2 = b.op("scalar", lambda s, W=W: s.activation(
                out=g["u384"][:, 0:W], in_=g["u384"][:, 0:W], func=AF.Exp,
                scale=0.5), deps=[td1])
            tz = b.op("vector", lambda v, W=W, mask=mask: v.tensor_sub(
                out=g["u384"][:, 0:W], in0=mask, in1=g["u384"][:, 0:W]),
                deps=[td2, g["t_masks"]])
            tP = b.op("scalar", lambda s, W=W: s.activation(
                out=g["P384"][:, 0:W], in_=g["u384"][:, 0:W], func=AF.Exp),
                deps=[tz])
            prev_P = tP
            tl_last = b.op("vector", lambda v, h=h, W=W: v.reduce_sum(
                out=g["lbuf"][:, h:h + 1], in_=g["P384"][:, 0:W], axis=AX.X),
                deps=[tP])
            pe_t = None
            for bi in range(nb):
                pe_t = b.op("tensor", lambda t, bi=bi,
                            oo=g["pT"][:, bi * 128:(bi + 1) * 128],
                            ii=g["P384"][:, bi * 128:(bi + 1) * 128],
                            ee=g["eye_sb"][:]: t.transpose(
                    out=oo, in_=ii, identity=ee),
                    deps=[tP, prev_PT, g["_l1b_ptfree"]] if bi == 0 else ())
            tptc = b.op("vector", lambda v, nb=nb: v.tensor_copy(
                out=g["PT3"][:, 0:nb * 128], in_=g["pT"][:, 0:nb * 128]),
                deps=[pe_t, g.get("_l1b_pvuse")])
            prev_PT = tptc
            mmv = None
            for bi, jb in enumerate(blocks):
                mmv = b.op("tensor", lambda t, bi=bi, jb=jb, h=h, nb=nb,
                           oo=g["pE"][:, h * 128:(h + 1) * 128],
                           ll=g["PT3"][:, bi * 128:(bi + 1) * 128],
                           rr=g["V"][:, jb * D + h * 128:
                                     jb * D + (h + 1) * 128],
                           st_=(bi == 0), sp=(bi == nb - 1): t.matmul(
                    out=oo, lhsT=ll, rhs=rr, start=st_, stop=sp),
                    deps=[tptc, prev_oev] if bi == 0 else ())
            g["_l1b_pvuse"] = mmv
        tr1 = b.op("vector", lambda v: v.reciprocal(out=g["f8"][:],
                                                    in_=g["lbuf"][:]),
                   deps=[tl_last])
        ts_ = b.op("vector", lambda v: v.tensor_mul(
            out=g["n2_8"][:], in0=g["lbuf"][:], in1=g["f8"][:]), deps=[tr1])
        ts2 = b.op("vector", lambda v: v.tensor_scalar(
            out=g["n2_8"][:], in0=g["n2_8"][:], scalar1=1e-10, scalar2=None,
            op0=ALU.add), deps=[ts_])
        ts3 = b.op("vector", lambda v: v.reciprocal(out=g["n2_8"][:],
                                                    in_=g["n2_8"][:]),
                   deps=[ts2])
        tf8 = b.op("vector", lambda v: v.tensor_mul(
            out=g["f8"][:], in0=g["f8"][:], in1=g["n2_8"][:]), deps=[ts3])
        tev = None
        for h in range(8):
            tev = b.op("vector", lambda v, h=h, i=i: v.tensor_scalar(
                out=o_buf[:, i * D + h * 128:i * D + (h + 1) * 128],
                in0=g["pE"][:, h * 128:(h + 1) * 128],
                scalar1=g["f8"][:, h:h + 1], scalar2=None, op0=ALU.mult),
                deps=[tf8, g["_l1b_pvuse"]] if h == 0 else [tf8])
        prev_oev = tev
        o_toks.append(tev)
    g["_l1b_ptfree"] = prev_PT
    return o_toks


def emit_l1b_entropy(b, g):
    c10, c10t = get_const(b, g, 1e-10)

    def softmax_transform(buf, n_cols):
        BT = buf.rearrange("p (c t) -> p c t", t=T)
        out_toks = []
        for i in range(NT):
            pe_last = None
            for c in range(8):
                pe_last = b.op("tensor", lambda t, c=c, i=i, BT=BT,
                               oo=g["pT"][:, c * 128:(c + 1) * 128],
                               ii=BT[:, c, i * 128:(i + 1) * 128],
                               ee=g["eye_sb"][:]: t.transpose(
                    out=oo, in_=ii, identity=ee),
                    deps=[g["_l1b_ptfree"]] if c == 0 else ())
            tc = b.op("vector", lambda v: v.tensor_copy(
                out=g["qn_bf"][:], in_=g["pT"][:, 0:1024]), deps=[pe_last])
            tQ = b.op("vector", lambda v, i=i, n_cols=n_cols: v.tensor_mul(
                out=g["scr4k"][:].rearrange("p (h k) -> p h k", k=128),
                in0=g["qn_bf"][:].rearrange("p (h k) -> p h k", k=128),
                in1=n_cols[:, i * 8:(i + 1) * 8].unsqueeze(2).broadcast_to(
                    [128, 8, 128])), deps=[tc])
            tmx = b.op("vector", lambda v: v.tensor_reduce(
                out=g["mx8"][:], in_=g["scr4k"][:].rearrange(
                    "p (h k) -> p h k", k=128), axis=AX.X, op=ALU.max),
                deps=[tQ])
            tsb = b.op("vector", lambda v: v.scalar_tensor_tensor(
                out=g["scr4k"][:].rearrange("p (h k) -> p h k", k=128),
                in0=g["mx8"][:].unsqueeze(2).broadcast_to([128, 8, 128]),
                scalar=-1.0,
                in1=g["scr4k"][:].rearrange("p (h k) -> p h k", k=128),
                op0=ALU.mult, op1=ALU.add), deps=[tmx])
            te = b.op("scalar", lambda s: s.activation(
                out=g["scr4k"][:], in_=g["scr4k"][:], func=AF.Exp),
                deps=[tsb])
            tsm = b.op("vector", lambda v: v.tensor_reduce(
                out=g["mx8"][:], in_=g["scr4k"][:].rearrange(
                    "p (h k) -> p h k", k=128), axis=AX.X, op=ALU.add),
                deps=[te])
            trs = b.op("vector", lambda v: v.reciprocal(
                out=g["mx8"][:], in_=g["mx8"][:]), deps=[tsm])
            tqs = b.op("vector", lambda v: v.tensor_mul(
                out=g["qn_bf"][:].rearrange("p (h k) -> p h k", k=128),
                in0=g["scr4k"][:].rearrange("p (h k) -> p h k", k=128),
                in1=g["mx8"][:].unsqueeze(2).broadcast_to([128, 8, 128])),
                deps=[trs])
            pe_last = None
            for c in range(8):
                pe_last = b.op("tensor", lambda t, c=c,
                               oo=g["pT"][:, c * 128:(c + 1) * 128],
                               ii=g["qn_bf"][:, c * 128:(c + 1) * 128],
                               ee=g["eye_sb"][:]: t.transpose(
                    out=oo, in_=ii, identity=ee),
                    deps=[tqs, tc] if c == 0 else ())
            tc2 = b.op("vector", lambda v, i=i, BT=BT: v.tensor_copy(
                out=BT[:, :, i * 128:(i + 1) * 128],
                in_=g["pT"][:].rearrange("p (c f) -> p c f", f=128)),
                deps=[pe_last])
            g["_l1b_ptfree"] = tc2
            out_toks.append(tc2)
        return out_toks

    qs_toks = softmax_transform(g["dT"][:], g["n_all"])
    ks_toks = softmax_transform(g["knT"][:], g["kn_all"])
    QT = g["dT"][:].rearrange("p (c t) -> p c t", t=T)
    KT = g["knT"][:].rearrange("p (c t) -> p c t", t=T)
    hj_done = []
    prev_r = [None, None]
    prev_hred = None
    blk = 0
    for h in range(8):
        slot = 0
        prev_ln = None
        for i in range(NT):
            for gs in range(4):
                pbuf = g["pS"][:, 0:512] if blk % 2 == 0 \
                    else g["pE"][:, 0:512]
                mm = b.op("tensor", lambda t, h=h, i=i, gs=gs, pbuf=pbuf,
                          ll=QT[:, h, i * 128:(i + 1) * 128],
                          rr=KT[:, h, gs * 512:(gs + 1) * 512]: t.matmul(
                    out=pbuf, lhsT=ll, rhs=rr, start=True, stop=True),
                    deps=[prev_r[blk % 2]] +
                    ([qs_toks[-1], ks_toks[-1]] if blk < 2 else []))
                tln = b.op("scalar", lambda s, pbuf=pbuf: s.activation(
                    out=g["scr2k"][:], in_=pbuf, func=AF.Ln,
                    bias=c10), deps=[mm, c10t])
                tst_ = b.op("vector", lambda v, slot=slot, pbuf=pbuf:
                            v.scalar_tensor_tensor(
                    out=g["scr2k"][:], in0=g["scr2k"][:], scalar=1.0,
                    in1=pbuf, op0=ALU.mult, op1=ALU.mult,
                    accum_out=g["hacc"][:, slot:slot + 1]), deps=[tln])
                prev_r[blk % 2] = tst_
                prev_ln = tst_
                blk += 1
                slot += 1
        tred = b.op("vector", lambda v: v.reduce_sum(
            out=g["hcol"][:, 0:1], in_=g["hacc"][:], axis=AX.X),
            deps=[prev_ln, prev_hred])
        tbf = b.op("vector", lambda v: v.tensor_copy(
            out=g["hcol_bf"][:], in_=g["hcol"][:, 0:1]), deps=[tred])
        tmm = b.op("tensor", lambda t, h=h: t.matmul(
            out=g["pFq"][0:1, 0:1], lhsT=g["ones_col"][:],
            rhs=g["hcol_bf"][:], start=True, stop=True), deps=[tbf])
        prev_hred = b.op("vector", lambda v, h=h: v.tensor_copy(
            out=g["Hrows"][0:1, h:h + 1], in_=g["pFq"][0:1, 0:1]),
            deps=[tmm])
        hj_done.append(prev_hred)
    tq9, tk9 = g["_l1b_hqk"]
    tS1 = b.op("vector", lambda v: v.tensor_add(
        out=g["Hrows"][0:1, 32:40], in0=g["Hrows"][0:1, 16:24],
        in1=g["Hrows"][0:1, 24:32]), deps=[tq9, tk9] + hj_done)
    tS2 = b.op("vector", lambda v: v.tensor_sub(
        out=g["Hrows"][0:1, 32:40], in0=g["Hrows"][0:1, 32:40],
        in1=g["Hrows"][0:1, 0:8]), deps=[tS1])
    tS3 = b.op("vector", lambda v: v.reduce_sum(
        out=g["Hrows"][0:1, 40:41], in_=g["Hrows"][0:1, 32:40], axis=AX.X),
        deps=[tS2])
    tS4 = b.op("vector", lambda v: v.tensor_scalar(
        out=g["Hrows"][0:1, 41:42], in0=g["Hrows"][0:1, 40:41],
        scalar1=0.125, scalar2=ENT_TH, op0=ALU.mult, op1=ALU.is_gt),
        deps=[tS3])
    tgb = b.op("vector", lambda v: v.tensor_copy(
        out=g["mrow_bf"][0:1, 1:2], in_=g["Hrows"][0:1, 41:42]), deps=[tS4])
    tg128 = b.op("tensor", lambda t: t.matmul(
        out=g["pS"][:, 0:1], lhsT=g["ones_row"][:],
        rhs=g["mrow_bf"][0:1, 1:2], start=True, stop=True), deps=[tgb])
    tgc = b.op("vector", lambda v: v.tensor_copy(
        out=g["g128"][:, 1:2], in_=g["pS"][:, 0:1]), deps=[tg128])
    return tgc, tS4


def emit_l1b_social(b, g, tgc, o_toks, knT_last):
    o_buf = g["xlnT"][:]
    o_bfA = g["scr4k"][:, 0:512].bitcast(BF16)
    h_bfA = g["scr4k"][:, 512:768].bitcast(BF16)
    tw1 = emit_wfull(b, g, g["W1T_d"], ncols=512)
    tw2l = b.dma("sync", out=g["bias_row"][0:1, 0:512], in_=g["w2_d"],
                 deps=[knT_last])
    mmw = None
    for o2 in (0, 256):
        mmw = b.op("tensor", lambda t, o2=o2,
                   oo=g["pS"][:, o2:o2 + 256],
                   rr=g["bias_row"][0:1, o2:o2 + 256]: t.matmul(
            out=oo, lhsT=g["ones_row"][:], rhs=rr, start=True, stop=True),
            deps=[tw2l, tgc] if o2 == 0 else ())
    tw2bc = b.op("vector", lambda v: v.tensor_copy(
        out=g["w2bc"][:], in_=g["pS"][:, 0:512]), deps=[mmw])
    tb2l = b.dma("sync", out=g["mrow"][0:1, 10:11], in_=g["b2_d"])
    tneg = b.op("vector", lambda v: v.tensor_scalar(
        out=g["mrow"][0:1, 11:12], in0=g["mrow"][0:1, 10:11], scalar1=-1.0,
        scalar2=None, op0=ALU.mult), deps=[tb2l])
    tnb = b.op("vector", lambda v: v.tensor_copy(
        out=g["mrow_bf"][0:1, 2:3], in_=g["mrow"][0:1, 11:12]), deps=[tneg])
    tmmb = b.op("tensor", lambda t: t.matmul(
        out=g["pS"][:, 0:1], lhsT=g["ones_row"][:],
        rhs=g["mrow_bf"][0:1, 2:3], start=True, stop=True),
        deps=[tnb, tw2bc])
    tnb2 = b.op("vector", lambda v: v.tensor_copy(
        out=g["nb2c"][:], in_=g["pS"][:, 0:1]), deps=[tmmb])
    tb1 = b.dma("sync", out=g["bias_row"][0:1, 0:512], in_=g["b1_d"],
                deps=[mmw])
    tcv0 = b.op("vector", lambda v: v.memset(g["cvacc"][:, 0:1], 0.0))
    spill_toks = []
    prev_pe = None
    prev_sp = None
    prev_relu = None
    for i in range(NT):
        tgate = b.op("vector", lambda v, i=i: v.tensor_scalar(
            out=o_bfA, in0=o_buf[:, i * D:(i + 1) * D],
            scalar1=g["g128"][:, 1:2], scalar2=None, op0=ALU.mult),
            deps=[tgc, prev_pe, prev_sp] + ([o_toks[-1]] if i == 0 else []))
        tsp_i = b.dma("gpsimd", out=g["o_dram"][i * 128:(i + 1) * 128, :],
                      in_=o_bfA, deps=[tgate])
        prev_sp = tsp_i
        spill_toks.append(tsp_i)
        pe_last = None
        for c in range(8):
            pe_last = b.op("tensor", lambda t, c=c,
                           oo=g["pT"][:, c * 128:(c + 1) * 128],
                           ii=o_bfA[:, c * 128:(c + 1) * 128],
                           ee=g["eye_sb"][:]: t.transpose(
                out=oo, in_=ii, identity=ee),
                deps=[tgate, g["_l1b_ptfree"]] if c == 0 else ())
        prev_pe = pe_last
        toT = b.op("vector", lambda v: v.tensor_copy(
            out=g["qn_bf"][:], in_=g["pT"][:, 0:1024]), deps=[pe_last])
        g["_l1b_ptfree"] = toT
        mm = None
        for dc in range(8):
            mm = b.op("tensor", lambda t, dc=dc,
                      ll=g["qn_bf"][:, dc * 128:(dc + 1) * 128],
                      rr=g["Wfull"][:, dc * 512:(dc + 1) * 512],
                      st_=(dc == 0): t.matmul(
                out=g["pS"][:, 0:512], lhsT=ll, rhs=rr, start=st_,
                stop=False),
                deps=[toT, tw1, tnb2, prev_relu] if dc == 0 else ())
        for bo in (0, 256):
            mm = b.op("tensor", lambda t, bo=bo,
                      oo=g["pS"][:, bo:bo + 256],
                      rr=g["bias_row"][0:1, bo:bo + 256]: t.matmul(
                out=oo, lhsT=g["ones_row"][:], rhs=rr, start=False,
                stop=True), deps=[tb1] if bo == 0 else ())
        trelu = b.op("scalar", lambda s: s.activation(
            out=h_bfA, in_=g["pS"][:, 0:512], func=AF.Relu), deps=[mm])
        prev_relu = trelu
        tz = b.op("vector", lambda v: v.scalar_tensor_tensor(
            out=g["scr2k"][:], in0=h_bfA, scalar=1.0, in1=g["w2bc"][:],
            op0=ALU.mult, op1=ALU.mult, accum_out=g["cvacc"][:, 1:2]),
            deps=[trelu, tw2bc])
        te = b.op("scalar", lambda s: s.activation(
            out=g["hcol"][:, 1:2], in_=g["cvacc"][:, 1:2], func=AF.Exp,
            scale=-1.0, bias=g["nb2c"][:]), deps=[tz, tnb2])
        tp1 = b.op("vector", lambda v: v.tensor_scalar(
            out=g["hcol"][:, 1:2], in0=g["hcol"][:, 1:2], scalar1=1.0,
            scalar2=None, op0=ALU.add), deps=[te])
        trc = b.op("vector", lambda v: v.reciprocal(
            out=g["hcol"][:, 1:2], in_=g["hcol"][:, 1:2]), deps=[tp1])
        b.op("vector", lambda v: v.tensor_add(
            out=g["cvacc"][:, 0:1], in0=g["cvacc"][:, 0:1],
            in1=g["hcol"][:, 1:2]), deps=[trc, tcv0])
    tcb = b.op("vector", lambda v: v.tensor_copy(
        out=g["hcol_bf"][:], in_=g["cvacc"][:, 0:1]), deps=[])
    tms = b.op("tensor", lambda t: t.matmul(
        out=g["pFq"][0:1, 4:5], lhsT=g["ones_col"][:], rhs=g["hcol_bf"][:],
        start=True, stop=True), deps=[tcb])
    tsp = b.op("vector", lambda v: v.tensor_copy(
        out=g["mrow"][0:1, 12:13], in_=g["pFq"][0:1, 4:5]), deps=[tms])
    return spill_toks, tsp


def build_l1b_program(debug=False):
    nc = bass.Bass("TRN2", target_bir_lowering=False)
    b = Bld(nc)
    g = {"nc": nc}
    g["lnst"] = {k: nc.alloc_sbuf_tensor("ln_" + k, [128, NT], F32)
                 for k in ("s1", "s2", "m", "v", "rstd", "mr")}
    g["junk"] = nc.alloc_sbuf_tensor("junk", [128, D], BF16)
    g["cbuf"] = nc.alloc_sbuf_tensor("cbuf", [128, 16], F32)
    alloc_l1b_buffers(nc, g)
    g["eye_sb"] = nc.alloc_sbuf_tensor("eye_sb", [128, 128], BF16)
    g["x_d"] = nc.dram_tensor("x", [T, D], F32, kind="ExternalInput").ap()
    for nm in ("WprojT", "WvT", "WqT", "WkT"):
        g[nm + "_d"] = nc.dram_tensor(nm, [D, D], F32,
                                      kind="ExternalInput").ap()
    g["W1T_d"] = nc.dram_tensor("W1T", [1024, 512], F32,
                                kind="ExternalInput").ap()
    g["bv_d"] = nc.dram_tensor("bv_row", [1, D], BF16,
                               kind="ExternalInput").ap()
    g["bq_d"] = nc.dram_tensor("bq_row", [1, D], BF16,
                               kind="ExternalInput").ap()
    g["bk_d"] = nc.dram_tensor("bk_row", [1, D], BF16,
                               kind="ExternalInput").ap()
    g["b1_d"] = nc.dram_tensor("b1_row", [1, 512], BF16,
                               kind="ExternalInput").ap()
    g["w2_d"] = nc.dram_tensor("w2_row", [1, 512], BF16,
                               kind="ExternalInput").ap()
    g["b2_d"] = nc.dram_tensor("b2_in", [1, 1], F32,
                               kind="ExternalInput").ap()
    gp_d = nc.dram_tensor("gp_in", [1, 8], F32, kind="ExternalInput").ap()
    bpc = nc.dram_tensor("bproj_cols_d", [128, 8], F32,
                         kind="ExternalInput").ap()
    eye_d = nc.dram_tensor("eye", [128, 128], BF16, kind="ExternalInput").ap()
    onr_d = nc.dram_tensor("ones_row_d", [1, 128], BF16,
                           kind="ExternalInput").ap()
    onc_d = nc.dram_tensor("ones_col_d", [128, 1], BF16,
                           kind="ExternalInput").ap()
    mm_d = nc.dram_tensor("mask_mid_d", [128, 384], BF16,
                          kind="ExternalInput").ap()
    mf_d = nc.dram_tensor("mask_first_d", [128, 256], BF16,
                          kind="ExternalInput").ap()
    g["o_dram"] = nc.dram_tensor("o_out", [T, D], BF16,
                                 kind="ExternalOutput").ap()
    g_out = nc.dram_tensor("g_out", [1, 1], F32, kind="ExternalOutput").ap()
    sp_out = nc.dram_tensor("sp_out", [1, 1], F32, kind="ExternalOutput").ap()
    dbg_outs = {}
    if debug:
        for nm, sh, dt in (("d_xqkT", [128, 8 * T], BF16),
                           ("d_qnT", [128, 8 * T], BF16),
                           ("d_knT", [128, 8 * T], BF16),
                           ("d_V", [128, NT * D], BF16),
                           ("d_nall", [128, 128], F32),
                           ("d_Hrows", [1, 64], F32),
                           ("d_g128", [128, 2], F32)):
            dbg_outs[nm] = nc.dram_tensor(nm, sh, dt, kind="ExternalOutput").ap()
    with (
        nc.psum_tensor("pS", [128, 512], F32) as pS,
        nc.psum_tensor("pT", [128, 1024], BF16) as pT,
        nc.psum_tensor("pE", [128, 1024], F32) as pE,
        nc.psum_tensor("pFq", [1, 1024], F32) as pFq,
    ):
        g["pS"], g["pT"], g["pE"], g["pFq"] = pS, pT, pE, pFq
        teye = b.dma("gpsimd", out=g["eye_sb"][:], in_=eye_d)
        tonr = b.dma("gpsimd", out=g["ones_row"][:], in_=onr_d)
        tonc = b.dma("gpsimd", out=g["ones_col"][:], in_=onc_d)
        tbp = b.dma("gpsimd", out=g["bproj_cols"][:], in_=bpc)
        tgp = b.dma("gpsimd", out=g["gp_row"][:], in_=gp_d)
        tmm_ = b.dma("gpsimd", out=g["mask_mid"][:], in_=mm_d)
        tmf_ = b.dma("gpsimd", out=g["mask_first"][:], in_=mf_d)
        g["t_bproj"] = tbp
        g["t_gp"] = tgp
        g["t_ones"] = tonr
        g["t_onec"] = tonc
        g["t_masks"] = tmm_
        pre = [teye, tonr, tonc, tbp, tgp, tmm_, tmf_]
        for e in ("vector", "scalar", "tensor"):
            b.wait(e, pre)
        xlnT_toks, qnT_toks, knT_toks = emit_l1b(b, g, [])
        o_toks = emit_l1b_attn(b, g, qnT_toks, knT_toks)
        dbg_t = []
        if debug:
            dbg_t.append(b.dma("scalar", out=dbg_outs["d_qnT"],
                               in_=g["dT"][:], deps=[o_toks[-1]]))
            dbg_t.append(b.dma("scalar", out=dbg_outs["d_knT"],
                               in_=g["knT"][:], deps=[o_toks[-1]]))
            dbg_t.append(b.dma("scalar", out=dbg_outs["d_V"],
                               in_=g["V"][:], deps=[o_toks[-1]]))
            dbg_t.append(b.dma("scalar", out=dbg_outs["d_nall"],
                               in_=g["n_all"][:], deps=[o_toks[-1]]))
        tgc, tS4 = emit_l1b_entropy(b, g)
        spill_toks, tsp = emit_l1b_social(b, g, tgc, o_toks,
                                          knT_toks[-1])
        tso = b.dma("gpsimd", out=sp_out, in_=g["mrow"][0:1, 12:13],
                    deps=[tsp])
        tgo = b.dma("gpsimd", out=g_out, in_=g["Hrows"][0:1, 41:42],
                    deps=[tS4])
        if debug:
            dbg_t.append(b.dma("scalar", out=dbg_outs["d_Hrows"],
                               in_=g["Hrows"][:], deps=[tsp]))
            dbg_t.append(b.dma("scalar", out=dbg_outs["d_g128"],
                               in_=g["g128"][:], deps=[tgc, tsp]))
        b.wait("gpsimd", spill_toks + [tso, tgo])
        if dbg_t:
            b.wait("scalar", dbg_t)
        nc_built = b.build()
    return nc_built



def emit_l1c(b, g, tin):
    tf1 = b.op("vector", lambda v: v.reduce_sum(
        out=g["frow"][0:1, 0:1], in_=g["sp_row"][:], axis=AX.X),
        deps=[g["t_sp"]])
    tf2 = b.op("scalar", lambda s: s.activation(
        out=g["frow"][0:1, 1:2], in_=g["frow"][0:1, 0:1], func=AF.Exp,
        scale=-2.0 / 16384.0), deps=[tf1])
    tf3 = b.op("vector", lambda v: v.tensor_scalar(
        out=g["frow"][0:1, 1:2], in0=g["frow"][0:1, 1:2], scalar1=1.5,
        scalar2=0.5, op0=ALU.min, op1=ALU.max), deps=[tf2])
    tf4 = b.op("vector", lambda v: v.tensor_scalar(
        out=g["frow"][0:1, 2:3], in0=g["frow"][0:1, 1:2], scalar1=0.3,
        scalar2=0.7, op0=ALU.mult, op1=ALU.add), deps=[tf3])
    tf5 = b.op("vector", lambda v: v.tensor_copy(
        out=g["frow_bf"][0:1, 0:1], in_=g["frow"][0:1, 2:3]), deps=[tf4])
    tfm = b.op("tensor", lambda t: t.matmul(
        out=g["pB"][:, 0:1], lhsT=g["ones_row"][:],
        rhs=g["frow_bf"][0:1, 0:1], start=True, stop=True),
        deps=[tf5, g["t_ones"]])
    tfc = b.op("vector", lambda v: v.tensor_copy(
        out=g["fcol"][:], in_=g["pB"][:, 0:1]), deps=[tfm])
    tbc = None
    for o in (0, 512):
        mmb = None
        for bo2 in (0, 256):
            mmb = b.op("tensor", lambda t, o=o, bo2=bo2,
                       oo=g["pB"][:, bo2:bo2 + 256],
                       rr=g["bias_row"][0:1, o + bo2:o + bo2 + 256]:
                       t.matmul(out=oo, lhsT=g["ones_row"][:], rhs=rr,
                                start=True, stop=True),
                       deps=[g["t_bo"], tfc, tbc] if bo2 == 0 else ())
        tbc = b.op("vector", lambda v, o=o: v.tensor_copy(
            out=g["bo_bc"][:, o:o + 512], in_=g["pB"][:, 0:512]),
            deps=[mmb])
    two = emit_wfull(b, g, g["WoT_d"])
    prev_ev = None
    prev_oT = None
    tile_toks = []
    od_toks = {}
    for i in range(NT):
        slot = i % 2
        tod = b.dma("gpsimd",
                    out=g["ostage"][:, slot * 1024:(slot + 1) * 1024],
                    in_=g["o_in_d"][i * 128:(i + 1) * 128, :],
                    deps=[od_toks.get(i - 2)])
        pe_last = None
        for c in range(8):
            pe_last = b.op("tensor", lambda t, c=c, slot=slot,
                           oo=g["pT"][:, c * 128:(c + 1) * 128],
                           ii=g["ostage"][:, slot * 1024 + c * 128:
                                          slot * 1024 + (c + 1) * 128],
                           ee=g["eye_sb"][:]: t.transpose(
                out=oo, in_=ii, identity=ee),
                deps=[tod, prev_oT] if c == 0 else ())
        od_toks[i] = pe_last
        toT = b.op("vector", lambda v: v.tensor_copy(
            out=g["oT_bf"][:], in_=g["pT"][:, 0:1024]), deps=[pe_last])
        prev_oT = toT
        for (o, w) in ((0, 512), (512, 512)):
            mm = None
            for dc in range(8):
                mm = b.op("tensor", lambda t, dc=dc, o=o,
                          ll=g["oT_bf"][:, dc * 128:(dc + 1) * 128],
                          rr=g["Wfull"][:, dc * 1024 + o:
                                        dc * 1024 + o + 512],
                          st_=(dc == 0), sp=(dc == 7): t.matmul(
                    out=g["pS"][:, 0:512], lhsT=ll, rhs=rr, start=st_,
                    stop=sp),
                    deps=[toT, two, prev_ev] if dc == 0 else ())
            tev = b.op("vector", lambda v, o=o: v.scalar_tensor_tensor(
                out=g["scr"][:], in0=g["pS"][:, 0:512],
                scalar=g["fcol"][:, 0:1], in1=g["bo_bc"][:, o:o + 512],
                op0=ALU.mult, op1=ALU.add), deps=[mm, tfc, tbc])
            prev_ev = b.op("vector", lambda v, i=i, o=o: v.tensor_add(
                out=g["X"][:, i * D + o:i * D + o + 512],
                in0=g["X"][:, i * D + o:i * D + o + 512],
                in1=g["scr"][:]), deps=[tev] + (list(tin) if i == 0 else []))
        tile_toks.append(prev_ev)
    return tile_toks


def _alloc_l1c_buffers(nc, g):
    g["Wfull"] = nc.alloc_sbuf_tensor("Wfull", [128, 8 * 1024], BF16)
    g["Wstage"] = nc.alloc_sbuf_tensor("Wstage", [128, 1024], F32)
    g["ostage"] = nc.alloc_sbuf_tensor("ostage", [128, 2 * 1024], BF16)
    g["oT_bf"] = nc.alloc_sbuf_tensor("oT_bf", [128, 1024], BF16)
    g["bo_bc"] = nc.alloc_sbuf_tensor("bo_bc", [128, 1024], F32)
    g["scr"] = nc.alloc_sbuf_tensor("scr", [128, 512], F32)
    g["frow"] = nc.alloc_sbuf_tensor("frow", [1, 16], F32)
    g["frow_bf"] = nc.alloc_sbuf_tensor("frow_bf", [1, 16], BF16)
    g["fcol"] = nc.alloc_sbuf_tensor("fcol", [128, 1], F32)
    g["sp_row"] = nc.alloc_sbuf_tensor("sp_row", [1, 8], F32)
    g["bias_row"] = nc.alloc_sbuf_tensor("bias_row", [1, 1024], BF16)


def build_l1c_program():
    nc = bass.Bass("TRN2", target_bir_lowering=False)
    b = Bld(nc)
    g = {"nc": nc}
    g["X"] = nc.alloc_sbuf_tensor("Xsb", [128, NT * D], F32)
    _alloc_l1c_buffers(nc, g)
    g["ones_row"] = nc.alloc_sbuf_tensor("ones_row", [1, 128], BF16)
    g["eye_sb"] = nc.alloc_sbuf_tensor("eye_sb", [128, 128], BF16)
    g["cbuf"] = nc.alloc_sbuf_tensor("cbuf", [128, 16], F32)
    x_in = nc.dram_tensor("x", [T, D], F32, kind="ExternalInput").ap()
    g["o_in_d"] = nc.dram_tensor("o_in", [T, D], BF16,
                                 kind="ExternalInput").ap()
    sp_d = nc.dram_tensor("sp_in", [1, 8], F32, kind="ExternalInput").ap()
    g["WoT_d"] = nc.dram_tensor("WoT", [D, D], F32,
                                kind="ExternalInput").ap()
    bo_d = nc.dram_tensor("bo_row", [1, D], BF16, kind="ExternalInput").ap()
    eye_d = nc.dram_tensor("eye", [128, 128], BF16, kind="ExternalInput").ap()
    onr_d = nc.dram_tensor("ones_row_d", [1, 128], BF16,
                           kind="ExternalInput").ap()
    y_out = nc.dram_tensor("y", [T, D], F32, kind="ExternalOutput").ap()
    with (
        nc.psum_tensor("pS", [128, 512], F32) as pS,
        nc.psum_tensor("pB", [128, 512], F32) as pB,
        nc.psum_tensor("pT", [128, 1024], BF16) as pT,
    ):
        g["pS"], g["pB"], g["pT"] = pS, pB, pT
        teye = b.dma("gpsimd", out=g["eye_sb"][:], in_=eye_d)
        tonr = b.dma("gpsimd", out=g["ones_row"][:], in_=onr_d)
        tspr = b.dma("gpsimd", out=g["sp_row"][:], in_=sp_d)
        tbo = b.dma("sync", out=g["bias_row"][:], in_=bo_d)
        g["t_sp"], g["t_bo"], g["t_ones"] = tspr, tbo, tonr
        tx_toks = []
        for hh in range(2):
            tx_toks.append(b.dma("gpsimd",
                           out=g["X"][:, hh * 8 * D:(hh + 1) * 8 * D].rearrange(
                               "p (i f) -> p i f", f=D),
                           in_=x_in.rearrange("(i p) f -> p i f", p=128)
                           [:, hh * 8:(hh + 1) * 8, :]))
        for e in ("vector", "scalar", "tensor"):
            b.wait(e, [teye, tonr, tspr, tbo] + tx_toks)
        tile_toks = emit_l1c(b, g, tx_toks)
        st_toks = []
        for i in range(NT):
            st_toks.append(b.dma("gpsimd",
                           out=y_out[i * 128:(i + 1) * 128, :],
                           in_=g["X"][:, i * D:(i + 1) * D],
                           deps=[tile_toks[i]]))
        b.wait("gpsimd", st_toks)
        nc_built = b.build()
    return nc_built



def build_moela_program():
    nc = bass.Bass("TRN2", target_bir_lowering=False)
    b = Bld(nc)
    g = {"nc": nc}
    g["X"] = nc.alloc_sbuf_tensor("Xsb", [128, NT * D], F32)
    g["lnst"] = {k: nc.alloc_sbuf_tensor("ln_" + k, [128, NT], F32)
                 for k in ("s1", "s2", "m", "v", "rstd", "mr")}
    g["junk"] = nc.alloc_sbuf_tensor("junk", [128, D], BF16)
    g["cbuf"] = nc.alloc_sbuf_tensor("cbuf", [128, 16], F32)
    alloc_moe_buffers(nc, g)
    g["eye_sb"] = nc.alloc_sbuf_tensor("eye_sb", [128, 128], BF16)
    g["ones_row"] = nc.alloc_sbuf_tensor("ones_row", [1, 128], BF16)
    g["ones_col"] = nc.alloc_sbuf_tensor("ones_col", [128, 1], BF16)
    g["xlnT"] = g["macc"][:].bitcast(BF16)
    g["Wstage"] = nc.alloc_sbuf_tensor("Wstage", [128, 1024], F32)
    g["Wblk"] = nc.alloc_sbuf_tensor("Wblk", [128, 2 * 1024], BF16)
    g["res_sb"] = nc.alloc_sbuf_tensor("res_sb", [128, 512], F32)
    g["rn2_sb"] = nc.alloc_sbuf_tensor("rn2_sb", [1, T], F32)
    g["xn2_row"] = nc.alloc_sbuf_tensor("xn2_row", [1, T], F32)
    g["gp_small"] = nc.alloc_sbuf_tensor("gp_small", [128, 4], F32)
    g["bproj_cols"] = nc.alloc_sbuf_tensor("bproj_cols", [128, 8], F32)
    g["res_bf"] = nc.alloc_sbuf_tensor("res_bf", [128, 512], BF16)
    x_in = nc.dram_tensor("x", [T, D], F32, kind="ExternalInput").ap()
    y_out = nc.dram_tensor("y", [T, D], F32, kind="ExternalOutput").ap()
    gp_out = nc.dram_tensor("gp", [1, 1], F32, kind="ExternalOutput").ap()
    g["ew_d"] = nc.dram_tensor("expert_w", [8 * D, D], F32,
                               kind="ExternalInput").ap()
    g["rw_d"] = nc.dram_tensor("router_w", [D, 8], F32,
                               kind="ExternalInput").ap()
    eb_d = nc.dram_tensor("expert_b", [8, D], F32, kind="ExternalInput").ap()
    rb_d = nc.dram_tensor("router_b", [1, 8], F32, kind="ExternalInput").ap()
    g["WprojT_d"] = nc.dram_tensor("WprojT", [D, D], F32,
                                   kind="ExternalInput").ap()
    bpc = nc.dram_tensor("bproj_cols_d", [128, 8], F32,
                         kind="ExternalInput").ap()
    eye_d = nc.dram_tensor("eye", [128, 128], BF16, kind="ExternalInput").ap()
    onr = nc.dram_tensor("ones_row_d", [1, 128], BF16,
                         kind="ExternalInput").ap()
    onc_d = nc.dram_tensor("ones_col_d", [128, 1], BF16,
                           kind="ExternalInput").ap()
    with (
        nc.psum_tensor("pA", [128, 512], F32) as pA,
        nc.psum_tensor("pE", [128, 1024], F32) as pE,
        nc.psum_tensor("pT", [128, 1024], BF16) as pT,
        nc.psum_tensor("pF", [1, 2048], F32) as pF,
    ):
        g["pA"], g["pE"], g["pT"], g["pF"] = pA, pE, pT, pF
        g["pS"] = pA
        teye = b.dma("gpsimd", out=g["eye_sb"][:], in_=eye_d)
        tonr = b.dma("gpsimd", out=g["ones_row"][:], in_=onr)
        tonc = b.dma("gpsimd", out=g["ones_col"][:], in_=onc_d)
        tbp = b.dma("gpsimd", out=g["bproj_cols"][:], in_=bpc)
        g["t_bproj"] = tbp
        tx_toks = []
        for hh in range(2):
            tx_toks.append(b.dma("gpsimd",
                           out=g["X"][:, hh * 8 * D:(hh + 1) * 8 * D].rearrange(
                               "p (i f) -> p i f", f=D),
                           in_=x_in.rearrange("(i p) f -> p i f", p=128)
                           [:, hh * 8:(hh + 1) * 8, :]))
        tx = tx_toks
        drw = b.dma("sync", out=g["macc"][:, 0:64].rearrange(
                        "p (c e) -> p c e", e=8),
                    in_=g["rw_d"].rearrange("(c p) e -> p c e", p=128))
        g["t_rw"] = b.op("vector", lambda v: v.tensor_copy(
            out=g["rw_bf"][:], in_=g["macc"][:, 0:64]), deps=[drw])
        deb = b.dma("sync", out=g["macc"][0:8, 64:64 + D], in_=eb_d)
        g["t_eb"] = b.op("vector", lambda v: v.tensor_copy(
            out=g["eb_bf"][:], in_=g["macc"][0:8, 64:64 + D]), deps=[deb])
        drb = b.dma("sync", out=g["macc"][0:1, 1088:1096], in_=rb_d)
        g["t_rb"] = b.op("vector", lambda v: v.tensor_copy(
            out=g["rb_row"][:], in_=g["macc"][0:1, 1088:1096]), deps=[drb])
        for e in ("vector", "scalar", "tensor"):
            b.wait(e, [teye, tonr, tonc, tbp, g["t_rw"], g["t_eb"],
                       g["t_rb"]] + tx)
        res = emit_moe(b, g, tx)
        tps = emit_l1a(b, g, list(res))
        tst = [b.dma("gpsimd", out=gp_out, in_=g["gp_small"][0:1, 0:1],
                     deps=[tps])]
        for hh in range(2):
            tst.append(b.dma("gpsimd",
                       out=y_out.rearrange("(i p) f -> p i f", p=128)
                       [:, hh * 8:(hh + 1) * 8, :],
                       in_=g["X"][:, hh * 8 * D:(hh + 1) * 8 * D].rearrange(
                           "p (i f) -> p i f", f=D),
                       deps=res))
        b.wait("gpsimd", tst)
        nc_built = b.build()
    return nc_built


def build_l1cmoe_program():
    nc = bass.Bass("TRN2", target_bir_lowering=False)
    b = Bld(nc)
    g = {"nc": nc}
    g["X"] = nc.alloc_sbuf_tensor("Xsb", [128, NT * D], F32)
    g["lnst"] = {k: nc.alloc_sbuf_tensor("ln_" + k, [128, NT], F32)
                 for k in ("s1", "s2", "m", "v", "rstd", "mr")}
    g["junk"] = nc.alloc_sbuf_tensor("junk", [128, D], BF16)
    g["cbuf"] = nc.alloc_sbuf_tensor("cbuf", [128, 16], F32)
    _alloc_l1c_buffers(nc, g)
    alloc_moe_buffers(nc, g)
    g["eye_sb"] = nc.alloc_sbuf_tensor("eye_sb", [128, 128], BF16)
    g["ones_row"] = nc.alloc_sbuf_tensor("ones_row", [1, 128], BF16)
    x_in = nc.dram_tensor("x", [T, D], F32, kind="ExternalInput").ap()
    g["o_in_d"] = nc.dram_tensor("o_in", [T, D], BF16,
                                 kind="ExternalInput").ap()
    sp_d = nc.dram_tensor("sp_in", [1, 8], F32, kind="ExternalInput").ap()
    g["WoT_d"] = nc.dram_tensor("WoT", [D, D], F32,
                                kind="ExternalInput").ap()
    bo_d = nc.dram_tensor("bo_row", [1, D], BF16, kind="ExternalInput").ap()
    g["ew_d"] = nc.dram_tensor("expert_w", [8 * D, D], F32,
                               kind="ExternalInput").ap()
    g["rw_d"] = nc.dram_tensor("router_w", [D, 8], F32,
                               kind="ExternalInput").ap()
    eb_d = nc.dram_tensor("expert_b", [8, D], F32, kind="ExternalInput").ap()
    rb_d = nc.dram_tensor("router_b", [1, 8], F32, kind="ExternalInput").ap()
    eye_d = nc.dram_tensor("eye", [128, 128], BF16, kind="ExternalInput").ap()
    onr = nc.dram_tensor("ones_row_d", [1, 128], BF16,
                         kind="ExternalInput").ap()
    y_out = nc.dram_tensor("y", [T, D], F32, kind="ExternalOutput").ap()
    with (
        nc.psum_tensor("pS", [128, 512], F32) as pS,
        nc.psum_tensor("pB", [128, 512], F32) as pB,
        nc.psum_tensor("pT", [128, 1024], BF16) as pT,
        nc.psum_tensor("pE", [128, 1024], F32) as pE,
    ):
        g["pS"], g["pB"], g["pT"], g["pE"] = pS, pB, pT, pE
        g["pA"] = pB
        teye = b.dma("gpsimd", out=g["eye_sb"][:], in_=eye_d)
        tonr = b.dma("gpsimd", out=g["ones_row"][:], in_=onr)
        tspr = b.dma("gpsimd", out=g["sp_row"][:], in_=sp_d)
        tbo = b.dma("sync", out=g["bias_row"][:], in_=bo_d)
        g["t_sp"], g["t_bo"], g["t_ones"] = tspr, tbo, tonr
        tx_toks = []
        for hh in range(2):
            tx_toks.append(b.dma("gpsimd",
                           out=g["X"][:, hh * 8 * D:(hh + 1) * 8 * D].rearrange(
                               "p (i f) -> p i f", f=D),
                           in_=x_in.rearrange("(i p) f -> p i f", p=128)
                           [:, hh * 8:(hh + 1) * 8, :]))
        drw = b.dma("sync", out=g["macc"][:, 0:64].rearrange(
                        "p (c e) -> p c e", e=8),
                    in_=g["rw_d"].rearrange("(c p) e -> p c e", p=128))
        g["t_rw"] = b.op("vector", lambda v: v.tensor_copy(
            out=g["rw_bf"][:], in_=g["macc"][:, 0:64]), deps=[drw])
        deb = b.dma("sync", out=g["macc"][0:8, 64:64 + D], in_=eb_d)
        g["t_eb"] = b.op("vector", lambda v: v.tensor_copy(
            out=g["eb_bf"][:], in_=g["macc"][0:8, 64:64 + D]), deps=[deb])
        drb = b.dma("sync", out=g["macc"][0:1, 1088:1096], in_=rb_d)
        g["t_rb"] = b.op("vector", lambda v: v.tensor_copy(
            out=g["rb_row"][:], in_=g["macc"][0:1, 1088:1096]), deps=[drb])
        for e in ("vector", "scalar", "tensor"):
            b.wait(e, [teye, tonr, tspr, tbo, g["t_rw"], g["t_eb"],
                       g["t_rb"]] + tx_toks)
        tile_toks = emit_l1c(b, g, tx_toks)
        res = emit_moe(b, g, tile_toks)
        tst = []
        for hh in range(2):
            tst.append(b.dma("gpsimd",
                       out=y_out.rearrange("(i p) f -> p i f", p=128)
                       [:, hh * 8:(hh + 1) * 8, :],
                       in_=g["X"][:, hh * 8 * D:(hh + 1) * 8 * D].rearrange(
                           "p (i f) -> p i f", f=D),
                       deps=res))
        b.wait("gpsimd", tst)
        nc_built = b.build()
    return nc_built


def _eye_bf():
    return np.eye(128, dtype=ml_dtypes.bfloat16)


def _l1b_masks():
    p = np.arange(128)[:, None]
    c = np.arange(384)[None, :]
    mm = np.where(np.abs(p + 128 - c) <= 32, 0.0, -1e9)
    mm = mm.astype(ml_dtypes.bfloat16)
    c2 = np.arange(256)[None, :]
    mf = np.where(np.abs(p - c2) <= 32, 0.0, -1e9)
    mf = mf.astype(ml_dtypes.bfloat16)
    return mm, mf


_PROGS = {}


def _prog(name):
    if name not in _PROGS:
        _PROGS[name] = {
            "ffn0": build_ffn0_program,
            "ffn0x": lambda: build_ffn0_program(export_wq=True),
            "ffn0p": lambda: build_ffn0_program(prequant=True),
            "attn0": build_attn0_program,
            "moe": build_moe_program,
            "l1a": build_l1a_program,
            "l1b": build_l1b_program,
            "l1c": build_l1c_program,
            "moela": build_moela_program,
            "l1cmoe": build_l1cmoe_program,
        }[name]()
    return _PROGS[name]


def _launch(name, per_core):
    nc = _prog(name)
    res = run_bass_kernel_spmd(nc, per_core, core_ids=list(range(8)))
    return res.results


def kernel(**inputs):
    BF = ml_dtypes.bfloat16
    f32c = np.ascontiguousarray
    x = f32c(inputs["L0_attn_x"].astype(np.float32))
    eye = _eye_bf()
    onr = np.ones((1, 128), dtype=BF)
    onc = np.ones((128, 1), dtype=BF)
    mm, mf = _l1b_masks()

    wg = f32c(inputs["L0_ffn_w_gate"].astype(np.float32))
    wu = f32c(inputs["L0_ffn_w_up"].astype(np.float32))
    wd = f32c(inputs["L0_ffn_w_down"].astype(np.float32))
    ffn_ins = [{"x": f32c(x[c]), "w_gate": wg, "w_up": wu, "w_down": wd,
                "eye": eye} for c in range(8)]
    r = _launch("ffn0x", ffn_ins)
    xs = [np.asarray(r[c]["y"]) for c in range(8)]
    wq_cache = {k: np.asarray(r[0][k]) for k in ("wgT_q", "wuT_q", "wdT_q")}

    WqT0 = f32c(inputs["L0_attn_Wq"].T.astype(np.float32))
    WkT0 = f32c(inputs["L0_attn_Wk"].T.astype(np.float32))
    bq0 = f32c(inputs["L0_attn_bq"].reshape(1, 1024).astype(np.float32))
    bk0 = f32c(inputs["L0_attn_bk"].reshape(8, 128).T.astype(np.float32))
    a_ins = [{"x": xs[c], "WqT": WqT0, "WkT": WkT0, "bq_row": bq0,
              "bk_cols": bk0, "eye": eye, "ones_row_d": onr,
              "ones_col_d": onc} for c in range(8)]
    r = _launch("attn0", a_ins)
    xs = [np.asarray(r[c]["y"]) for c in range(8)]

    ffn_ins = [dict(x=xs[c], eye=eye, **wq_cache) for c in range(8)]
    r = _launch("ffn0p", ffn_ins)
    xs = [np.asarray(r[c]["y"]) for c in range(8)]

    ew = f32c(inputs["L1_ffn_expert_w"].reshape(8 * 1024, 1024)
              .astype(np.float32))
    eb = f32c(inputs["L1_ffn_expert_b"].astype(np.float32))
    rw = f32c(inputs["L1_ffn_router_w"].astype(np.float32))
    rb = f32c(inputs["L1_ffn_router_b"].reshape(1, 8).astype(np.float32))
    WprojT = f32c(inputs["L1_attn_Wproj"].T.astype(np.float32))
    bproj_cols = f32c(inputs["L1_attn_bproj"].reshape(8, 128).T
                      .astype(np.float32))
    m_ins = [{"x": xs[c], "expert_w": ew, "router_w": rw, "expert_b": eb,
              "router_b": rb, "eye": eye, "ones_row_d": onr,
              "ones_col_d": onc, "WprojT": WprojT,
              "bproj_cols_d": bproj_cols}
             for c in range(8)]
    r = _launch("moela", m_ins)
    xs = [np.asarray(r[c]["y"]) for c in range(8)]
    gp_row = np.concatenate([np.asarray(r[c]["gp"]) for c in range(8)],
                            axis=1).astype(np.float32)
    gp_row = f32c(gp_row)

    lb_ins = [{
        "x": xs[c],
        "WprojT": WprojT,
        "WvT": f32c(inputs["L1_attn_Wv"].T.astype(np.float32)),
        "WqT": f32c(inputs["L1_attn_Wq"].T.astype(np.float32)),
        "WkT": f32c(inputs["L1_attn_Wk"].T.astype(np.float32)),
        "W1T": f32c(inputs["L1_attn_W1"].T.astype(np.float32)),
        "bv_row": inputs["L1_attn_bv"].reshape(1, 1024).astype(BF),
        "bq_row": inputs["L1_attn_bq"].reshape(1, 1024).astype(BF),
        "bk_row": inputs["L1_attn_bk"].reshape(1, 1024).astype(BF),
        "b1_row": inputs["L1_attn_b1"].reshape(1, 512).astype(BF),
        "w2_row": inputs["L1_attn_W2"].reshape(1, 512).astype(BF),
        "b2_in": f32c(inputs["L1_attn_b2"].reshape(1, 1)
                      .astype(np.float32)),
        "gp_in": gp_row,
        "bproj_cols_d": bproj_cols,
        "eye": eye, "ones_row_d": onr, "ones_col_d": onc,
        "mask_mid_d": mm, "mask_first_d": mf,
    } for c in range(8)]
    r = _launch("l1b", lb_ins)
    os_ = [np.asarray(r[c]["o_out"]) for c in range(8)]
    sp_row = np.concatenate([np.asarray(r[c]["sp_out"]) for c in range(8)],
                            axis=1).astype(np.float32)
    sp_row = f32c(sp_row)

    WoT = f32c(inputs["L1_attn_Wo"].T.astype(np.float32))
    bo_row = inputs["L1_attn_bo"].reshape(1, 1024).astype(BF)
    lc_ins = [{"x": xs[c], "o_in": os_[c], "sp_in": sp_row, "WoT": WoT,
               "bo_row": bo_row, "eye": eye, "ones_row_d": onr,
               "expert_w": ew, "router_w": rw, "expert_b": eb,
               "router_b": rb}
              for c in range(8)]
    r = _launch("l1cmoe", lc_ins)
    out = np.stack([np.asarray(r[c]["y"]) for c in range(8)], axis=0)
    return out.astype(np.float32)
